# version 01
# baseline (speedup 1.0000x reference)
import numpy as np
from contextlib import ExitStack

import concourse.bass as bass
import concourse.mybir as mybir
from concourse.bass_utils import run_bass_kernel_spmd

F32 = mybir.dt.float32
BF16 = mybir.dt.bfloat16
I32 = mybir.dt.int32
AF = mybir.ActivationFunctionType
ALU = mybir.AluOpType
AX = mybir.AxisListType

NCORES = 8
D = 2048
NFT = D // 128


class Sems:

    def __init__(self, nc, stack):
        self.nc = nc
        self.stack = stack
        self.h = {}
        self.n = {}

    def sem(self, name):
        if name not in self.h:
            self.h[name] = self.stack.enter_context(self.nc.semaphore(name))
            self.n[name] = 0
        return self.h[name]

    def reserve(self, name, count):
        self.sem(name)
        base = self.n[name]
        self.n[name] += count
        return base

    def inc(self, inst, name, k=1):
        inst.then_inc(self.sem(name), k)

    def wait(self, eng, name, value):
        eng.wait_ge(self.sem(name), value)


def vfence(eng, S, inst):
    nm = "vf_" + str(getattr(eng, "engine", "x")).replace(".", "_")
    b = S.reserve(nm, 1)
    S.inc(inst, nm)
    S.wait(eng, nm, b + 1)


def emit_ident(block, S, ident_ap):
    b = S.reserve("id", 1)

    @block.gpsimd
    def _(g):
        inst = g.memset(ident_ap, 0.0)
        vfence(g, S, inst)
        S.inc(g.affine_select(out=ident_ap, in_=ident_ap, compare_op=ALU.not_equal,
                              fill=1.0, base=0, pattern=[[-1, ident_ap.shape[1]]],
                              channel_multiplier=1), "id")
    return b + 1


def emit_ln(block, S, xsrc_fn, ntiles, stats, lnscr, apply_fn, lr_wait,
            tp_base_fn=None, eps=1e-5, n=D):
    sq0 = S.reserve("lnsq", ntiles)
    st0 = S.reserve("lnst", 1)
    r0 = S.reserve("lnr", 1)
    ln0 = S.reserve("ln", ntiles)

    @block.scalar
    def _(s_):
        S.wait(s_, lr_wait[0], lr_wait[1])
        for i in range(ntiles):
            if i > 0:
                S.wait(s_, "lnsq", sq0 + i)
            S.inc(s_.activation(out=lnscr[:, 0:n], in_=xsrc_fn(i), func=AF.Square,
                                accum_out=stats[:, 4 + i:5 + i]), "lnsq")

    @block.vector
    def _(v):
        S.wait(v, lr_wait[0], lr_wait[1])
        for i in range(ntiles):
            inst = v.reduce_sum(out=stats[:, 0 + i:1 + i], in_=xsrc_fn(i), axis=AX.X)
        vfence(v, S, inst)
        S.wait(v, "lnsq", sq0 + ntiles)
        for i in range(ntiles):
            v.tensor_scalar_mul(out=stats[:, 8 + i:9 + i],
                                in0=stats[:, 0 + i:1 + i], scalar1=1.0 / n)
            inst = v.tensor_scalar_mul(out=stats[:, 12 + i:13 + i],
                                       in0=stats[:, 4 + i:5 + i], scalar1=1.0 / n)
        vfence(v, S, inst)
        for i in range(ntiles):
            inst = v.tensor_tensor(out=stats[:, 20 + i:21 + i],
                                   in0=stats[:, 8 + i:9 + i],
                                   in1=stats[:, 8 + i:9 + i], op=ALU.mult)
        vfence(v, S, inst)
        for i in range(ntiles):
            inst = v.tensor_tensor(out=stats[:, 12 + i:13 + i],
                                   in0=stats[:, 12 + i:13 + i],
                                   in1=stats[:, 20 + i:21 + i], op=ALU.subtract)
        vfence(v, S, inst)
        for i in range(ntiles):
            inst = v.tensor_scalar(out=stats[:, 12 + i:13 + i],
                                   in0=stats[:, 12 + i:13 + i],
                                   scalar1=0.0, scalar2=eps,
                                   op0=ALU.max, op1=ALU.add)
        S.inc(inst, "lnst")

    @block.scalar
    def _(s_):
        S.wait(s_, "lnst", st0 + 1)
        inst = s_.activation(out=stats[:, 16:16 + ntiles],
                             in_=stats[:, 12:12 + ntiles], func=AF.Ln)
        vfence(s_, S, inst)
        S.inc(s_.activation(out=stats[:, 16:16 + ntiles],
                            in_=stats[:, 16:16 + ntiles],
                            func=AF.Exp, scale=-0.5), "lnr")

    @block.vector
    def _(v):
        S.wait(v, "lnr", r0 + 1)
        for i in range(ntiles):
            if tp_base_fn:
                nm, val = tp_base_fn(i)
                if nm:
                    S.wait(v, nm, val)
            S.inc(apply_fn(v, i, stats[:, 8 + i:9 + i], stats[:, 16 + i:17 + i]),
                  "ln")
    return ln0


def emit_transposes(block, S, items, src_fn, dst_fn, ident, ptr,
                    pe_pre=None, sc_pre=None, src_tick=None):
    nit = len(items)
    tp0 = S.reserve("tp", nit)
    ev0 = S.reserve("tev", nit)

    @block.tensor
    def _(t):
        if "id" in S.n:
            S.wait(t, "id", 1)
        if pe_pre:
            pe_pre(t)
        for j, it in enumerate(items):
            if src_tick:
                nm, val = src_tick(j)
                if nm:
                    S.wait(t, nm, val)
            if j >= 2:
                S.wait(t, "tev", ev0 + j - 1)
            elif ev0 > 0:
                S.wait(t, "tev", ev0)
            S.inc(t.transpose(ptr[j % 2][:], in_=src_fn(it), identity=ident), "tp")

    @block.scalar
    def _(s):
        if sc_pre:
            sc_pre(s)
        for j, it in enumerate(items):
            S.wait(s, "tp", tp0 + j + 1)
            S.inc(s.copy(out=dst_fn(it), in_=ptr[j % 2][:]), "tev")

    return tp0, ev0


def emit_wstream(block, S, items, nbuf, load_fn, cast_fn):
    n = len(items)
    bases = [S.reserve(f"lw{k}", 16 * len(range(k, n, nbuf))) for k in range(nbuf)]
    cw0 = S.reserve("cw", n)
    uw0 = S.reserve("uw", n)

    @block.sync
    def _(sync):
        for j, it in enumerate(items):
            val = cw0 + j - nbuf + 1 if j >= nbuf else cw0
            if val > 0:
                S.wait(sync, "cw", val)
            S.inc(load_fn(sync, it, j % nbuf), f"lw{j % nbuf}", 16)

    @block.gpsimd
    def _(g):
        for j, it in enumerate(items):
            S.wait(g, f"lw{j % nbuf}", bases[j % nbuf] + 16 * (j // nbuf + 1))
            val = uw0 + j - nbuf + 1 if j >= nbuf else uw0
            if val > 0:
                S.wait(g, "uw", val)
            S.inc(cast_fn(g, it, j % nbuf), "cw")

    return cw0, uw0


def build_L1a(tok=1024, dbg=False):
    nc = bass.Bass("TRN2", target_bir_lowering=False, debug=False)
    x_in = nc.dram_tensor("x", [tok, D], F32, kind="ExternalInput").ap()
    W_in = {w: nc.dram_tensor(w, [D, D], F32, kind="ExternalInput").ap()
            for w in ("Wq", "Wk", "Wv", "Wo")}
    x_out = nc.dram_tensor("x_out", [tok, D], F32, kind="ExternalOutput").ap()
    if dbg:
        d_yT = nc.dram_tensor("d_yT", [128, 16 * 512], BF16, kind="ExternalOutput").ap()
        d_q = nc.dram_tensor("d_q", [128, 4 * D], BF16, kind="ExternalOutput").ap()
        d_k = nc.dram_tensor("d_k", [128, 4 * D], BF16, kind="ExternalOutput").ap()
        d_v = nc.dram_tensor("d_v", [128, 4 * D], BF16, kind="ExternalOutput").ap()
        d_scor = nc.dram_tensor("d_scor", [128, 1024], F32, kind="ExternalOutput").ap()
        d_stats = nc.dram_tensor("d_stats", [128, 24], F32, kind="ExternalOutput").ap()
        d_o = nc.dram_tensor("d_o", [128, 4 * D], BF16, kind="ExternalOutput").ap()

    NH, HD = 16, 128
    HALF = tok // 2
    NTT = HALF // 128

    with ExitStack() as st:
        sb = lambda n, sh, dt=F32: st.enter_context(nc.sbuf_tensor(n, sh, dt))
        S = Sems(nc, st)

        ident = sb("ident", [128, 128], BF16)
        xr = sb("xr", [128, NTT * D])
        stats = sb("stats", [128, 24])
        lnscr = sb("lnscr", [128, D])
        ynt = sb("ynt", [128, D], BF16)
        yT = sb("yT", [128, NFT * HALF], BF16)
        wf = sb("wf", [128, 2 * 1024])
        wb = sb("wb", [128, 2 * 1024], BF16)
        q_tm = sb("q_tm", [128, NTT * D], BF16)
        k_tm = sb("k_tm", [128, NTT * D], BF16)
        v_tm = sb("v_tm", [128, NTT * D], BF16)
        o_tm = sb("o_tm", [128, NTT * D], BF16)
        oT = sb("oT", [128, NFT * HALF], BF16)
        prod = sb("prod", [128, 2 * D])
        scor = sb("scor", [128, NTT * NH * NH])

        block = st.enter_context(nc.Block())
        emit_ident(block, S, ident[:])

        xv = x_in.rearrange("(i p) d -> p i d", p=128)
        xov = x_out.rearrange("(i p) d -> p i d", p=128)
        DSTS = [q_tm, k_tm, v_tm]

        for half in range(2):
            t0 = half * NTT

            lr0 = S.reserve("lr", 16 * NTT)
            prev_st = S.n.get("st", 0)

            @block.sync
            def _(sync, half=half, t0=t0, prev_st=prev_st):
                for i in range(NTT):
                    if half > 0:
                        S.wait(sync, "st", prev_st)
                    S.inc(sync.dma_start(out=xr[:, i * D:(i + 1) * D],
                                         in_=xv[:, t0 + i, :]), "lr", 16)

            tpB0 = S.n.get("tp", 0)
            ln0 = emit_ln(
                block, S,
                xsrc_fn=lambda i: xr[:, i * D:(i + 1) * D],
                ntiles=NTT, stats=stats, lnscr=lnscr,
                apply_fn=lambda v, i, mean, rstd: v.tensor_scalar(
                    out=ynt[:], in0=xr[:, i * D:(i + 1) * D],
                    scalar1=mean, scalar2=rstd,
                    op0=ALU.subtract, op1=ALU.mult),
                lr_wait=("lr", lr0 + 16 * NTT),
                tp_base_fn=lambda i, tpB0=tpB0: ("tp", tpB0 + i * NFT) if i > 0 or tpB0 > 0 else (None, 0))

            with ExitStack() as ph:
                ptr = [ph.enter_context(nc.psum_tensor(f"ptA{half}_{j}", [128, 128], BF16))
                       for j in range(2)]
                items = [(i, f) for i in range(NTT) for f in range(NFT)]
                emit_transposes(
                    block, S, items,
                    src_fn=lambda it: ynt[:, it[1] * 128:(it[1] + 1) * 128],
                    dst_fn=lambda it: yT[:, it[1] * HALF + it[0] * 128:
                                         it[1] * HALF + (it[0] + 1) * 128],
                    ident=ident[:], ptr=ptr,
                    src_tick=lambda j, ln0=ln0: ("ln", ln0 + j // NFT + 1))
            yT_done = S.n["tev"]

            with ExitStack() as ph:
                pj = [ph.enter_context(nc.psum_tensor(f"pjC{half}_{j}", [128, 512], F32))
                      for j in range(8)]
                rounds = [(wi, ng) for wi in range(3) for ng in range(2)]
                strip_items = [(wi, ng, k) for (wi, ng) in rounds for k in range(16)]

                cw0, uw0 = emit_wstream(
                    block, S, strip_items, 2,
                    load_fn=lambda sync, it, slot: sync.dma_start(
                        out=wf[:, slot * 1024:(slot + 1) * 1024],
                        in_=W_in[("Wq", "Wk", "Wv")[it[0]]][
                            it[2] * 128:(it[2] + 1) * 128,
                            it[1] * 1024:(it[1] + 1) * 1024]),
                    cast_fn=lambda g, it, slot: g.tensor_copy(
                        out=wb[:, slot * 1024:(slot + 1) * 1024],
                        in_=wf[:, slot * 1024:(slot + 1) * 1024]))

                eq0 = S.reserve("eq", len(rounds) * 8)

                @block.tensor
                def _(t, half=half, cw0=cw0, uw0=uw0, eq0=eq0,
                      yT_done=yT_done):
                    S.wait(t, "tev", yT_done)
                    for r, (wi, ng) in enumerate(rounds):
                        if r > 0:
                            S.wait(t, "eq", eq0 + r * 8)
                        for k in range(16):
                            j = r * 16 + k
                            S.wait(t, "cw", cw0 + j + 1)
                            for i in range(NTT):
                                for n2 in range(2):
                                    inst = t.matmul(
                                        pj[i * 2 + n2][:],
                                        lhsT=yT[:, k * HALF + i * 128:
                                                k * HALF + (i + 1) * 128],
                                        rhs=wb[:, (j % 2) * 1024 + n2 * 512:
                                               (j % 2) * 1024 + (n2 + 1) * 512],
                                        start=(k == 0), stop=(k == 15))
                            S.inc(inst, "uw")

                @block.scalar
                def _(s, half=half, uw0=uw0):
                    for r, (wi, ng) in enumerate(rounds):
                        S.wait(s, "uw", uw0 + (r + 1) * 16)
                        for i in range(NTT):
                            for n2 in range(2):
                                S.inc(s.copy(
                                    out=DSTS[wi][:, i * D + ng * 1024 + n2 * 512:
                                                 i * D + ng * 1024 + (n2 + 1) * 512],
                                    in_=pj[i * 2 + n2][:]), "eq")
            qkv_done = S.n["eq"]

            if dbg and half == 0:
                db0 = S.reserve("dbst", 16 * 7)

                @block.scalar
                def _(s, yT_done=yT_done, qkv_done=qkv_done):
                    S.wait(s, "tev", yT_done)
                    S.inc(s.dma_start(out=d_yT, in_=yT[:]), "dbst", 16)
                    S.inc(s.dma_start(out=d_stats, in_=stats[:]), "dbst", 16)
                    S.wait(s, "eq", qkv_done)
                    S.inc(s.dma_start(out=d_q, in_=q_tm[:]), "dbst", 16)
                    S.inc(s.dma_start(out=d_k, in_=k_tm[:]), "dbst", 16)
                    S.inc(s.dma_start(out=d_v, in_=v_tm[:]), "dbst", 16)

            es0 = S.reserve("es", 2)

            @block.vector
            def _(v, qkv_done=qkv_done):
                S.wait(v, "eq", qkv_done)
                inst = None
                with nc.allow_low_precision(reason="bf16 rounding of matmul-equiv einsum output"):
                    for i in range(NTT):
                        for h0 in range(0, NH, 2):
                            inst = _einsum_pair(v, S, i, h0, 2, q_tm, k_tm, v_tm,
                                                o_tm, scor, prod)
                S.inc(v.tensor_copy(out=prod[:, 0:1], in_=scor[:, 0:1]), "es", 2)

            if dbg and half == 0:
                @block.scalar
                def _(s, es0=es0):
                    S.wait(s, "es", es0 + 2)
                    S.inc(s.dma_start(out=d_scor, in_=scor[:]), "dbst", 16)
                    S.inc(s.dma_start(out=d_o, in_=o_tm[:]), "dbst", 16)

            with ExitStack() as ph:
                ptr = [ph.enter_context(nc.psum_tensor(f"ptE{half}_{j}", [128, 128], BF16))
                       for j in range(2)]
                items = [(i, f) for i in range(NTT) for f in range(NFT)]
                emit_transposes(
                    block, S, items,
                    src_fn=lambda it: o_tm[:, it[0] * D + it[1] * 128:
                                           it[0] * D + (it[1] + 1) * 128],
                    dst_fn=lambda it: oT[:, it[1] * HALF + it[0] * 128:
                                         it[1] * HALF + (it[0] + 1) * 128],
                    ident=ident[:], ptr=ptr,
                    pe_pre=lambda t, es0=es0: S.wait(t, "es", es0 + 2))
            oT_done = S.n["tev"]

            with ExitStack() as ph:
                pj = [ph.enter_context(nc.psum_tensor(f"pjF{half}_{j}", [128, 512], F32))
                      for j in range(8)]
                strip_items = [(ng, k) for ng in range(2) for k in range(16)]
                cw0, uw0 = emit_wstream(
                    block, S, strip_items, 2,
                    load_fn=lambda sync, it, slot: sync.dma_start(
                        out=wf[:, slot * 1024:(slot + 1) * 1024],
                        in_=W_in["Wo"][it[1] * 128:(it[1] + 1) * 128,
                                       it[0] * 1024:(it[0] + 1) * 1024]),
                    cast_fn=lambda g, it, slot: g.tensor_copy(
                        out=wb[:, slot * 1024:(slot + 1) * 1024],
                        in_=wf[:, slot * 1024:(slot + 1) * 1024]))

                ra0 = S.reserve("ra", 2 * 8)

                @block.tensor
                def _(t, cw0=cw0, ra0=ra0, oT_done=oT_done):
                    S.wait(t, "tev", oT_done)
                    for r in range(2):
                        if r > 0:
                            S.wait(t, "ra", ra0 + r * 8)
                        for k in range(16):
                            j = r * 16 + k
                            S.wait(t, "cw", cw0 + j + 1)
                            for i in range(NTT):
                                for n2 in range(2):
                                    inst = t.matmul(
                                        pj[i * 2 + n2][:],
                                        lhsT=oT[:, k * HALF + i * 128:
                                                k * HALF + (i + 1) * 128],
                                        rhs=wb[:, (j % 2) * 1024 + n2 * 512:
                                               (j % 2) * 1024 + (n2 + 1) * 512],
                                        start=(k == 0), stop=(k == 15))
                            S.inc(inst, "uw")

                @block.vector
                def _(v, uw0=uw0, ra0=ra0, lr0=lr0):
                    S.wait(v, "lr", lr0 + 16 * NTT)
                    for r in range(2):
                        S.wait(v, "uw", uw0 + (r + 1) * 16)
                        for i in range(NTT):
                            for n2 in range(2):
                                c = i * D + r * 1024 + n2 * 512
                                S.inc(v.tensor_tensor(
                                    out=xr[:, c:c + 512], in0=xr[:, c:c + 512],
                                    in1=pj[i * 2 + n2][:], op=ALU.add), "ra")

                st0 = S.reserve("st", 16 * NTT)

                @block.scalar
                def _(s, ra0=ra0, t0=t0):
                    S.wait(s, "ra", ra0 + 16)
                    for i in range(NTT):
                        S.inc(s.dma_start(out=xov[:, t0 + i, :],
                                          in_=xr[:, i * D:(i + 1) * D]), "st", 16)

        final_st = S.n["st"]

        @block.gpsimd
        def _(g):
            S.wait(g, "st", final_st)

    return nc


def _einsum_pair(v, S, i, h0, nh, q_tm, k_tm, v_tm, o_tm, scor, prod):
    NH, HD = 16, 128
    Dl = NH * HD
    hs = [h0 + a for a in range(nh)]
    for a, h in enumerate(hs):
        inst = v.tensor_tensor(
            out=prod[:, a * Dl:(a + 1) * Dl].rearrange("p (k d) -> p k d", d=HD),
            in0=q_tm[:, i * Dl + h * HD:i * Dl + (h + 1) * HD].unsqueeze(1)
                .broadcast_to([128, NH, HD]),
            in1=k_tm[:, i * Dl:(i + 1) * Dl].rearrange("p (k d) -> p k d", d=HD),
            op=ALU.mult)
    vfence(v, S, inst)
    for a, h in enumerate(hs):
        inst = v.reduce_sum(
            out=scor[:, i * 256 + h * 16:i * 256 + (h + 1) * 16],
            in_=prod[:, a * Dl:(a + 1) * Dl].rearrange("p (k d) -> p k d", d=HD),
            axis=AX.X)
    vfence(v, S, inst)
    for a, h in enumerate(hs):
        inst = v.tensor_tensor(
            out=prod[:, a * Dl:(a + 1) * Dl].rearrange("p (d k) -> p d k", k=NH),
            in0=scor[:, i * 256 + h * 16:i * 256 + (h + 1) * 16].unsqueeze(1)
                .broadcast_to([128, HD, NH]),
            in1=v_tm[:, i * Dl:(i + 1) * Dl].rearrange("p (k d) -> p d k", d=HD),
            op=ALU.mult)
    vfence(v, S, inst)
    for a, h in enumerate(hs):
        inst = v.reduce_sum(
            out=o_tm[:, i * Dl + h * HD:i * Dl + (h + 1) * HD],
            in_=prod[:, a * Dl:(a + 1) * Dl].rearrange("p (d k) -> p d k", k=NH),
            axis=AX.X)
    vfence(v, S, inst)
    return inst


def emit_wprep(block, S, strips, sint, sf32, sbf, wts, ptr, ident,
               transform, dst_fn, tag="pw", first_wait=None):
    n = len(strips)
    nch = [it["C"] // 128 for it in strips]
    pfx = [0]
    for c in nch:
        pfx.append(pfx[-1] + c)
    lbases = [S.reserve(f"{tag}l{k}", 16 * len(range(k, n, 2))) for k in range(2)]
    c0 = S.reserve(tag + "c", n)
    v0 = S.reserve(tag + "v", n)
    t0 = S.reserve(tag + "t", pfx[-1])
    e0 = S.reserve(tag + "e", pfx[-1])
    obases = [S.reserve(f"{tag}o{k}", 16 * len(range(k, n, 2))) for k in range(2)]

    @block.sync
    def _(sync):
        for j, it in enumerate(strips):
            if j >= 2:
                S.wait(sync, tag + "c", c0 + j - 1)
            elif first_wait is not None:
                S.wait(sync, first_wait[0], first_wait[1])
            S.inc(sync.dma_start(out=sint[:, (j % 2) * it["Cmax"]:
                                          (j % 2) * it["Cmax"] + it["C"]],
                                 in_=it["src"]), f"{tag}l{j % 2}", 16)

    @block.gpsimd
    def _(g):
        for j, it in enumerate(strips):
            S.wait(g, f"{tag}l{j % 2}", lbases[j % 2] + 16 * (j // 2 + 1))
            if j >= 1:
                S.wait(g, tag + "v", v0 + j)
            S.inc(transform(g, it,
                            sint[:, (j % 2) * it["Cmax"]:(j % 2) * it["Cmax"] + it["C"]],
                            sf32[:, 0:it["C"]], 0), tag + "c")

    @block.vector
    def _(v):
        for j, it in enumerate(strips):
            S.wait(v, tag + "c", c0 + j + 1)
            if j >= 2:
                S.wait(v, tag + "t", t0 + pfx[j - 1])
            S.inc(transform(v, it, sf32[:, 0:it["C"]],
                            sbf[:, (j % 2) * it["Cmax"]:(j % 2) * it["Cmax"] + it["C"]],
                            1), tag + "v")

    @block.tensor
    def _(t):
        if "id" in S.n:
            S.wait(t, "id", 1)
        tp = 0
        for j, it in enumerate(strips):
            S.wait(t, tag + "v", v0 + j + 1)
            for c in range(nch[j]):
                if tp >= 2:
                    S.wait(t, tag + "e", e0 + tp - 1)
                S.inc(t.transpose(ptr[tp % 2][:],
                                  in_=sbf[:, (j % 2) * it["Cmax"] + c * 128:
                                          (j % 2) * it["Cmax"] + (c + 1) * 128],
                                  identity=ident), tag + "t")
                tp += 1

    @block.scalar
    def _(s):
        tp = 0
        for j, it in enumerate(strips):
            if j >= 2:
                S.wait(s, f"{tag}o{j % 2}",
                       obases[j % 2] + 16 * (j // 2))
            for c in range(nch[j]):
                S.wait(s, tag + "t", t0 + tp + 1)
                S.inc(s.copy(out=wts[:, (j % 2) * it["Cmax"] + c * 128:
                                      (j % 2) * it["Cmax"] + (c + 1) * 128],
                             in_=ptr[tp % 2][:]), tag + "e")
                tp += 1
            S.wait(s, tag + "e", e0 + tp)
            S.inc(s.dma_start(
                out=dst_fn(it),
                in_=wts[:, (j % 2) * it["Cmax"]:(j % 2) * it["Cmax"] + it["C"]]
                    .rearrange("p (c j) -> p c j", j=128)), f"{tag}o{j % 2}", 16)

    return [(f"{tag}o{k}", obases[k] + 16 * len(range(k, n, 2))) for k in range(2)]


def emit_wload(block, S, items, nbuf, load_fn, tag="wl"):
    n = len(items)
    bases = [S.reserve(f"{tag}d{k}", 16 * len(range(k, n, nbuf)))
             for k in range(nbuf)]
    u0 = S.reserve(tag + "u", n)

    @block.sync
    def _(sync):
        for j, it in enumerate(items):
            val = u0 + j - nbuf + 1 if j >= nbuf else u0
            if val > 0:
                S.wait(sync, tag + "u", val)
            S.inc(load_fn(sync, it, j % nbuf), f"{tag}d{j % nbuf}", 16)

    def wait_item(eng, j):
        S.wait(eng, f"{tag}d{j % nbuf}", bases[j % nbuf] + 16 * (j // nbuf + 1))

    return wait_item, u0


def emit_mm(block, S, t, rounds, wtag, wd0, wu0, eq_name, eq0, eq_per_round,
            pre=None):
    if pre:
        pre(t)
    for r, rd in enumerate(rounds):
        if r > 0 and eq_name is not None:
            S.wait(t, eq_name, eq0 + r * eq_per_round)
        nk = rd["nk"]
        for k in range(nk):
            if rd.get("witem") is not None:
                j = rd["witem"] + k
                S.wait(t, wtag + "d", wd0 + 16 * (j + 1))
            inst = None
            for (psum_fn, lhsT_fn, rhs_fn) in rd["mms"]:
                inst = t.matmul(psum_fn(k), lhsT=lhsT_fn(k), rhs=rhs_fn(k),
                                start=(k == 0), stop=(k == nk - 1))
            if rd.get("witem") is not None:
                S.inc(inst, wtag + "u")
        if rd.get("witem") is None:
            S.inc(inst, wtag + "u")


def build_L1f(tok=1024, dbg=False):
    nc = bass.Bass("TRN2", target_bir_lowering=False, debug=False)
    HID = 5632
    NM = HID // 128
    x_in = nc.dram_tensor("x", [tok, D], F32, kind="ExternalInput").ap()
    wq_g = nc.dram_tensor("wq_gate", [HID, D], I32, kind="ExternalInput").ap()
    wq_u = nc.dram_tensor("wq_up", [HID, D], I32, kind="ExternalInput").ap()
    wq_d = nc.dram_tensor("wq_down", [D, HID], I32, kind="ExternalInput").ap()
    s_g = nc.dram_tensor("s_gate", [HID], F32, kind="ExternalInput").ap()
    s_u = nc.dram_tensor("s_up", [HID], F32, kind="ExternalInput").ap()
    s_d = nc.dram_tensor("s_down", [D], F32, kind="ExternalInput").ap()
    x_out = nc.dram_tensor("x_out", [tok, D], F32, kind="ExternalOutput").ap()

    WgT = nc.dram_tensor("WgT", [D, HID], BF16, kind="Internal").ap()
    WuT = nc.dram_tensor("WuT", [D, HID], BF16, kind="Internal").ap()
    WdT = nc.dram_tensor("WdT", [HID, D], BF16, kind="Internal").ap()

    HALF = tok // 2
    NTT = HALF // 128

    with ExitStack() as st:
        sb = lambda n_, sh, dt=F32: st.enter_context(nc.sbuf_tensor(n_, sh, dt))
        S = Sems(nc, st)

        ident = sb("ident", [128, 128], BF16)
        xr = sb("xr", [128, NTT * D])
        stats = sb("stats", [128, 24])
        lnscr = sb("lnscr", [128, D])
        ynt = sb("ynt", [128, D], BF16)
        yT = sb("yT", [128, NFT * HALF], BF16)
        sint = sb("sint", [128, 2 * D], I32)
        sf32 = sb("sf32", [128, D])
        sbf = sb("sbf", [128, 2 * D], BF16)
        wts = sb("wts", [128, 2 * D], BF16)
        scol = sb("scol", [128, 108])
        wbuf = sb("wbuf", [128, 4 * D], BF16)
        h2 = sb("h2", [128, NM * HALF], BF16)
        gsil = sb("gsil", [128, HALF])

        block = st.enter_context(nc.Block())
        emit_ident(block, S, ident[:])

        xv = x_in.rearrange("(i p) d -> p i d", p=128)
        xov = x_out.rearrange("(i p) d -> p i d", p=128)

        sc0 = S.reserve("sc", 16 * (NM + NM + NFT))

        @block.sync
        def _(sync):
            for m in range(NM):
                S.inc(sync.dma_start(out=scol[:, m:m + 1],
                                     in_=s_g[m * 128:(m + 1) * 128].unsqueeze(1)),
                      "sc", 16)
            for m in range(NM):
                S.inc(sync.dma_start(out=scol[:, 44 + m:45 + m],
                                     in_=s_u[m * 128:(m + 1) * 128].unsqueeze(1)),
                      "sc", 16)
            for m in range(NFT):
                S.inc(sync.dma_start(out=scol[:, 88 + m:89 + m],
                                     in_=s_d[m * 128:(m + 1) * 128].unsqueeze(1)),
                      "sc", 16)

        n128_0 = S.reserve("n128", 1)

        @block.vector
        def _(v):
            S.inc(v.memset(scol[:, 104:105], -128.0), "n128")

        strips = []
        for m in range(NM):
            strips.append(dict(src=wq_g[m * 128:(m + 1) * 128, :], C=D, Cmax=D,
                               sc=scol[:, m:m + 1],
                               dst=WgT.rearrange("(c p) (m j) -> p c m j",
                                                 p=128, j=128)[:, :, m, :]))
        for m in range(NM):
            strips.append(dict(src=wq_u[m * 128:(m + 1) * 128, :], C=D, Cmax=D,
                               sc=scol[:, 44 + m:45 + m],
                               dst=WuT.rearrange("(c p) (m j) -> p c m j",
                                                 p=128, j=128)[:, :, m, :]))
        for o in range(NFT):
            for hc in range(4):
                strips.append(dict(
                    src=wq_d[o * 128:(o + 1) * 128, hc * 1408:(hc + 1) * 1408],
                    C=1408, Cmax=D, sc=scol[:, 88 + o:89 + o],
                    dst=WdT.rearrange("(a p) (o j) -> p a o j",
                                      p=128, j=128)[:, hc * 11:(hc + 1) * 11, o, :]))

        def transform(eng, it, src, dst, stage):
            if stage == 0:
                return eng.tensor_copy(out=dst, in_=src)
            S.wait(eng, "sc", sc0 + 16 * (NM + NM + NFT))
            S.wait(eng, "n128", n128_0 + 1)
            return eng.tensor_scalar(out=dst, in0=src, scalar1=scol[:, 104:105],
                                     scalar2=it["sc"], op0=ALU.add, op1=ALU.mult)

        with ExitStack() as ph:
            ptr = [ph.enter_context(nc.psum_tensor(f"ppw{j}", [128, 128], BF16))
                   for j in range(2)]
            prep_done = emit_wprep(block, S, strips, sint, sf32, sbf, wts, ptr,
                                   ident[:], transform,
                                   dst_fn=lambda it: it["dst"], tag="pw")

        for half in range(2):
            t0 = half * NTT
            lr0 = S.reserve("lr", 16 * NTT)
            prev_st = S.n.get("st", 0)

            @block.sync
            def _(sync, t0=t0, prev_st=prev_st, half=half):
                for i in range(NTT):
                    if half > 0:
                        S.wait(sync, "st", prev_st)
                    S.inc(sync.dma_start(out=xr[:, i * D:(i + 1) * D],
                                         in_=xv[:, t0 + i, :]), "lr", 16)

            tpB0 = S.n.get("tp", 0)
            ln0 = emit_ln(
                block, S, xsrc_fn=lambda i: xr[:, i * D:(i + 1) * D],
                ntiles=NTT, stats=stats, lnscr=lnscr,
                apply_fn=lambda v, i, mean, rstd: v.tensor_scalar(
                    out=ynt[:], in0=xr[:, i * D:(i + 1) * D],
                    scalar1=mean, scalar2=rstd, op0=ALU.subtract, op1=ALU.mult),
                lr_wait=("lr", lr0 + 16 * NTT),
                tp_base_fn=lambda i, tpB0=tpB0: ("tp", tpB0 + i * NFT)
                if (i > 0 or tpB0 > 0) else (None, 0))

            with ExitStack() as ph:
                ptr = [ph.enter_context(nc.psum_tensor(f"ptA{half}_{j}",
                                                       [128, 128], BF16))
                       for j in range(2)]
                items = [(i, f) for i in range(NTT) for f in range(NFT)]
                emit_transposes(
                    block, S, items,
                    src_fn=lambda it: ynt[:, it[1] * 128:(it[1] + 1) * 128],
                    dst_fn=lambda it: yT[:, it[1] * HALF + it[0] * 128:
                                         it[1] * HALF + (it[0] + 1) * 128],
                    ident=ident[:], ptr=ptr,
                    src_tick=lambda j, ln0=ln0: ("ln", ln0 + j // NFT + 1))
            yT_done = S.n["tev"]

            with ExitStack() as ph:
                pj = [ph.enter_context(nc.psum_tensor(f"pg{half}_{j}",
                                                      [128, 512], F32))
                      for j in range(8)]
                witems = [(w, m) for m in range(NM) for w in range(2)]
                WT = [WgT, WuT]
                wld, wu0 = emit_wload(
                    block, S, witems, 4,
                    load_fn=lambda sync, it, slot: sync.dma_start(
                        out=wbuf[:, slot * D:(slot + 1) * D],
                        in_=WT[it[0]].rearrange("(k p) (m j) -> p k m j",
                                                p=128, j=128)[:, :, it[1], :]),
                    tag="wl")

                gs0 = S.reserve("gs", NM)
                fq0 = S.reserve("fq", NM)

                @block.tensor
                def _(t, half=half, wld=wld, yT_done=yT_done, fq0=fq0):
                    S.wait(t, "tev", yT_done)
                    if half == 0:
                        for nm_, val_ in prep_done:
                            S.wait(t, nm_, val_)
                    for m in range(NM):
                        if m >= 4:
                            S.wait(t, "fq", fq0 + m - 3)
                        for w in range(2):
                            j = m * 2 + w
                            wld(t, j)
                            for k in range(NFT):
                                inst = t.matmul(
                                    pj[(m % 4) * 2 + w][:],
                                    lhsT=wbuf[:, (j % 4) * D + k * 128:
                                              (j % 4) * D + (k + 1) * 128],
                                    rhs=yT[:, k * HALF:k * HALF + HALF],
                                    start=(k == 0), stop=(k == NFT - 1))
                            S.inc(inst, "wlu")

                @block.scalar
                def _(s, half=half, wu0=wu0, gs0=gs0, fq0=fq0):
                    for m in range(NM):
                        S.wait(s, "wlu", wu0 + 2 * m + 2)
                        if m > 0:
                            S.wait(s, "fq", fq0 + m)
                        S.inc(s.activation(out=gsil[:], in_=pj[(m % 4) * 2][:],
                                           func=AF.Silu), "gs")

                @block.vector
                def _(v, gs0=gs0, fq0=fq0):
                    for m in range(NM):
                        S.wait(v, "gs", gs0 + m + 1)
                        S.inc(v.tensor_tensor(out=h2[:, m * HALF:(m + 1) * HALF],
                                              in0=gsil[:],
                                              in1=pj[(m % 4) * 2 + 1][:],
                                              op=ALU.mult), "fq")

            with ExitStack() as ph:
                pj = [ph.enter_context(nc.psum_tensor(f"pd{half}_{j}",
                                                      [128, 512], F32))
                      for j in range(8)]
                witems = [(ng, k) for ng in range(2) for k in range(NM)]
                wld, wu0 = emit_wload(
                    block, S, witems, 4,
                    load_fn=lambda sync, it, slot: sync.dma_start(
                        out=wbuf[:, slot * D:slot * D + 1024],
                        in_=WdT[it[1] * 128:(it[1] + 1) * 128,
                                it[0] * 1024:(it[0] + 1) * 1024]),
                    tag="wl")
                ra0 = S.reserve("ra", 2 * 8)

                @block.tensor
                def _(t, wld=wld, ra0=ra0, fq0=fq0):
                    S.wait(t, "fq", fq0 + NM)
                    for r in range(2):
                        if r > 0:
                            S.wait(t, "ra", ra0 + r * 8)
                        for k in range(NM):
                            j = r * NM + k
                            wld(t, j)
                            for i in range(NTT):
                                for n2 in range(2):
                                    inst = t.matmul(
                                        pj[i * 2 + n2][:],
                                        lhsT=h2[:, k * HALF + i * 128:
                                                k * HALF + (i + 1) * 128],
                                        rhs=wbuf[:, (j % 4) * D + n2 * 512:
                                                 (j % 4) * D + (n2 + 1) * 512],
                                        start=(k == 0), stop=(k == NM - 1))
                            S.inc(inst, "wlu")

                @block.vector
                def _(v, wu0=wu0, ra0=ra0, lr0=lr0):
                    S.wait(v, "lr", lr0 + 16 * NTT)
                    for r in range(2):
                        S.wait(v, "wlu", wu0 + (r + 1) * NM)
                        for i in range(NTT):
                            for n2 in range(2):
                                c = i * D + r * 1024 + n2 * 512
                                S.inc(v.tensor_tensor(
                                    out=xr[:, c:c + 512], in0=xr[:, c:c + 512],
                                    in1=pj[i * 2 + n2][:], op=ALU.add), "ra")

                st0 = S.reserve("st", 16 * NTT)

                @block.scalar
                def _(s, ra0=ra0, t0=t0):
                    S.wait(s, "ra", ra0 + 16)
                    for i in range(NTT):
                        S.inc(s.dma_start(out=xov[:, t0 + i, :],
                                          in_=xr[:, i * D:(i + 1) * D]), "st", 16)

        final_st = S.n["st"]

        @block.gpsimd
        def _(g):
            S.wait(g, "st", final_st)

    return nc


def build_L0f(tok=1024, dbg=False):
    nc = bass.Bass("TRN2", target_bir_lowering=False, debug=False)
    HID = 5632
    NM = HID // 128
    VD = 4096
    NV = VD // 128
    x_in = nc.dram_tensor("x", [tok, D], F32, kind="ExternalInput").ap()
    o0T = nc.dram_tensor("o0T", [VD, tok], BF16, kind="ExternalInput").ap()
    W_out = nc.dram_tensor("W_out", [D, VD], F32, kind="ExternalInput").ap()
    w1 = nc.dram_tensor("w1", [HID, D], F32, kind="ExternalInput").ap()
    w2 = nc.dram_tensor("w2", [HID, D], F32, kind="ExternalInput").ap()
    w3 = nc.dram_tensor("w3", [D, HID], F32, kind="ExternalInput").ap()
    x_out = nc.dram_tensor("x_out", [tok, D], F32, kind="ExternalOutput").ap()

    WoT = nc.dram_tensor("WoT", [VD, D], BF16, kind="Internal").ap()
    W1T = nc.dram_tensor("W1T", [D, HID], BF16, kind="Internal").ap()
    W2T = nc.dram_tensor("W2T", [D, HID], BF16, kind="Internal").ap()
    W3T = nc.dram_tensor("W3T", [HID, D], BF16, kind="Internal").ap()

    HALF = tok // 2
    NTT = HALF // 128

    with ExitStack() as st:
        sb = lambda n_, sh, dt=F32: st.enter_context(nc.sbuf_tensor(n_, sh, dt))
        S = Sems(nc, st)

        ident = sb("ident", [128, 128], BF16)
        xr = sb("xr", [128, NTT * D])
        stats = sb("stats", [128, 24])
        lnscr = sb("lnscr", [128, D])
        ynt = sb("ynt", [128, D], BF16)
        yT = sb("yT", [128, NFT * HALF], BF16)
        sint = sb("sint", [128, 2 * D])
        sf32 = sb("sf32", [128, D])
        sbf = sb("sbf", [128, 2 * D], BF16)
        wts = sb("wts", [128, 2 * D], BF16)
        iscr = sb("iscr", [128, D], I32)
        scol = sb("scol", [128, 16])
        gacc = sb("gacc", [128, 160])
        onerow = st.enter_context(nc.sbuf_tensor("onerow", [1, 128], F32))
        wbuf = sb("wbuf", [128, 4 * D], BF16)
        h2 = sb("h2", [128, NM * HALF], BF16)
        gsig = sb("gsig", [128, HALF])
        oT = sb("oT", [128, NV * HALF], BF16)

        block = st.enter_context(nc.Block())
        emit_ident(block, S, ident[:])

        xv = x_in.rearrange("(i p) d -> p i d", p=128)
        xov = x_out.rearrange("(i p) d -> p i d", p=128)

        on0 = S.reserve("on", 1)

        @block.vector
        def _(v):
            v.memset(scol[:, 8:9], 1.0)
            S.inc(v.memset(onerow[:], 1.0), "on")

        gchunks = []
        for m in range(NM):
            gchunks.append(dict(src=w1[m * 128:(m + 1) * 128, :], C=D, w=0))
        for m in range(NM):
            gchunks.append(dict(src=w2[m * 128:(m + 1) * 128, :], C=D, w=1))
        for o in range(NFT):
            for hc in range(4):
                gchunks.append(dict(
                    src=w3[o * 128:(o + 1) * 128, hc * 1408:(hc + 1) * 1408],
                    C=1408, w=2))
        ng = len(gchunks)
        glb = [S.reserve(f"gl{k}", 16 * len(range(k, ng, 2))) for k in range(2)]
        ga0 = S.reserve("ga", ng)

        @block.sync
        def _(sync):
            for j, it in enumerate(gchunks):
                if j >= 2:
                    S.wait(sync, "ga", ga0 + j - 1)
                S.inc(sync.dma_start(out=sint[:, (j % 2) * D:(j % 2) * D + it["C"]],
                                     in_=it["src"]), f"gl{j % 2}", 16)

        @block.scalar
        def _(s_):
            for j, it in enumerate(gchunks):
                S.wait(s_, f"gl{j % 2}", glb[j % 2] + 16 * (j // 2 + 1))
                if j > 0:
                    S.wait(s_, "ga", ga0 + j)
                S.inc(s_.activation(out=lnscr[:, 0:it["C"]],
                                    in_=sint[:, (j % 2) * D:(j % 2) * D + it["C"]],
                                    func=AF.Abs, accum_out=gacc[:, j:j + 1]),
                      "ga")

        gv0 = S.reserve("gv", 1)
        ranges = [(0, NM), (NM, 2 * NM), (2 * NM, ng)]

        @block.vector
        def _(v):
            S.wait(v, "ga", ga0 + ng)
            for w in range(3):
                a, b = ranges[w]
                inst = v.reduce_sum(out=gacc[:, 152 + w:153 + w],
                                    in_=gacc[:, a:b], axis=AX.X)
            S.inc(inst, "gv")

        with ExitStack() as ph:
            pg = ph.enter_context(nc.psum_tensor("pgm", [128, 8], F32))
            gm0 = S.reserve("gm", 2)
            gi0 = S.reserve("gi", 2)

            @block.tensor
            def _(t):
                S.wait(t, "gv", gv0 + 1)
                S.wait(t, "on", on0 + 1)
                for w in range(3):
                    inst = t.matmul(pg[0:1, w:w + 1], lhsT=scol[:, 8:9],
                                    rhs=gacc[:, 152 + w:153 + w],
                                    start=True, stop=True)
                S.inc(inst, "gm")
                S.wait(t, "gi", gi0 + 1)
                for w in range(3):
                    inst = t.matmul(pg[:, 4 + w:5 + w], lhsT=onerow[:],
                                    rhs=gacc[0:1, 156 + w:157 + w],
                                    start=True, stop=True)
                S.inc(inst, "gm")

            @block.vector
            def _(v):
                S.wait(v, "gm", gm0 + 1)
                counts = [float(HID * D), float(HID * D), float(D * HID)]
                for w in range(3):
                    inst = v.tensor_scalar(out=gacc[0:1, 156 + w:157 + w],
                                           in0=pg[0:1, w:w + 1],
                                           scalar1=1.0 / counts[w], scalar2=1e-8,
                                           op0=ALU.mult, op1=ALU.add)
                vfence(v, S, inst)
                for w in range(3):
                    inst = v.reciprocal(out=gacc[0:1, 156 + w:157 + w],
                                        in_=gacc[0:1, 156 + w:157 + w])
                S.inc(inst, "gi")
                S.wait(v, "gm", gm0 + 2)
                for w in range(3):
                    inst = v.tensor_copy(out=scol[:, w:w + 1], in_=pg[:, 4 + w:5 + w])
                S.inc(inst, "gi")
        giv_done = S.n["gi"]

        strips = []
        for o in range(NFT):
            for vc in range(2):
                strips.append(dict(src=W_out[o * 128:(o + 1) * 128,
                                             vc * 2048:(vc + 1) * 2048],
                                   C=D, Cmax=D, kind="cast",
                                   dst=WoT.rearrange("(c p) (m j) -> p c m j",
                                                     p=128, j=128)
                                       [:, vc * 16:(vc + 1) * 16, o, :]))
        for m in range(NM):
            strips.append(dict(src=w1[m * 128:(m + 1) * 128, :], C=D, Cmax=D,
                               kind="quant", sc=scol[:, 0:1],
                               dst=W1T.rearrange("(c p) (m j) -> p c m j",
                                                 p=128, j=128)[:, :, m, :]))
        for m in range(NM):
            strips.append(dict(src=w2[m * 128:(m + 1) * 128, :], C=D, Cmax=D,
                               kind="quant", sc=scol[:, 1:2],
                               dst=W2T.rearrange("(c p) (m j) -> p c m j",
                                                 p=128, j=128)[:, :, m, :]))
        for o in range(NFT):
            for hc in range(4):
                strips.append(dict(
                    src=w3[o * 128:(o + 1) * 128, hc * 1408:(hc + 1) * 1408],
                    C=1408, Cmax=D, kind="quant", sc=scol[:, 2:3],
                    dst=W3T.rearrange("(a p) (o j) -> p a o j",
                                      p=128, j=128)[:, hc * 11:(hc + 1) * 11, o, :]))

        def transform(eng, it, src, dst, stage):
            if it["kind"] == "cast":
                return eng.tensor_copy(out=dst, in_=src)
            if stage == 0:
                S.wait(eng, "gi", giv_done)
                return eng.tensor_scalar_mul(out=dst, in0=src, scalar1=it["sc"])
            C = it["C"]
            MAGIC = 12582912.0
            inst = eng.tensor_scalar(out=sf32[:, 0:C], in0=src,
                                     scalar1=MAGIC, scalar2=-MAGIC,
                                     op0=ALU.add, op1=ALU.add)
            vfence(eng, S, inst)
            return eng.tensor_scalar(out=dst, in0=sf32[:, 0:C],
                                     scalar1=-1.0, scalar2=1.0,
                                     op0=ALU.max, op1=ALU.min)

        with ExitStack() as ph:
            ptr = [ph.enter_context(nc.psum_tensor(f"ppw{j}", [128, 128], BF16))
                   for j in range(2)]
            prep_done = emit_wprep(block, S, strips, sint, sf32, sbf, wts, ptr,
                                   ident[:], transform,
                                   dst_fn=lambda it: it["dst"], tag="pw",
                                   first_wait=("ga", ga0 + ng))

        for half in range(2):
            t0 = half * NTT
            lr0 = S.reserve("lr", 16 * NTT)
            lo0 = S.reserve("lo", 16 * NV)
            prev_st = S.n.get("st", 0)

            @block.sync
            def _(sync, t0=t0, prev_st=prev_st, half=half, lo0=lo0):
                for i in range(NTT):
                    if half > 0:
                        S.wait(sync, "st", prev_st)
                    S.inc(sync.dma_start(out=xr[:, i * D:(i + 1) * D],
                                         in_=xv[:, t0 + i, :]), "lr", 16)
                for vc in range(NV):
                    if half > 0:
                        S.wait(sync, "dlu", S.n["dlu"])
                    S.inc(sync.dma_start(
                        out=oT[:, vc * HALF:(vc + 1) * HALF],
                        in_=o0T[vc * 128:(vc + 1) * 128,
                                half * HALF:(half + 1) * HALF]), "lo", 16)

            with ExitStack() as ph:
                pj = [ph.enter_context(nc.psum_tensor(f"pdl{half}_{j}",
                                                      [128, 512], F32))
                      for j in range(8)]
                witems = [(ng_, k) for ng_ in range(2) for k in range(NV)]
                wld, wu0 = emit_wload(
                    block, S, witems, 4,
                    load_fn=lambda sync, it, slot: sync.dma_start(
                        out=wbuf[:, slot * D:slot * D + 1024],
                        in_=WoT[it[1] * 128:(it[1] + 1) * 128,
                                it[0] * 1024:(it[0] + 1) * 1024]),
                    tag="wl")
                dlu0 = S.reserve("dlu", 1)
                ra0 = S.reserve("ra", 2 * 8)

                @block.tensor
                def _(t, wld=wld, ra0=ra0, lo0=lo0, half=half):
                    if half == 0:
                        for nm_, val_ in prep_done:
                            S.wait(t, nm_, val_)
                    S.wait(t, "lo", lo0 + 16 * NV)
                    for r in range(2):
                        if r > 0:
                            S.wait(t, "ra", ra0 + r * 8)
                        for k in range(NV):
                            j = r * NV + k
                            wld(t, j)
                            for i in range(NTT):
                                for n2 in range(2):
                                    inst = t.matmul(
                                        pj[i * 2 + n2][:],
                                        lhsT=oT[:, k * HALF + i * 128:
                                                k * HALF + (i + 1) * 128],
                                        rhs=wbuf[:, (j % 4) * D + n2 * 512:
                                                 (j % 4) * D + (n2 + 1) * 512],
                                        start=(k == 0), stop=(k == NV - 1))
                            S.inc(inst, "wlu")

                @block.vector
                def _(v, wu0=wu0, ra0=ra0, lr0=lr0):
                    S.wait(v, "lr", lr0 + 16 * NTT)
                    for r in range(2):
                        S.wait(v, "wlu", wu0 + (r + 1) * NV)
                        for i in range(NTT):
                            for n2 in range(2):
                                c = i * D + r * 1024 + n2 * 512
                                inst = v.tensor_tensor(
                                    out=xr[:, c:c + 512], in0=xr[:, c:c + 512],
                                    in1=pj[i * 2 + n2][:], op=ALU.add)
                                S.inc(inst, "ra")
                    S.inc(v.tensor_copy(out=gacc[:, 159:160], in_=gacc[:, 152:153]),
                          "dlu")

            lr1 = S.reserve("lr", 16 * NTT)

            @block.scalar
            def _(s_, ra0=ra0):
                S.wait(s_, "ra", ra0 + 16)
                for i in range(NTT):
                    S.inc(s_.dma_start(out=xov[:, t0 + i, :],
                                       in_=xr[:, i * D:(i + 1) * D]), "lr", 16)

            tpB0 = S.n.get("tp", 0)
            ln0 = emit_ln(
                block, S, xsrc_fn=lambda i: xr[:, i * D:(i + 1) * D],
                ntiles=NTT, stats=stats, lnscr=lnscr,
                apply_fn=lambda v, i, mean, rstd: v.tensor_scalar(
                    out=ynt[:], in0=xr[:, i * D:(i + 1) * D],
                    scalar1=mean, scalar2=rstd, op0=ALU.subtract, op1=ALU.mult),
                lr_wait=("lr", lr1 + 16 * NTT),
                tp_base_fn=lambda i, tpB0=tpB0: ("tp", tpB0 + i * NFT)
                if (i > 0 or tpB0 > 0) else (None, 0))

            with ExitStack() as ph:
                ptr = [ph.enter_context(nc.psum_tensor(f"ptA{half}_{j}",
                                                       [128, 128], BF16))
                       for j in range(2)]
                items = [(i, f) for i in range(NTT) for f in range(NFT)]
                emit_transposes(
                    block, S, items,
                    src_fn=lambda it: ynt[:, it[1] * 128:(it[1] + 1) * 128],
                    dst_fn=lambda it: yT[:, it[1] * HALF + it[0] * 128:
                                         it[1] * HALF + (it[0] + 1) * 128],
                    ident=ident[:], ptr=ptr,
                    src_tick=lambda j, ln0=ln0: ("ln", ln0 + j // NFT + 1))
            yT_done = S.n["tev"]

            with ExitStack() as ph:
                pj = [ph.enter_context(nc.psum_tensor(f"pg{half}_{j}",
                                                      [128, 512], F32))
                      for j in range(8)]
                witems = [(w, m) for m in range(NM) for w in range(2)]
                WT = [W1T, W2T]
                wld, wu0 = emit_wload(
                    block, S, witems, 4,
                    load_fn=lambda sync, it, slot: sync.dma_start(
                        out=wbuf[:, slot * D:(slot + 1) * D],
                        in_=WT[it[0]].rearrange("(k p) (m j) -> p k m j",
                                                p=128, j=128)[:, :, it[1], :]),
                    tag="wl")
                gs0 = S.reserve("gs", NM)
                fq0 = S.reserve("fq", NM)

                @block.tensor
                def _(t, half=half, wld=wld, yT_done=yT_done, fq0=fq0):
                    S.wait(t, "tev", yT_done)
                    for m in range(NM):
                        if m >= 4:
                            S.wait(t, "fq", fq0 + m - 3)
                        for w in range(2):
                            j = m * 2 + w
                            wld(t, j)
                            for k in range(NFT):
                                inst = t.matmul(
                                    pj[(m % 4) * 2 + w][:],
                                    lhsT=wbuf[:, (j % 4) * D + k * 128:
                                              (j % 4) * D + (k + 1) * 128],
                                    rhs=yT[:, k * HALF:k * HALF + HALF],
                                    start=(k == 0), stop=(k == NFT - 1))
                            S.inc(inst, "wlu")

                @block.scalar
                def _(s_, half=half, wu0=wu0, gs0=gs0, fq0=fq0):
                    for m in range(NM):
                        S.wait(s_, "wlu", wu0 + 2 * m + 2)
                        if m > 0:
                            S.wait(s_, "fq", fq0 + m)
                        S.inc(s_.activation(out=gsig[:], in_=pj[(m % 4) * 2][:],
                                            func=AF.Sigmoid), "gs")

                @block.vector
                def _(v, gs0=gs0, fq0=fq0):
                    for m in range(NM):
                        S.wait(v, "gs", gs0 + m + 1)
                        S.inc(v.tensor_tensor(out=h2[:, m * HALF:(m + 1) * HALF],
                                              in0=gsig[:],
                                              in1=pj[(m % 4) * 2 + 1][:],
                                              op=ALU.mult), "fq")

            with ExitStack() as ph:
                pj = [ph.enter_context(nc.psum_tensor(f"pd{half}_{j}",
                                                      [128, 512], F32))
                      for j in range(8)]
                witems = [(ng_, k) for ng_ in range(2) for k in range(NM)]
                wld, wu0 = emit_wload(
                    block, S, witems, 4,
                    load_fn=lambda sync, it, slot: sync.dma_start(
                        out=wbuf[:, slot * D:slot * D + 1024],
                        in_=W3T[it[1] * 128:(it[1] + 1) * 128,
                                it[0] * 1024:(it[0] + 1) * 1024]),
                    tag="wl")
                ra1 = S.reserve("ra", 2 * 8)

                @block.tensor
                def _(t, wld=wld, ra1=ra1, fq0=fq0):
                    S.wait(t, "fq", fq0 + NM)
                    for r in range(2):
                        if r > 0:
                            S.wait(t, "ra", ra1 + r * 8)
                        for k in range(NM):
                            j = r * NM + k
                            wld(t, j)
                            for i in range(NTT):
                                for n2 in range(2):
                                    inst = t.matmul(
                                        pj[i * 2 + n2][:],
                                        lhsT=h2[:, k * HALF + i * 128:
                                                k * HALF + (i + 1) * 128],
                                        rhs=wbuf[:, (j % 4) * D + n2 * 512:
                                                 (j % 4) * D + (n2 + 1) * 512],
                                        start=(k == 0), stop=(k == NM - 1))
                            S.inc(inst, "wlu")

                @block.vector
                def _(v, wu0=wu0, ra1=ra1):
                    for r in range(2):
                        S.wait(v, "wlu", wu0 + (r + 1) * NM)
                        for i in range(NTT):
                            for n2 in range(2):
                                c = i * D + r * 1024 + n2 * 512
                                S.inc(v.tensor_tensor(
                                    out=xr[:, c:c + 512], in0=xr[:, c:c + 512],
                                    in1=pj[i * 2 + n2][:], op=ALU.add), "ra")

                st0 = S.reserve("st", 16 * NTT)

                @block.scalar
                def _(s_, ra1=ra1, t0=t0):
                    S.wait(s_, "ra", ra1 + 16)
                    for i in range(NTT):
                        S.inc(s_.dma_start(out=xov[:, t0 + i, :],
                                           in_=xr[:, i * D:(i + 1) * D]), "st", 16)

        final_st = S.n["st"]

        @block.gpsimd
        def _(g):
            S.wait(g, "st", final_st)

    return nc


def build_L2f(tok=1024, dbg=False):
    nc = bass.Bass("TRN2", target_bir_lowering=False, debug=False)
    x_in = nc.dram_tensor("x", [tok, D], F32, kind="ExternalInput").ap()
    ctx2T = nc.dram_tensor("ctx2T", [D, tok], BF16, kind="ExternalInput").ap()
    WoT2 = nc.dram_tensor("WoT2", [D, D], F32, kind="ExternalInput").ap()
    AoT = nc.dram_tensor("AoT", [D, 32], F32, kind="ExternalInput").ap()
    BoT = nc.dram_tensor("BoT", [32, D], F32, kind="ExternalInput").ap()
    s_o = nc.dram_tensor("s_o", [1], F32, kind="ExternalInput").ap()
    WT = nc.dram_tensor("WT", [D, D], F32, kind="ExternalInput").ap()
    gwT = nc.dram_tensor("gwT", [D, 8], F32, kind="ExternalInput").ap()
    AfT = nc.dram_tensor("AfT", [D, 128], F32, kind="ExternalInput").ap()
    B_re = nc.dram_tensor("B_re", [128, D], F32, kind="ExternalInput").ap()
    x_out = nc.dram_tensor("x_out", [tok, D], F32, kind="ExternalOutput").ap()

    HALF = tok // 2
    NTT = HALF // 128

    with ExitStack() as st:
        sb = lambda n_, sh, dt=F32: st.enter_context(nc.sbuf_tensor(n_, sh, dt))
        S = Sems(nc, st)

        ident = sb("ident", [128, 128], BF16)
        identf = sb("identf", [128, 128])
        xr = sb("xr", [128, NTT * D])
        stats = sb("stats", [128, 24])
        lnscr = sb("lnscr", [128, D])
        ynt = sb("ynt", [128, D], BF16)
        yT = sb("yT", [128, NFT * HALF], BF16)
        cT = sb("cT", [128, NFT * HALF], BF16)
        wf = sb("wf", [128, 2 * 1024])
        wb = sb("wb", [128, 2 * 1024], BF16)
        socol = sb("socol", [128, 1])
        aob = sb("aob", [128, 16 * 32], BF16)
        bob = sb("bob", [32, D], BF16)
        afb = sb("afb", [128, 16 * 128], BF16)
        breb = sb("breb", [128, D], BF16)
        uTs = sb("uTs", [32, HALF], BF16)
        map8 = sb("map8", [8, 128], BF16)
        lgs = sb("lgs", [8, HALF])
        lgtm = sb("lgtm", [128, NTT * 8])
        gscr = sb("gscr", [128, NTT * 8])
        gst = sb("gst", [128, 16])
        wmtm = sb("wmtm", [128, NTT * 8])
        wgfm = sb("wgfm", [8, HALF], BF16)
        Gs = sb("Gs", [128, HALF])
        hGs = sb("hGs", [128, HALF], BF16)

        block = st.enter_context(nc.Block())
        emit_ident(block, S, ident[:])

        id2_0 = S.reserve("id2", 1)
        mp0 = S.reserve("mp", 1)

        @block.gpsimd
        def _(g):
            inst = g.memset(identf[:], 0.0)
            vfence(g, S, inst)
            S.inc(g.affine_select(out=identf[:], in_=identf[:],
                                  compare_op=ALU.not_equal, fill=1.0, base=0,
                                  pattern=[[-1, 128]], channel_multiplier=1), "id2")
            inst = g.memset(map8[:], 1.0)
            vfence(g, S, inst)
            inst = g.affine_select(out=map8[:], in_=map8[:], compare_op=ALU.is_ge,
                                   fill=0.0, base=0, pattern=[[1, 128]],
                                   channel_multiplier=-16)
            vfence(g, S, inst)
            S.inc(g.affine_select(out=map8[:], in_=map8[:], compare_op=ALU.is_ge,
                                  fill=0.0, base=15, pattern=[[-1, 128]],
                                  channel_multiplier=16), "mp")

        xv = x_in.rearrange("(i p) d -> p i d", p=128)
        xov = x_out.rearrange("(i p) d -> p i d", p=128)

        gwf = sb("gwf", [128, 128])
        onerow = st.enter_context(nc.sbuf_tensor("onerow", [1, 128], F32))
        onescol = sb("onescol", [128, 1])
        gsrep = sb("gsrep", [128, 8])
        xTf = sb("xTf", [128, NFT * HALF])
        with ExitStack() as scope:
            aof = scope.enter_context(nc.sbuf_tensor("aof", [128, 512], F32))
            bof = scope.enter_context(nc.sbuf_tensor("bof", [32, D], F32))
            aff = scope.enter_context(nc.sbuf_tensor("aff", [128, D], F32))
            brf = scope.enter_context(nc.sbuf_tensor("brf", [128, D], F32))
            sr0 = S.reserve("sr", 16 * 6)
            cv0 = S.reserve("cv", 1)

            @block.sync
            def _(sync):
                S.inc(sync.dma_start(out=socol[:, 0:1],
                                     in_=s_o.unsqueeze(0).broadcast_to([128, 1])),
                      "sr", 16)
                S.inc(sync.dma_start(out=aof[:],
                                     in_=AoT.rearrange("(k p) r -> p k r", p=128)),
                      "sr", 16)
                S.inc(sync.dma_start(out=bof[:], in_=BoT), "sr", 16)
                S.inc(sync.dma_start(out=gwf[:],
                                     in_=gwT.rearrange("(k p) r -> p k r", p=128)),
                      "sr", 16)
                S.inc(sync.dma_start(out=aff[:],
                                     in_=AfT.rearrange("(k p) r -> p k r", p=128)),
                      "sr", 16)
                S.inc(sync.dma_start(out=brf[:], in_=B_re), "sr", 16)

            @block.vector
            def _(v):
                v.memset(onerow[:], 1.0)
                v.memset(onescol[:], 1.0)
                S.wait(v, "sr", sr0 + 16 * 6)
                v.tensor_copy(out=aob[:], in_=aof[:])
                v.tensor_copy(out=bob[:], in_=bof[:])
                v.tensor_copy(out=afb[:], in_=aff[:])
                S.inc(v.tensor_copy(out=breb[:], in_=brf[:]), "cv")
        res_done = S.n["cv"]

        with ExitStack() as ph:
            pgs = ph.enter_context(nc.psum_tensor("pgs", [1, 8], F32))
            pgr = ph.enter_context(nc.psum_tensor("pgr", [128, 8], F32))
            gg0 = S.reserve("gg", 4)

            @block.tensor
            def _(t):
                S.wait(t, "cv", res_done)
                for k in range(NFT):
                    inst = t.matmul(pgs[:], lhsT=onescol[:],
                                    rhs=gwf[:, k * 8:(k + 1) * 8],
                                    start=(k == 0), stop=(k == NFT - 1))
                S.inc(inst, "gg")
                S.wait(t, "gg", 2)
                S.inc(t.matmul(pgr[:], lhsT=onerow[:], rhs=lgs[0:1, 0:8],
                               start=True, stop=True), "gg")

            @block.scalar
            def _(s_):
                S.wait(s_, "gg", 1)
                S.inc(s_.copy(out=lgs[0:1, 0:8], in_=pgs[:]), "gg")
                S.wait(s_, "gg", 3)
                S.inc(s_.copy(out=gsrep[:], in_=pgr[:]), "gg")
        gsr_done = S.n["gg"]

        for half in range(2):
            t0 = half * NTT
            lr0 = S.reserve("lr", 16 * NTT)
            lc0 = S.reserve("lc", 16 * NFT)
            prev_st = S.n.get("st", 0)
            prev_dl = S.n.get("dl", 0)

            @block.sync
            def _(sync, t0=t0, prev_st=prev_st, prev_dl=prev_dl, half=half):
                for i in range(NTT):
                    if half > 0:
                        S.wait(sync, "st", prev_st)
                    S.inc(sync.dma_start(out=xr[:, i * D:(i + 1) * D],
                                         in_=xv[:, t0 + i, :]), "lr", 16)
                for k in range(NFT):
                    if half > 0:
                        S.wait(sync, "dl", prev_dl)
                    S.inc(sync.dma_start(
                        out=cT[:, k * HALF:(k + 1) * HALF],
                        in_=ctx2T[k * 128:(k + 1) * 128,
                                  half * HALF:(half + 1) * HALF]), "lc", 16)

            with ExitStack() as ph:
                pu = ph.enter_context(nc.psum_tensor(f"pu{half}", [32, 512], F32))
                ut0 = S.reserve("ut", 2)

                @block.tensor
                def _(t, lc0=lc0, half=half):
                    if half == 0:
                        S.wait(t, "cv", res_done)
                    S.wait(t, "lc", lc0 + 16 * NFT)
                    for k in range(NFT):
                        inst = t.matmul(pu[:], lhsT=aob[:, k * 32:(k + 1) * 32],
                                        rhs=cT[:, k * HALF:(k + 1) * HALF],
                                        start=(k == 0), stop=(k == NFT - 1))
                    S.inc(inst, "ut")

                @block.scalar
                def _(s_, ut0=ut0):
                    S.wait(s_, "ut", ut0 + 1)
                    S.inc(s_.copy(out=uTs[:], in_=pu[:]), "ut")

            with ExitStack() as ph:
                pj = [ph.enter_context(nc.psum_tensor(f"pdl{half}_{j}",
                                                      [128, 512], F32))
                      for j in range(8)]
                witems = [(ng_, k) for ng_ in range(2) for k in range(NFT)]
                def _wo_cast(g, it, slot, sr0=sr0):
                    S.wait(g, "sr", sr0 + 16 * 6)
                    return g.tensor_scalar_mul(
                        out=wb[:, slot * 1024:(slot + 1) * 1024],
                        in0=wf[:, slot * 1024:(slot + 1) * 1024],
                        scalar1=socol[:, 0:1])

                wd0, wu0 = emit_wstream(
                    block, S, witems, 2,
                    load_fn=lambda sync, it, slot: sync.dma_start(
                        out=wf[:, slot * 1024:(slot + 1) * 1024],
                        in_=WoT2[it[1] * 128:(it[1] + 1) * 128,
                                 it[0] * 1024:(it[0] + 1) * 1024]),
                    cast_fn=_wo_cast)
                ra0 = S.reserve("ra", 2 * 8)
                dl0 = S.reserve("dl", 2)

                @block.tensor
                def _(t, cw0=wd0, ra0=ra0, ut0=ut0, half=half):
                    S.wait(t, "ut", ut0 + 2)
                    for r in range(2):
                        if r > 0:
                            S.wait(t, "ra", ra0 + r * 8)
                        for k in range(NFT):
                            j = r * NFT + k
                            S.wait(t, "cw", cw0 + j + 1)
                            for i in range(NTT):
                                for n2 in range(2):
                                    inst = t.matmul(
                                        pj[i * 2 + n2][:],
                                        lhsT=cT[:, k * HALF + i * 128:
                                                k * HALF + (i + 1) * 128],
                                        rhs=wb[:, (j % 2) * 1024 + n2 * 512:
                                               (j % 2) * 1024 + (n2 + 1) * 512],
                                        start=(k == 0), stop=False)
                            S.inc(inst, "uw")
                        for i in range(NTT):
                            for n2 in range(2):
                                inst = t.matmul(
                                    pj[i * 2 + n2][:],
                                    lhsT=uTs[:, i * 128:(i + 1) * 128],
                                    rhs=bob[:, r * 1024 + n2 * 512:
                                            r * 1024 + (n2 + 1) * 512],
                                    start=False, stop=True)
                        S.inc(inst, "dl")

                @block.vector
                def _(v, ra0=ra0, lr0=lr0, dl0=dl0):
                    S.wait(v, "lr", lr0 + 16 * NTT)
                    for r in range(2):
                        S.wait(v, "dl", dl0 + r + 1)
                        for i in range(NTT):
                            for n2 in range(2):
                                c = i * D + r * 1024 + n2 * 512
                                S.inc(v.tensor_tensor(
                                    out=xr[:, c:c + 512], in0=xr[:, c:c + 512],
                                    in1=pj[i * 2 + n2][:], op=ALU.add), "ra")

            lr1 = S.reserve("lr", 16 * NTT)

            @block.scalar
            def _(s_, ra0=ra0, t0=t0):
                S.wait(s_, "ra", ra0 + 16)
                for i in range(NTT):
                    S.inc(s_.dma_start(out=xov[:, t0 + i, :],
                                       in_=xr[:, i * D:(i + 1) * D]), "lr", 16)

            tpB0 = S.n.get("tp", 0)
            ln0 = emit_ln(
                block, S, xsrc_fn=lambda i: xr[:, i * D:(i + 1) * D],
                ntiles=NTT, stats=stats, lnscr=lnscr,
                apply_fn=lambda v, i, mean, rstd: v.tensor_scalar(
                    out=ynt[:], in0=xr[:, i * D:(i + 1) * D],
                    scalar1=mean, scalar2=rstd, op0=ALU.subtract, op1=ALU.mult),
                lr_wait=("lr", lr1 + 16 * NTT),
                tp_base_fn=lambda i, tpB0=tpB0: ("tp", tpB0 + i * NFT)
                if (i > 0 or tpB0 > 0) else (None, 0))

            with ExitStack() as ph:
                ptr = [ph.enter_context(nc.psum_tensor(f"ptA{half}_{j}",
                                                       [128, 128], BF16))
                       for j in range(2)]
                items = [(i, f) for i in range(NTT) for f in range(NFT)]
                emit_transposes(
                    block, S, items,
                    src_fn=lambda it: ynt[:, it[1] * 128:(it[1] + 1) * 128],
                    dst_fn=lambda it: yT[:, it[1] * HALF + it[0] * 128:
                                         it[1] * HALF + (it[0] + 1) * 128],
                    ident=ident[:], ptr=ptr,
                    src_tick=lambda j, ln0=ln0: ("ln", ln0 + j // NFT + 1))
            yT_done = S.n["tev"]

            with ExitStack() as ph:
                ptx = [ph.enter_context(nc.psum_tensor(f"ptx{half}_{j}",
                                                       [128, 128], F32))
                       for j in range(2)]
                items = [(i, ft) for i in range(NTT) for ft in range(NFT)]
                emit_transposes(
                    block, S, items,
                    src_fn=lambda it: xr[:, it[0] * D + it[1] * 128:
                                         it[0] * D + (it[1] + 1) * 128],
                    dst_fn=lambda it: xTf[:, it[1] * HALF + it[0] * 128:
                                          it[1] * HALF + (it[0] + 1) * 128],
                    ident=identf[:], ptr=ptx,
                    src_tick=lambda j, lr1=lr1: ("lr", lr1 + 16 * NTT))
            xTf_done = S.n["tev"]

            gl0 = S.reserve("gl", 2)
            with ExitStack() as ph:
                pl = ph.enter_context(nc.psum_tensor(f"pl{half}", [8, 512], F32))
                ptf = [ph.enter_context(nc.psum_tensor(f"ptf{half}_{j}",
                                                       [128, 128], F32))
                       for j in range(2)]

                @block.tensor
                def _(t, xTf_done=xTf_done, gl0=gl0, half=half):
                    S.wait(t, "tev", xTf_done)
                    if half == 0:
                        S.wait(t, "gg", gsr_done)
                    for k in range(NFT):
                        inst = t.matmul(pl[:], lhsT=gwf[:, k * 8:(k + 1) * 8],
                                        rhs=xTf[:, k * HALF:(k + 1) * HALF],
                                        start=(k == 0), stop=(k == NFT - 1))
                    S.inc(inst, "gl")

                @block.scalar
                def _(s_, gl0=gl0):
                    S.wait(s_, "gl", gl0 + 1)
                    S.inc(s_.copy(out=lgs[:], in_=pl[:]), "gl")

                tt0 = S.reserve("tt", 2 * NTT)

                @block.tensor
                def _(t, gl0=gl0, half=half):
                    S.wait(t, "id2", 1)
                    S.wait(t, "gl", gl0 + 2)
                    for c in range(NTT):
                        if c >= 2:
                            S.wait(t, "tt", tt0 + 2 * c - 1)
                        S.inc(t.transpose(ptf[c % 2][0:128, 0:8],
                                          in_=lgs[0:8, c * 128:(c + 1) * 128],
                                          identity=identf[0:8, 0:8]), "tt")

                @block.scalar
                def _(s_, tt0=tt0):
                    for c in range(NTT):
                        S.wait(s_, "tt", tt0 + 2 * c + 1)
                        S.inc(s_.copy(out=lgtm[:, c * 8:(c + 1) * 8],
                                      in_=ptf[c % 2][0:128, 0:8]), "tt")

            g2a = S.reserve("g2a", 1)
            g2b = S.reserve("g2b", 1)
            ttB = S.n["tt"]

            @block.vector
            def _(v, ttB=ttB):
                S.wait(v, "tt", ttB)
                for c in range(NTT):
                    inst = v.tensor_scalar_mul(out=gscr[:, c * 8:(c + 1) * 8],
                                               in0=gsrep[:],
                                               scalar1=stats[:, 8 + c:9 + c])
                vfence(v, S, inst)
                for c in range(NTT):
                    inst = v.tensor_tensor(out=lgtm[:, c * 8:(c + 1) * 8],
                                           in0=lgtm[:, c * 8:(c + 1) * 8],
                                           in1=gscr[:, c * 8:(c + 1) * 8],
                                           op=ALU.subtract)
                vfence(v, S, inst)
                for c in range(NTT):
                    inst = v.tensor_scalar_mul(out=lgtm[:, c * 8:(c + 1) * 8],
                                               in0=lgtm[:, c * 8:(c + 1) * 8],
                                               scalar1=stats[:, 16 + c:17 + c])
                vfence(v, S, inst)
                for c in range(NTT):
                    inst = v.reduce_max(out=gst[:, c:c + 1],
                                        in_=lgtm[:, c * 8:(c + 1) * 8], axis=AX.X)
                vfence(v, S, inst)
                for c in range(NTT):
                    inst = v.tensor_scalar(out=wmtm[:, c * 8:(c + 1) * 8],
                                           in0=lgtm[:, c * 8:(c + 1) * 8],
                                           scalar1=gst[:, c:c + 1], scalar2=None,
                                           op0=ALU.is_equal)
                vfence(v, S, inst)
                for c in range(NTT):
                    inst = v.scalar_tensor_tensor(
                        out=gscr[:, c * 8:(c + 1) * 8],
                        in0=wmtm[:, c * 8:(c + 1) * 8], scalar=-1e30,
                        in1=lgtm[:, c * 8:(c + 1) * 8],
                        op0=ALU.mult, op1=ALU.add)
                vfence(v, S, inst)
                for c in range(NTT):
                    inst = v.reduce_max(out=gst[:, 4 + c:5 + c],
                                        in_=gscr[:, c * 8:(c + 1) * 8], axis=AX.X)
                vfence(v, S, inst)
                inst = v.tensor_tensor(out=gst[:, 8:12], in0=gst[:, 0:4],
                                       in1=gst[:, 4:8], op=ALU.subtract)
                S.inc(inst, "g2a")

            @block.scalar
            def _(s_, g2a=g2a):
                S.wait(s_, "g2a", g2a + 1)
                S.inc(s_.activation(out=gst[:, 12:16], in_=gst[:, 8:12],
                                    func=AF.Sigmoid), "g2b")

            g2c = S.reserve("g2c", 1)

            @block.vector
            def _(v, g2b=g2b):
                S.wait(v, "g2b", g2b + 1)
                inst = v.tensor_scalar_mul(out=gst[:, 12:16], in0=gst[:, 12:16],
                                           scalar1=16.0)
                vfence(v, S, inst)
                inst = v.tensor_scalar(out=gst[:, 8:12], in0=gst[:, 12:16],
                                       scalar1=-1.0, scalar2=16.0,
                                       op0=ALU.mult, op1=ALU.add)
                vfence(v, S, inst)
                for c in range(NTT):
                    inst = v.tensor_scalar(out=lgtm[:, c * 8:(c + 1) * 8],
                                           in0=gscr[:, c * 8:(c + 1) * 8],
                                           scalar1=gst[:, 4 + c:5 + c], scalar2=None,
                                           op0=ALU.is_equal)
                vfence(v, S, inst)
                for c in range(NTT):
                    inst = v.tensor_scalar_mul(out=wmtm[:, c * 8:(c + 1) * 8],
                                               in0=wmtm[:, c * 8:(c + 1) * 8],
                                               scalar1=gst[:, 12 + c:13 + c])
                vfence(v, S, inst)
                for c in range(NTT):
                    inst = v.tensor_scalar_mul(out=lgtm[:, c * 8:(c + 1) * 8],
                                               in0=lgtm[:, c * 8:(c + 1) * 8],
                                               scalar1=gst[:, 8 + c:9 + c])
                vfence(v, S, inst)
                inst = v.tensor_tensor(out=wmtm[:, 0:8 * NTT], in0=wmtm[:, 0:8 * NTT],
                                       in1=lgtm[:, 0:8 * NTT], op=ALU.add)
                S.inc(inst, "g2c")

            hg0 = S.reserve("hg", 3)
            with ExitStack() as ph:
                ptf = [ph.enter_context(nc.psum_tensor(f"ptg{half}_{j}",
                                                       [128, 128], F32))
                       for j in range(2)]
                pG = ph.enter_context(nc.psum_tensor(f"pG{half}", [128, 512], F32))
                pH = ph.enter_context(nc.psum_tensor(f"pH{half}", [128, 512], F32))
                tt1 = S.reserve("tt", 2 * NTT)

                @block.tensor
                def _(t, g2c=g2c, tt1=tt1, hg0=hg0, half=half):
                    S.wait(t, "g2c", g2c + 1)
                    for c in range(NTT):
                        if c >= 2:
                            S.wait(t, "tt", tt1 + 2 * c - 1)
                        S.inc(t.transpose(ptf[c % 2][0:8, 0:128],
                                          in_=wmtm[:, c * 8:(c + 1) * 8],
                                          identity=identf[:]), "tt")
                    S.wait(t, "tt", tt1 + 2 * NTT)
                    S.wait(t, "mp", 1)
                    t.matmul(pG[:], lhsT=map8[:], rhs=wgfm[:],
                             start=True, stop=True)
                    for k in range(NFT):
                        inst = t.matmul(pH[:], lhsT=afb[:, k * 128:(k + 1) * 128],
                                        rhs=yT[:, k * HALF:(k + 1) * HALF],
                                        start=(k == 0), stop=(k == NFT - 1))
                    S.inc(inst, "hg")

                @block.scalar
                def _(s_, tt1=tt1, hg0=hg0):
                    for c in range(NTT):
                        S.wait(s_, "tt", tt1 + 2 * c + 1)
                        S.inc(s_.copy(out=wgfm[0:8, c * 128:(c + 1) * 128],
                                      in_=ptf[c % 2][0:8, 0:128]), "tt")
                    S.wait(s_, "hg", hg0 + 1)
                    S.inc(s_.copy(out=Gs[:], in_=pG[:]), "hg")

                @block.vector
                def _(v, hg0=hg0):
                    S.wait(v, "hg", hg0 + 2)
                    S.inc(v.tensor_tensor(out=hGs[:], in0=Gs[:], in1=pH[:],
                                          op=ALU.mult), "hg")

            with ExitStack() as ph:
                pj = [ph.enter_context(nc.psum_tensor(f"pm{half}_{j}",
                                                      [128, 512], F32))
                      for j in range(8)]
                witems = [(ng_, k) for ng_ in range(2) for k in range(NFT)]
                wd0, wu0 = emit_wstream(
                    block, S, witems, 2,
                    load_fn=lambda sync, it, slot: sync.dma_start(
                        out=wf[:, slot * 1024:(slot + 1) * 1024],
                        in_=WT[it[1] * 128:(it[1] + 1) * 128,
                               it[0] * 1024:(it[0] + 1) * 1024]),
                    cast_fn=lambda g, it, slot: g.tensor_copy(
                        out=wb[:, slot * 1024:(slot + 1) * 1024],
                        in_=wf[:, slot * 1024:(slot + 1) * 1024]))
                ra1 = S.reserve("ra", 2 * 8)
                hgf = S.n["hg"]

                @block.tensor
                def _(t, cw0=wd0, ra1=ra1, hgf=hgf):
                    S.wait(t, "hg", hgf)
                    for r in range(2):
                        if r > 0:
                            S.wait(t, "ra", ra1 + r * 8)
                        for k in range(NFT):
                            j = r * NFT + k
                            S.wait(t, "cw", cw0 + j + 1)
                            for i in range(NTT):
                                for n2 in range(2):
                                    inst = t.matmul(
                                        pj[i * 2 + n2][:],
                                        lhsT=yT[:, k * HALF + i * 128:
                                                k * HALF + (i + 1) * 128],
                                        rhs=wb[:, (j % 2) * 1024 + n2 * 512:
                                               (j % 2) * 1024 + (n2 + 1) * 512],
                                        start=(k == 0), stop=False)
                            S.inc(inst, "uw")
                        for i in range(NTT):
                            for n2 in range(2):
                                inst = t.matmul(
                                    pj[i * 2 + n2][:],
                                    lhsT=hGs[:, i * 128:(i + 1) * 128],
                                    rhs=breb[:, r * 1024 + n2 * 512:
                                             r * 1024 + (n2 + 1) * 512],
                                    start=False, stop=True)
                        S.inc(inst, "hg")

                @block.vector
                def _(v, ra1=ra1, hgf=hgf):
                    for r in range(2):
                        S.wait(v, "hg", hgf + r + 1)
                        for i in range(NTT):
                            for n2 in range(2):
                                c = i * D + r * 1024 + n2 * 512
                                S.inc(v.tensor_tensor(
                                    out=xr[:, c:c + 512], in0=xr[:, c:c + 512],
                                    in1=pj[i * 2 + n2][:], op=ALU.add), "ra")
                S.reserve("hg", 2)

                st0 = S.reserve("st", 16 * NTT)

                @block.scalar
                def _(s_, ra1=ra1, t0=t0):
                    S.wait(s_, "ra", ra1 + 16)
                    for i in range(NTT):
                        S.inc(s_.dma_start(out=xov[:, t0 + i, :],
                                           in_=xr[:, i * D:(i + 1) * D]), "st", 16)

        final_st = S.n["st"]

        @block.gpsimd
        def _(g):
            S.wait(g, "st", final_st)

    return nc


def build_L3f(tok=1024, dbg=False):
    nc = bass.Bass("TRN2", target_bir_lowering=False, debug=False)
    E, I = 8, 1024
    x_in = nc.dram_tensor("x", [tok, D], F32, kind="ExternalInput").ap()
    ctx3T = nc.dram_tensor("ctx3T", [D, tok], BF16, kind="ExternalInput").ap()
    w_out = nc.dram_tensor("w_out", [D, D], F32, kind="ExternalInput").ap()
    gwT = nc.dram_tensor("gwT", [D, E], F32, kind="ExternalInput").ap()
    guw = nc.dram_tensor("guw", [E, D, 2 * I], F32, kind="ExternalInput").ap()
    dnw = nc.dram_tensor("dnw", [E, I, D], F32, kind="ExternalInput").ap()
    x_out = nc.dram_tensor("x_out", [tok, D], F32, kind="ExternalOutput").ap()
    rdram = nc.dram_tensor("rdram", [2, E, tok // 2], F32, kind="Internal").ap()
    if dbg:
        d_wm = nc.dram_tensor("d_wm", [128, 32], F32, kind="ExternalOutput").ap()
        d_rrow = nc.dram_tensor("d_rrow", [1, 8 * 512], F32, kind="ExternalOutput").ap()
        d_lg = nc.dram_tensor("d_lg", [128, 32], F32, kind="ExternalOutput").ap()
        d_lgs = nc.dram_tensor("d_lgs", [8, 512], F32, kind="ExternalOutput").ap()
        d_gs = nc.dram_tensor("d_gs", [128, 8], F32, kind="ExternalOutput").ap()
        d_xtf = nc.dram_tensor("d_xtf", [128, 512], F32, kind="ExternalOutput").ap()
        d_lg0 = nc.dram_tensor("d_lg0", [128, 32], F32, kind="ExternalOutput").ap()

    HALF = tok // 2
    NTT = HALF // 128

    with ExitStack() as st:
        sb = lambda n_, sh, dt=F32: st.enter_context(nc.sbuf_tensor(n_, sh, dt))
        S = Sems(nc, st)

        ident = sb("ident", [128, 128], BF16)
        identf = sb("identf", [128, 128])
        xr = sb("xr", [128, NTT * D])
        stats = sb("stats", [128, 24])
        lnscr = sb("lnscr", [128, D])
        ynt = sb("ynt", [128, D], BF16)
        yT = sb("yT", [128, NFT * HALF], BF16)
        cT = sb("cT", [128, NFT * HALF], BF16)
        wf = sb("wf", [128, 2 * D])
        wb = sb("wb", [128, 4 * D], BF16)
        gwf = sb("gwf", [128, NFT * E])
        onerow = st.enter_context(nc.sbuf_tensor("onerow", [1, 128], F32))
        onescol = sb("onescol", [128, 1])
        gsrep = sb("gsrep", [128, E])
        xTf = sb("xTf", [128, NFT * HALF])
        lgs = sb("lgs", [E, HALF])
        lgtm = sb("lgtm", [128, NTT * E])
        gscr = sb("gscr", [128, NTT * E])
        gst = sb("gst", [128, 16])
        wmtm = sb("wmtm", [128, NTT * E])
        rrow = st.enter_context(nc.sbuf_tensor("rrow", [1, E * HALF], F32))
        Rr = sb("Rr", [128, HALF])
        wfms = sb("wfms", [8, HALF])
        ysc = sb("ysc", [128, NFT * HALF], BF16)
        act = sb("act", [128, 8 * HALF], BF16)
        gsil = sb("gsil", [128, HALF])

        block = st.enter_context(nc.Block())
        emit_ident(block, S, ident[:])

        id2_0 = S.reserve("id2", 1)

        @block.gpsimd
        def _(g):
            inst = g.memset(identf[:], 0.0)
            vfence(g, S, inst)
            S.inc(g.affine_select(out=identf[:], in_=identf[:],
                                  compare_op=ALU.not_equal, fill=1.0, base=0,
                                  pattern=[[-1, 128]], channel_multiplier=1), "id2")

        xv = x_in.rearrange("(i p) d -> p i d", p=128)
        xov = x_out.rearrange("(i p) d -> p i d", p=128)

        sr0 = S.reserve("sr", 16)
        on0 = S.reserve("on", 1)

        @block.sync
        def _(sync):
            S.inc(sync.dma_start(out=gwf[:],
                                 in_=gwT.rearrange("(k p) r -> p k r", p=128)),
                  "sr", 16)

        @block.vector
        def _(v):
            v.memset(onerow[:], 1.0)
            S.inc(v.memset(onescol[:], 1.0), "on")

        with ExitStack() as ph:
            pgs = ph.enter_context(nc.psum_tensor("pgs", [1, E], F32))
            pgr = ph.enter_context(nc.psum_tensor("pgr", [128, E], F32))
            gg0 = S.reserve("gg", 4)

            @block.tensor
            def _(t):
                S.wait(t, "sr", sr0 + 16)
                S.wait(t, "on", on0 + 1)
                for k in range(NFT):
                    inst = t.matmul(pgs[:], lhsT=onescol[:],
                                    rhs=gwf[:, k * E:(k + 1) * E],
                                    start=(k == 0), stop=(k == NFT - 1))
                S.inc(inst, "gg")
                S.wait(t, "gg", 2)
                S.inc(t.matmul(pgr[:], lhsT=onerow[:], rhs=lgs[0:1, 0:E],
                               start=True, stop=True), "gg")

            @block.scalar
            def _(s_):
                S.wait(s_, "gg", 1)
                S.inc(s_.copy(out=lgs[0:1, 0:E], in_=pgs[:]), "gg")
                S.wait(s_, "gg", 3)
                S.inc(s_.copy(out=gsrep[:], in_=pgr[:]), "gg")
        gsr_done = S.n["gg"]

        for half in range(2):
            t0 = half * NTT
            lr0 = S.reserve("lr", 16 * NTT)
            lc0 = S.reserve("lc", 16 * NFT)
            prev_st = S.n.get("st", 0)
            prev_uw = S.n.get("uw", 0)

            @block.sync
            def _(sync, t0=t0, prev_st=prev_st, prev_uw=prev_uw, half=half):
                for i in range(NTT):
                    if half > 0:
                        S.wait(sync, "st", prev_st)
                    S.inc(sync.dma_start(out=xr[:, i * D:(i + 1) * D],
                                         in_=xv[:, t0 + i, :]), "lr", 16)
                for k in range(NFT):
                    if half > 0:
                        S.wait(sync, "uw", prev_uw)
                    S.inc(sync.dma_start(
                        out=cT[:, k * HALF:(k + 1) * HALF],
                        in_=ctx3T[k * 128:(k + 1) * 128,
                                  half * HALF:(half + 1) * HALF]), "lc", 16)

            with ExitStack() as ph:
                pj = [ph.enter_context(nc.psum_tensor(f"pdl{half}_{j}",
                                                      [128, 512], F32))
                      for j in range(8)]
                witems = [(ng_, k) for ng_ in range(2) for k in range(NFT)]
                wd0, wu0 = emit_wstream(
                    block, S, witems, 2,
                    load_fn=lambda sync, it, slot: sync.dma_start(
                        out=wf[:, slot * 1024:(slot + 1) * 1024],
                        in_=w_out[it[1] * 128:(it[1] + 1) * 128,
                                  it[0] * 1024:(it[0] + 1) * 1024]),
                    cast_fn=lambda g, it, slot: g.tensor_copy(
                        out=wb[:, slot * 1024:(slot + 1) * 1024],
                        in_=wf[:, slot * 1024:(slot + 1) * 1024]))
                ra0 = S.reserve("ra", 2 * 8)

                @block.tensor
                def _(t, cw0=wd0, ra0=ra0, lc0=lc0, half=half):
                    if half == 0:
                        S.wait(t, "gg", gsr_done)
                    S.wait(t, "lc", lc0 + 16 * NFT)
                    for r in range(2):
                        if r > 0:
                            S.wait(t, "ra", ra0 + r * 8)
                        for k in range(NFT):
                            j = r * NFT + k
                            S.wait(t, "cw", cw0 + j + 1)
                            for i in range(NTT):
                                for n2 in range(2):
                                    inst = t.matmul(
                                        pj[i * 2 + n2][:],
                                        lhsT=cT[:, k * HALF + i * 128:
                                                k * HALF + (i + 1) * 128],
                                        rhs=wb[:, (j % 2) * 1024 + n2 * 512:
                                               (j % 2) * 1024 + (n2 + 1) * 512],
                                        start=(k == 0), stop=(k == NFT - 1))
                            S.inc(inst, "uw")

                @block.vector
                def _(v, ra0=ra0, lr0=lr0, wu0=wu0):
                    S.wait(v, "lr", lr0 + 16 * NTT)
                    for r in range(2):
                        S.wait(v, "uw", wu0 + (r + 1) * NFT)
                        for i in range(NTT):
                            for n2 in range(2):
                                c = i * D + r * 1024 + n2 * 512
                                S.inc(v.tensor_tensor(
                                    out=xr[:, c:c + 512], in0=xr[:, c:c + 512],
                                    in1=pj[i * 2 + n2][:], op=ALU.add), "ra")

            lr1 = S.reserve("lr", 16 * NTT)

            @block.scalar
            def _(s_, ra0=ra0, t0=t0):
                S.wait(s_, "ra", ra0 + 16)
                for i in range(NTT):
                    S.inc(s_.dma_start(out=xov[:, t0 + i, :],
                                       in_=xr[:, i * D:(i + 1) * D]), "lr", 16)

            tpB0 = S.n.get("tp", 0)
            ln0 = emit_ln(
                block, S, xsrc_fn=lambda i: xr[:, i * D:(i + 1) * D],
                ntiles=NTT, stats=stats, lnscr=lnscr,
                apply_fn=lambda v, i, mean, rstd: v.tensor_scalar(
                    out=ynt[:], in0=xr[:, i * D:(i + 1) * D],
                    scalar1=mean, scalar2=rstd, op0=ALU.subtract, op1=ALU.mult),
                lr_wait=("lr", lr1 + 16 * NTT),
                tp_base_fn=lambda i, tpB0=tpB0: ("tp", tpB0 + i * NFT)
                if (i > 0 or tpB0 > 0) else (None, 0))

            with ExitStack() as ph:
                ptr = [ph.enter_context(nc.psum_tensor(f"ptA{half}_{j}",
                                                       [128, 128], BF16))
                       for j in range(2)]
                items = [(i, ft) for i in range(NTT) for ft in range(NFT)]
                emit_transposes(
                    block, S, items,
                    src_fn=lambda it: ynt[:, it[1] * 128:(it[1] + 1) * 128],
                    dst_fn=lambda it: yT[:, it[1] * HALF + it[0] * 128:
                                         it[1] * HALF + (it[0] + 1) * 128],
                    ident=ident[:], ptr=ptr,
                    src_tick=lambda j, ln0=ln0: ("ln", ln0 + j // NFT + 1))
            yT_done = S.n["tev"]

            with ExitStack() as ph:
                ptx = [ph.enter_context(nc.psum_tensor(f"ptx{half}_{j}",
                                                       [128, 128], F32))
                       for j in range(2)]
                items = [(i, ft) for i in range(NTT) for ft in range(NFT)]
                emit_transposes(
                    block, S, items,
                    src_fn=lambda it: xr[:, it[0] * D + it[1] * 128:
                                         it[0] * D + (it[1] + 1) * 128],
                    dst_fn=lambda it: xTf[:, it[1] * HALF + it[0] * 128:
                                          it[1] * HALF + (it[0] + 1) * 128],
                    ident=identf[:], ptr=ptx,
                    src_tick=lambda j, lr1=lr1: ("lr", lr1 + 16 * NTT))
            xTf_done = S.n["tev"]

            gl0 = S.reserve("gl", 2)
            with ExitStack() as ph:
                pl = ph.enter_context(nc.psum_tensor(f"pl{half}", [E, 512], F32))
                ptf = [ph.enter_context(nc.psum_tensor(f"ptf{half}_{j}",
                                                       [128, 128], F32))
                       for j in range(2)]

                @block.tensor
                def _(t, xTf_done=xTf_done, gl0=gl0, half=half):
                    S.wait(t, "tev", xTf_done)
                    if half == 0:
                        S.wait(t, "gg", gsr_done)
                    for k in range(NFT):
                        inst = t.matmul(pl[:], lhsT=gwf[:, k * E:(k + 1) * E],
                                        rhs=xTf[:, k * HALF:(k + 1) * HALF],
                                        start=(k == 0), stop=(k == NFT - 1))
                    S.inc(inst, "gl")

                @block.scalar
                def _(s_, gl0=gl0, half=half):
                    S.wait(s_, "gl", gl0 + 1)
                    S.inc(s_.copy(out=lgs[:], in_=pl[:]), "gl")
                    if dbg and half == 0:
                        dq0 = S.reserve("dbq", 16 * 3)
                        S.wait(s_, "gl", gl0 + 2)
                        S.inc(s_.dma_start(out=d_lgs, in_=lgs[:]), "dbq", 16)
                        S.inc(s_.dma_start(out=d_gs, in_=gsrep[:]), "dbq", 16)
                        S.inc(s_.dma_start(out=d_xtf, in_=xTf[:, 0:512]), "dbq", 16)

                tt0 = S.reserve("tt", 2 * NTT)

                @block.tensor
                def _(t, gl0=gl0, tt0=tt0, half=half):
                    S.wait(t, "id2", 1)
                    S.wait(t, "gl", gl0 + 2)
                    for c in range(NTT):
                        if c >= 2:
                            S.wait(t, "tt", tt0 + 2 * c - 1)
                        S.inc(t.transpose(ptf[c % 2][0:128, 0:E],
                                          in_=lgs[0:E, c * 128:(c + 1) * 128],
                                          identity=identf[0:E, 0:E]), "tt")

                @block.scalar
                def _(s_, tt0=tt0, half=half):
                    for c in range(NTT):
                        S.wait(s_, "tt", tt0 + 2 * c + 1)
                        S.inc(s_.copy(out=lgtm[:, c * E:(c + 1) * E],
                                      in_=ptf[c % 2][0:128, 0:E]), "tt")
                    if dbg and half == 0:
                        dq1 = S.reserve("dbq", 16)
                        S.wait(s_, "tt", tt0 + 2 * NTT)
                        S.inc(s_.dma_start(out=d_lg0, in_=lgtm[:]), "dbq", 16)

            g2a = S.reserve("g2a", 1)
            g2b = S.reserve("g2b", 1)
            g2c = S.reserve("g2c", 1)
            ttB = S.n["tt"]

            @block.vector
            def _(v, ttB=ttB):
                S.wait(v, "tt", ttB)
                for c in range(NTT):
                    inst = v.tensor_scalar_mul(out=gscr[:, c * E:(c + 1) * E],
                                               in0=gsrep[:],
                                               scalar1=stats[:, 8 + c:9 + c])
                vfence(v, S, inst)
                for c in range(NTT):
                    inst = v.tensor_tensor(out=lgtm[:, c * E:(c + 1) * E],
                                           in0=lgtm[:, c * E:(c + 1) * E],
                                           in1=gscr[:, c * E:(c + 1) * E],
                                           op=ALU.subtract)
                vfence(v, S, inst)
                for c in range(NTT):
                    inst = v.tensor_scalar_mul(out=lgtm[:, c * E:(c + 1) * E],
                                               in0=lgtm[:, c * E:(c + 1) * E],
                                               scalar1=stats[:, 16 + c:17 + c])
                vfence(v, S, inst)
                for c in range(NTT):
                    inst = v.reduce_max(out=gst[:, c:c + 1],
                                        in_=lgtm[:, c * E:(c + 1) * E], axis=AX.X)
                vfence(v, S, inst)
                for c in range(NTT):
                    inst = v.tensor_scalar(out=wmtm[:, c * E:(c + 1) * E],
                                           in0=lgtm[:, c * E:(c + 1) * E],
                                           scalar1=gst[:, c:c + 1], scalar2=None,
                                           op0=ALU.is_equal)
                S.inc(inst, "g2a")

            @block.scalar
            def _(s_, g2a=g2a):
                S.wait(s_, "g2a", g2a + 1)
                S.inc(s_.activation(out=gst[:, 4:8], in_=gst[:, 0:4],
                                    func=AF.Sigmoid), "g2b")

            @block.vector
            def _(v, g2b=g2b):
                S.wait(v, "g2b", g2b + 1)
                for c in range(NTT):
                    inst = v.tensor_scalar_mul(out=wmtm[:, c * E:(c + 1) * E],
                                               in0=wmtm[:, c * E:(c + 1) * E],
                                               scalar1=gst[:, 4 + c:5 + c])
                S.inc(inst, "g2c")

            wfm0 = S.reserve("wfm", 2 * NTT)
            rt0 = S.reserve("rt", 16 * 2)
            with ExitStack() as ph:
                ptf2 = [ph.enter_context(nc.psum_tensor(f"ptw{half}_{j}",
                                                        [128, 128], F32))
                        for j in range(2)]

                @block.tensor
                def _(t, g2c=g2c, wfm0=wfm0):
                    S.wait(t, "g2c", g2c + 1)
                    for c in range(NTT):
                        if c >= 2:
                            S.wait(t, "wfm", wfm0 + 2 * c - 1)
                        S.inc(t.transpose(ptf2[c % 2][0:E, 0:128],
                                          in_=wmtm[:, c * E:(c + 1) * E],
                                          identity=identf[:]), "wfm")

                @block.scalar
                def _(s_, wfm0=wfm0, rt0=rt0):
                    for c in range(NTT):
                        S.wait(s_, "wfm", wfm0 + 2 * c + 1)
                        S.inc(s_.copy(out=wfms[0:E, c * 128:(c + 1) * 128],
                                      in_=ptf2[c % 2][0:E, 0:128]), "wfm")
                    S.wait(s_, "wfm", wfm0 + 2 * NTT)
                    S.inc(s_.dma_start(out=rdram[half], in_=wfms[:]), "rt", 16)
                    S.wait(s_, "rt", rt0 + 16)
                    S.inc(s_.dma_start(
                        out=rrow[:],
                        in_=rdram[half].rearrange("e h -> (e h)").unsqueeze(0)),
                        "rt", 16)
                    if dbg and half == 0:
                        db0 = S.reserve("dbg", 16 * 3)
                        S.wait(s_, "rt", rt0 + 32)
                        S.inc(s_.dma_start(out=d_wm, in_=wmtm[:]), "dbg", 16)
                        S.inc(s_.dma_start(out=d_rrow, in_=rrow[:]), "dbg", 16)
                        S.inc(s_.dma_start(out=d_lg, in_=lgtm[:]), "dbg", 16)

            with ExitStack() as ph:
                pj = [ph.enter_context(nc.psum_tensor(f"pe{half}_{j}",
                                                      [128, 512], F32))
                      for j in range(8)]
                mo_order = [0, 8, 1, 9, 2, 10, 3, 11, 4, 12, 5, 13, 6, 14, 7, 15]
                witems = []
                for e in range(E):
                    for mo in mo_order:
                        witems.append(("gu", e, mo, 0))
                    for rg in range(2):
                        for k in range(8):
                            witems.append(("dn", e, rg, k))

                def w_load(sync, it, slot):
                    if it[0] == "gu":
                        return sync.dma_start(
                            out=wf[:, slot * D:(slot + 1) * D],
                            in_=guw[it[1]].rearrange("(k p) (m j) -> p k m j",
                                                     p=128, j=128)[:, :, it[2], :])
                    return sync.dma_start(
                        out=wf[:, slot * D:slot * D + 1024],
                        in_=dnw[it[1]][it[3] * 128:(it[3] + 1) * 128,
                                       it[2] * 1024:(it[2] + 1) * 1024])

                def w_cast(g, it, slot):
                    if it[0] == "gu":
                        return g.tensor_copy(out=wb[:, slot * D:(slot + 1) * D],
                                             in_=wf[:, slot * D:(slot + 1) * D])
                    return g.tensor_copy(out=wb[:, (2 + slot) * D:(2 + slot) * D + 1024],
                                         in_=wf[:, slot * D:slot * D + 1024])

                w_cw0, w_uw0 = emit_wstream(block, S, witems, 2, w_load, w_cast)

                rr0 = S.reserve("rr", 2 * E)
                ys0 = S.reserve("ys", E)
                fe0 = S.reserve("fe", E * 8)
                sg0 = S.reserve("sg", E * 8)
                ea0 = S.reserve("ea", E * 16)

                @block.tensor
                def _(t, rt0=rt0, rr0=rr0, ys0=ys0, fe0=fe0, ea0=ea0,
                      w_cw0=w_cw0, half=half):
                    S.wait(t, "rt", rt0 + 32)
                    for e in range(E):
                        if e > 0:
                            S.wait(t, "ea", ea0 + e * 16)
                        S.inc(t.matmul(pj[0][:], lhsT=onerow[:],
                                       rhs=rrow[:, e * HALF:(e + 1) * HALF],
                                       start=True, stop=True), "rr")
                        S.wait(t, "ys", ys0 + e + 1)
                        for m in range(8):
                            if m >= 4:
                                S.wait(t, "fe", fe0 + e * 8 + m - 3)
                            for w01 in range(2):
                                j = e * 32 + 2 * m + w01
                                S.wait(t, "cw", w_cw0 + j + 1)
                                for k in range(NFT):
                                    inst = t.matmul(
                                        pj[2 * (m % 4) + w01][:],
                                        lhsT=wb[:, (j % 2) * D + k * 128:
                                                (j % 2) * D + (k + 1) * 128],
                                        rhs=ysc[:, k * HALF:(k + 1) * HALF],
                                        start=(k == 0), stop=(k == NFT - 1))
                                S.inc(inst, "uw")
                        for rg in range(2):
                            S.wait(t, "fe", fe0 + e * 8 + 8)
                            if rg > 0:
                                S.wait(t, "ea", ea0 + e * 16 + 8)
                            for k in range(8):
                                j = e * 32 + 16 + rg * 8 + k
                                S.wait(t, "cw", w_cw0 + j + 1)
                                for i in range(NTT):
                                    for n2 in range(2):
                                        inst = t.matmul(
                                            pj[i * 2 + n2][:],
                                            lhsT=act[:, k * HALF + i * 128:
                                                     k * HALF + (i + 1) * 128],
                                            rhs=wb[:, (2 + (j % 2)) * D + n2 * 512:
                                                   (2 + (j % 2)) * D + (n2 + 1) * 512],
                                            start=(k == 0), stop=(k == 7))
                                S.inc(inst, "uw")

                @block.scalar
                def _(s_, rr0=rr0, sg0=sg0, fe0=fe0, w_uw0=w_uw0, half=half):
                    for e in range(E):
                        S.wait(s_, "rr", rr0 + 2 * e + 1)
                        S.inc(s_.copy(out=Rr[:], in_=pj[0][:]), "rr")
                        for m in range(8):
                            S.wait(s_, "uw", w_uw0 + e * 32 + 2 * m + 2)
                            if e * 8 + m > 0:
                                S.wait(s_, "fe", fe0 + e * 8 + m)
                            S.inc(s_.activation(out=gsil[:],
                                                in_=pj[2 * (m % 4) + 1][:],
                                                func=AF.Silu), "sg")

                @block.vector
                def _(v, rr0=rr0, ys0=ys0, sg0=sg0, fe0=fe0, ea0=ea0,
                      w_uw0=w_uw0, half=half):
                    for e in range(E):
                        S.wait(v, "rr", rr0 + 2 * e + 2)
                        for k in range(NFT):
                            inst = v.tensor_tensor(
                                out=ysc[:, k * HALF:(k + 1) * HALF],
                                in0=yT[:, k * HALF:(k + 1) * HALF],
                                in1=Rr[:], op=ALU.mult)
                        S.inc(inst, "ys")
                        for m in range(8):
                            S.wait(v, "sg", sg0 + e * 8 + m + 1)
                            S.inc(v.tensor_tensor(
                                out=act[:, m * HALF:(m + 1) * HALF],
                                in0=gsil[:], in1=pj[2 * (m % 4)][:],
                                op=ALU.mult), "fe")
                        for rg in range(2):
                            S.wait(v, "uw", w_uw0 + e * 32 + 16 + (rg + 1) * 8)
                            for i in range(NTT):
                                for n2 in range(2):
                                    c = i * D + n2 * 512 + (0 if rg == 0 else 1024)
                                    S.inc(v.tensor_tensor(
                                        out=xr[:, c:c + 512], in0=xr[:, c:c + 512],
                                        in1=pj[i * 2 + n2][:], op=ALU.add), "ea")

                st0 = S.reserve("st", 16 * NTT)

                @block.scalar
                def _(s_, ea0=ea0, t0=t0):
                    S.wait(s_, "ea", ea0 + 16 * E)
                    for i in range(NTT):
                        S.inc(s_.dma_start(out=xov[:, t0 + i, :],
                                           in_=xr[:, i * D:(i + 1) * D]), "st", 16)

        final_st = S.n["st"]

        @block.gpsimd
        def _(g):
            S.wait(g, "st", final_st)

    return nc


def emit_rope_tables(block, S, g_iota, cost, sint, tag="rp",
                     npos=4096, nf=32, base_div=32.0, chunk=1024,
                     flip_first=False):
    import math
    MAGIC = 12582912.0
    C1 = 6.28125
    C2 = 2.0 * math.pi - C1
    nck = npos // chunk
    r0 = S.reserve(tag, 1)

    half = nf // 2

    @block.gpsimd
    def _(g):
        inst = g.iota(sint[0:half, 3 * chunk:3 * chunk + 1], pattern=[[1, 1]],
                      base=0, channel_multiplier=1,
                      allow_small_or_imprecise_dtypes=True)
        vfence(g, S, inst)
        inst = g.iota(sint[half:nf, 3 * chunk:3 * chunk + 1], pattern=[[1, 1]],
                      base=0, channel_multiplier=1,
                      allow_small_or_imprecise_dtypes=True)
        S.inc(inst, tag)

    r1 = S.reserve(tag, 1)

    @block.scalar
    def _(s_):
        S.wait(s_, tag, r0 + 1)
        S.inc(s_.activation(out=sint[:, 3 * chunk:3 * chunk + 1],
                            in_=sint[:, 3 * chunk:3 * chunk + 1],
                            func=AF.Exp, scale=-math.log(10000.0) / base_div),
              tag)

    for ck in range(nck):
        b0 = S.reserve(tag, 1)

        @block.gpsimd
        def _(g, ck=ck, b0=b0):
            S.wait(g, tag, b0)
            S.inc(g.iota(sint[:, 0:chunk], pattern=[[1, chunk]], base=ck * chunk,
                         channel_multiplier=0,
                         allow_small_or_imprecise_dtypes=True), tag)

        bm = S.reserve(tag, 1)

        @block.vector
        def _(v, ck=ck, b0=b0):
            S.wait(v, tag, b0 + 1)
            S.inc(v.tensor_scalar_mul(out=sint[:, 0:chunk], in0=sint[:, 0:chunk],
                                      scalar1=sint[:, 3 * chunk:3 * chunk + 1]),
                  tag)

        for phase, outoff in ((0.0, npos), (math.pi / 2.0, 0)):
            bb = S.reserve(tag, 2)
            fa = sint[:, chunk:2 * chunk]
            kk = sint[:, 2 * chunk:3 * chunk]

            @block.vector
            def _(v, ck=ck, phase=phase, bb=bb, fa=fa, kk=kk):
                S.wait(v, tag, bb)
                inst = v.tensor_scalar(out=fa, in0=sint[:, 0:chunk],
                                       scalar1=phase, scalar2=0.0,
                                       op0=ALU.add, op1=ALU.add)
                vfence(v, S, inst)
                inst = v.tensor_scalar(out=kk, in0=fa,
                                       scalar1=1.0 / (2.0 * math.pi),
                                       scalar2=MAGIC, op0=ALU.mult, op1=ALU.add)
                vfence(v, S, inst)
                inst = v.tensor_scalar_add(out=kk, in0=kk, scalar1=-MAGIC)
                vfence(v, S, inst)
                inst = v.scalar_tensor_tensor(out=fa, in0=kk, scalar=-C1, in1=fa,
                                              op0=ALU.mult, op1=ALU.add)
                vfence(v, S, inst)
                inst = v.scalar_tensor_tensor(out=fa, in0=kk, scalar=-C2, in1=fa,
                                              op0=ALU.mult, op1=ALU.add)
                vfence(v, S, inst)
                PI_IN = 3.1415925
                inst = v.tensor_scalar(out=fa, in0=fa, scalar1=-PI_IN,
                                       scalar2=PI_IN, op0=ALU.max, op1=ALU.min)
                S.inc(inst, tag)

            @block.scalar
            def _(s_, ck=ck, outoff=outoff, bb=bb, fa=fa):
                S.wait(s_, tag, bb + 1)
                S.inc(s_.activation(out=cost[:, outoff + ck * chunk:
                                             outoff + (ck + 1) * chunk],
                                    in_=fa, func=AF.Sin), tag)

    bsf = S.reserve(tag, 1)

    @block.vector
    def _(v):
        rows = slice(0, half) if flip_first else slice(half, nf)
        S.wait(v, tag, bsf)
        S.inc(v.tensor_scalar_mul(out=cost[rows, npos:2 * npos],
                                  in0=cost[rows, npos:2 * npos],
                                  scalar1=-1.0), tag)

    return S.n[tag]


def build_L2a(seq=4096, dbg=False, attn=True):
    import math
    nc = bass.Bass("TRN2", target_bir_lowering=False, debug=False)
    x_in = nc.dram_tensor("x", [seq, D], F32, kind="ExternalInput").ap()
    WqkvT = nc.dram_tensor("WqkvT", [D, 1024], F32, kind="ExternalInput").ap()
    sqkv = nc.dram_tensor("sqkv", [3], F32, kind="ExternalInput").ap()
    AqkvT = nc.dram_tensor("AqkvT", [D, 96], F32, kind="ExternalInput").ap()
    Bcat = nc.dram_tensor("Bcat", [96, 1024], F32, kind="ExternalInput").ap()
    qgain = nc.dram_tensor("qgain", [4], F32, kind="ExternalInput").ap()
    ctx_o = nc.dram_tensor("ctx", [seq, 512], BF16, kind="ExternalOutput").ap()

    SLAB = 256
    NSL = seq // SLAB
    NTS = seq // 128
    EPSR = 1.1920929e-07

    with ExitStack() as st:
        sb = lambda n_, sh, dt=F32: st.enter_context(nc.sbuf_tensor(n_, sh, dt))
        S = Sems(nc, st)

        ident = sb("ident", [128, 128], BF16)
        xr = sb("xr", [128, 2 * D])
        stats = sb("stats", [128, 24])
        ynt = sb("ynt", [128, D], BF16)
        yT = sb("yT", [128, NFT * SLAB], BF16)
        wf = sb("wf", [128, 2 * 1024])
        wqkv_b = sb("wqkv_b", [128, 16 * 1024], BF16)
        ab = sb("ab", [128, 16 * 96], BF16)
        bcat_b = sb("bcat_b", [96, 1024], BF16)
        socol3 = sb("socol3", [128, 3])
        ghcol = sb("ghcol", [128, 4])
        onerow = st.enter_context(nc.sbuf_tensor("onerow", [1, 128], F32))
        onescol = sb("onescol", [128, 1])
        us = sb("us", [96, SLAB], BF16)
        sqscr = sb("sqscr", [128, SLAB])
        prepS = sb("prepS", [128, SLAB])
        rr_ = st.enter_context(nc.sbuf_tensor("rr_", [1, SLAB], F32))
        q_s = sb("q_s", [128, 4 * seq], BF16)
        k_s = sb("k_s", [128, 2 * seq], BF16)
        v_tm = sb("v_tm", [128, 2 * seq], BF16)
        vtmp = sb("vtmp", [128, 2 * SLAB], BF16)
        cost = st.enter_context(nc.sbuf_tensor("cost", [64, 2 * seq], BF16))
        ropscr = st.enter_context(nc.sbuf_tensor("ropscr", [64, 3 * 512 + 1], F32))
        pswap = st.enter_context(nc.sbuf_tensor("pswap", [64, 64], BF16))
        usw = st.enter_context(nc.sbuf_tensor("usw", [64, SLAB], BF16))
        rop_a = st.enter_context(nc.sbuf_tensor("rop_a", [64, SLAB], BF16))
        rop_b = st.enter_context(nc.sbuf_tensor("rop_b", [64, SLAB], BF16))
        scores = sb("scores", [128, seq])
        pb = sb("pb", [128, seq], BF16)
        ptb = sb("ptb", [128, 2 * 128], BF16)
        m128 = sb("m128", [128, 128])
        ctxev = sb("ctxev", [128, 128], BF16)
        smc = sb("smc", [128, 4])
        dmy = sb("dmy", [128, 1])
        lnscr_a = scores[:, 0:D] if seq >= D else sb("lnscr", [128, D])[:, 0:D]

        block = st.enter_context(nc.Block())
        emit_ident(block, S, ident[:])
        emit_rope_tables(block, S, None, cost, ropscr, tag="rp", chunk=512,
                         npos=seq, nf=64, base_div=32.0, flip_first=False)

        msk0 = S.reserve("msk", 2)

        @block.gpsimd
        def _(g):
            inst = g.memset(m128[:], 0.0)
            vfence(g, S, inst)
            S.inc(g.affine_select(out=m128[:], in_=m128[:], compare_op=ALU.is_ge,
                                  fill=-3e38, base=0, pattern=[[-1, 128]],
                                  channel_multiplier=1), "msk")
            inst = g.memset(pswap[:], 0.0)
            vfence(g, S, inst)
            inst = g.affine_select(out=pswap[:], in_=pswap[:],
                                   compare_op=ALU.not_equal, fill=1.0,
                                   base=-32, pattern=[[1, 64]],
                                   channel_multiplier=-1)
            vfence(g, S, inst)
            S.inc(g.affine_select(out=pswap[:], in_=pswap[:],
                                  compare_op=ALU.not_equal, fill=1.0,
                                  base=32, pattern=[[1, 64]],
                                  channel_multiplier=-1), "msk")

        sr0 = S.reserve("sr", 16 * 9)
        cv0 = S.reserve("cv", 3)

        @block.sync
        def _(sync):
            for j in range(3):
                S.inc(sync.dma_start(
                    out=socol3[:, j:j + 1],
                    in_=sqkv[j:j + 1].unsqueeze(0).broadcast_to([128, 1])),
                    "sr", 16)
            for j in range(4):
                S.inc(sync.dma_start(
                    out=ghcol[:, j:j + 1],
                    in_=qgain[j:j + 1].unsqueeze(0).broadcast_to([128, 1])),
                    "sr", 16)
            S.inc(sync.dma_start(
                out=wf[:, 0:1536],
                in_=AqkvT.rearrange("(k p) r -> p k r", p=128)), "sr", 16)
            S.wait(sync, "cv", cv0 + 1)
            S.inc(sync.dma_start(out=wf[0:96, 1536:2048].bitcast(F32)
                                 if False else wf[0:96, 0:1024], in_=Bcat),
                  "sr", 16)

        @block.vector
        def _(v):
            v.memset(onerow[:], 1.0)
            v.memset(onescol[:], 1.0)
            v.memset(dmy[:], 0.0)
            S.wait(v, "sr", sr0 + 16 * 8)
            S.inc(v.tensor_copy(out=ab[:], in_=wf[:, 0:1536]), "cv")
            S.wait(v, "sr", sr0 + 16 * 9)
            S.inc(v.tensor_copy(out=bcat_b[:], in_=wf[0:96, 0:1024]), "cv")
            S.inc(v.tensor_scalar_mul(out=ghcol[:], in0=ghcol[:],
                                      scalar1=float(1.0 / math.sqrt(128.0))),
                  "cv")

        rl = [S.reserve(f"rl{k}", 16 * 8) for k in range(2)]
        c2w0 = S.reserve("c2w", 16)

        @block.sync
        def _(sync):
            for k in range(16):
                if k >= 2:
                    S.wait(sync, "c2w", c2w0 + k - 1)
                else:
                    S.wait(sync, "cv", cv0 + 2)
                S.inc(sync.dma_start(
                    out=wf[:, (k % 2) * 1024:(k % 2 + 1) * 1024],
                    in_=WqkvT[k * 128:(k + 1) * 128, :]), f"rl{k % 2}", 16)

        @block.gpsimd
        def _(g):
            S.wait(g, "sr", sr0 + 16 * 9)
            for k in range(16):
                S.wait(g, f"rl{k % 2}", rl[k % 2] + 16 * (k // 2 + 1))
                sl_ = (k % 2) * 1024
                g.tensor_scalar_mul(out=wqkv_b[:, k * 1024:k * 1024 + 512],
                                    in0=wf[:, sl_:sl_ + 512],
                                    scalar1=socol3[:, 0:1])
                g.tensor_scalar_mul(out=wqkv_b[:, k * 1024 + 512:k * 1024 + 768],
                                    in0=wf[:, sl_ + 512:sl_ + 768],
                                    scalar1=socol3[:, 1:2])
                S.inc(g.tensor_scalar_mul(
                    out=wqkv_b[:, k * 1024 + 768:(k + 1) * 1024],
                    in0=wf[:, sl_ + 768:sl_ + 1024],
                    scalar1=socol3[:, 2:3]), "c2w")

        for sl in range(NSL):
            t0 = sl * 2
            lr0 = S.reserve("lr", 16 * 2)
            lnprev = S.n.get("ln", 0)

            @block.sync
            def _(sync, sl=sl, t0=t0, lnprev=lnprev):
                for i in range(2):
                    if sl > 0:
                        S.wait(sync, "ln", lnprev)
                    S.inc(sync.dma_start(
                        out=xr[:, i * D:(i + 1) * D],
                        in_=x_in.rearrange("(a p) d -> p a d", p=128)[:, t0 + i, :]),
                        "lr", 16)

            tpB0 = S.n.get("tp", 0)
            ln0 = emit_ln(
                block, S, xsrc_fn=lambda i: xr[:, i * D:(i + 1) * D],
                ntiles=2, stats=stats, lnscr=lnscr_a,
                apply_fn=lambda v, i, mean, rstd: v.tensor_scalar(
                    out=ynt[:], in0=xr[:, i * D:(i + 1) * D],
                    scalar1=mean, scalar2=rstd, op0=ALU.subtract, op1=ALU.mult),
                lr_wait=("lr", lr0 + 32),
                tp_base_fn=lambda i, tpB0=tpB0: ("tp", tpB0 + i * NFT)
                if (i > 0 or tpB0 > 0) else (None, 0))

            with ExitStack() as ph:
                ptr = [ph.enter_context(nc.psum_tensor(f"ptA{sl}_{j}",
                                                       [128, 128], BF16))
                       for j in range(2)]
                items = [(i, ft) for i in range(2) for ft in range(NFT)]
                emit_transposes(
                    block, S, items,
                    src_fn=lambda it: ynt[:, it[1] * 128:(it[1] + 1) * 128],
                    dst_fn=lambda it: yT[:, it[1] * SLAB + it[0] * 128:
                                         it[1] * SLAB + (it[0] + 1) * 128],
                    ident=ident[:], ptr=ptr,
                    src_tick=lambda j, ln0=ln0: ("ln", ln0 + j // NFT + 1))
            yT_done = S.n["tev"]

            with ExitStack() as phu:
                pu = phu.enter_context(nc.psum_tensor(f"pu{sl}", [96, SLAB], F32))
                ut0 = S.reserve("ut", 2)

                @block.tensor
                def _(t, yT_done=yT_done, sl=sl, ut0=ut0):
                    S.wait(t, "tev", yT_done)
                    if sl == 0:
                        S.wait(t, "cv", cv0 + 3)
                        S.wait(t, "c2w", c2w0 + 16)
                    for k in range(NFT):
                        inst = t.matmul(pu[:], lhsT=ab[:, k * 96:(k + 1) * 96],
                                        rhs=yT[:, k * SLAB:(k + 1) * SLAB],
                                        start=(k == 0), stop=(k == NFT - 1))
                    S.inc(inst, "ut")

                @block.scalar
                def _(s_, ut0=ut0):
                    S.wait(s_, "ut", ut0 + 1)
                    S.inc(s_.copy(out=us[:], in_=pu[:]), "ut")

            with ExitStack() as ph:
                pj = [ph.enter_context(nc.psum_tensor(f"pj{sl}_{j}",
                                                      [128, SLAB], F32))
                      for j in range(4)]
                prow = ph.enter_context(nc.psum_tensor(f"prow{sl}", [1, SLAB], F32))
                prep = ph.enter_context(nc.psum_tensor(f"prep{sl}", [128, SLAB], F32))
                ptv = [ph.enter_context(nc.psum_tensor(f"ptv{sl}_{j}",
                                                       [128, 128], BF16))
                       for j in range(2)]
                pm0 = S.reserve("pm", 8)
                sq0_ = S.reserve("sq", 6)
                rw0 = S.reserve("rw", 6)
                rsa0 = S.reserve("rsa", 6)
                rs0 = S.reserve("rs", 6)
                pp0 = S.reserve("pp", 6)
                pv0 = S.reserve("pv", 6)
                qm0 = S.reserve("qm", 6)
                sw0 = S.reserve("sw", 6)
                us20 = S.reserve("us2", 6)
                hd0 = S.reserve("hd", 6)
                vt0 = S.reserve("vt", 4)
                ve0 = S.reserve("ve", 6)

                @block.tensor
                def _(t, sl=sl, ut0=ut0, pm0=pm0, rw0=rw0, rsa0=rsa0, rs0=rs0,
                      pp0=pp0, sq0_=sq0_, hd0=hd0, vt0=vt0, ve0=ve0, pv0=pv0,
                      qm0=qm0, sw0=sw0, us20=us20):
                    S.wait(t, "ut", ut0 + 2)

                    def proj(m):
                        b = m % 4
                        if m >= 4:
                            S.wait(t, "qm", qm0 + m - 3)
                        elif qm0 > 0:
                            S.wait(t, "qm", qm0)
                        for k in range(NFT):
                            t.matmul(pj[b][:],
                                     lhsT=wqkv_b[:, k * 1024 + m * 128:
                                                 k * 1024 + (m + 1) * 128],
                                     rhs=yT[:, k * SLAB:(k + 1) * SLAB],
                                     start=(k == 0), stop=False)
                        r0_ = [0, 0, 0, 0, 32, 32, 64, 64][m]
                        inst = t.matmul(pj[b][:],
                                        lhsT=bcat_b[r0_:r0_ + 32,
                                                    m * 128:(m + 1) * 128],
                                        rhs=us[r0_:r0_ + 32, :],
                                        start=False, stop=True)
                        S.inc(inst, "pm")

                    def rmsmm(h):
                        S.wait(t, "sq", sq0_ + h + 1)
                        if h > 0:
                            S.wait(t, "rsa", rsa0 + h)
                        S.inc(t.matmul(prow[:], lhsT=onescol[:], rhs=sqscr[:],
                                       start=True, stop=True), "rw")
                        S.wait(t, "rs", rs0 + h + 1)
                        if h > 0:
                            S.wait(t, "pv", pv0 + h)
                            S.wait(t, "us2", us20 + h)
                        S.inc(t.matmul(prep[:], lhsT=onerow[:], rhs=rr_[:],
                                       start=True, stop=True), "pp")
                        S.wait(t, "qm", qm0 + h + 1)
                        dstbuf = q_s if h < 4 else k_s
                        dcol = (h if h < 4 else h - 4) * seq + sl * SLAB
                        S.inc(t.matmul(prep[0:64, :], lhsT=pswap[:],
                                       rhs=dstbuf[0:64, dcol:dcol + SLAB],
                                       start=True, stop=True), "sw")

                    for m in range(4):
                        proj(m)
                    for h in range(4):
                        rmsmm(h)
                    proj(4)
                    proj(5)
                    rmsmm(4)
                    rmsmm(5)
                    proj(6)
                    proj(7)
                    for vh in range(2):
                        S.wait(t, "ve", ve0 + vh * 3 + 1)
                        for ht in range(2):
                            S.inc(t.transpose(
                                ptv[ht][:],
                                in_=vtmp[:, vh * SLAB + ht * 128:
                                         vh * SLAB + (ht + 1) * 128],
                                identity=ident[:]), "vt")


                @block.scalar
                def _(s_, sl=sl, pm0=pm0, sq0_=sq0_, rw0=rw0, rsa0=rsa0, rs0=rs0,
                      pp0=pp0, pv0=pv0, hd0=hd0, vt0=vt0, ve0=ve0, sw0=sw0,
                      qm0=qm0, us20=us20):
                    for h in range(6):
                        b = [0, 1, 2, 3, 0, 1][h]
                        S.wait(s_, "pm", pm0 + h + 1)
                        if h > 0:
                            S.wait(s_, "rw", rw0 + h)
                        S.inc(s_.activation(out=sqscr[:], in_=pj[b][:],
                                            func=AF.Square), "sq")
                        S.wait(s_, "rsa", rsa0 + h + 1)
                        inst = s_.activation(out=rr_[:], in_=rr_[:], func=AF.Ln)
                        vfence(s_, S, inst)
                        S.inc(s_.activation(out=rr_[:], in_=rr_[:],
                                            func=AF.Exp, scale=-0.5), "rs")
                        S.wait(s_, "pp", pp0 + h + 1)
                        if h > 0:
                            S.wait(s_, "qm", qm0 + h)
                        S.inc(s_.copy(out=prepS[:], in_=prep[:]), "pv")
                        S.wait(s_, "sw", sw0 + h + 1)
                        if h > 0:
                            S.wait(s_, "hd", hd0 + h)
                        S.inc(s_.copy(out=usw[:], in_=prep[0:64, :]), "us2")
                    for vh in range(2):
                        S.wait(s_, "pm", pm0 + 6 + vh + 1)
                        S.inc(s_.copy(out=vtmp[:, vh * SLAB:(vh + 1) * SLAB],
                                      in_=pj[2 + vh][:]), "ve")
                        for ht in range(2):
                            S.wait(s_, "vt", vt0 + vh * 2 + ht + 1)
                            S.inc(s_.copy(
                                out=v_tm[:, vh * seq + (sl * 2 + ht) * 128:
                                         vh * seq + (sl * 2 + ht + 1) * 128],
                                in_=ptv[ht][:]), "ve")

                @block.vector
                def _(v, sl=sl, pv0=pv0, hd0=hd0, rw0=rw0, rsa0=rsa0, qm0=qm0,
                      us20=us20):
                    for h in range(6):
                        b = [0, 1, 2, 3, 0, 1][h]
                        dstbuf = q_s if h < 4 else k_s
                        dcol = (h if h < 4 else h - 4) * seq + sl * SLAB
                        S.wait(v, "rw", rw0 + h + 1)
                        S.inc(v.tensor_scalar(out=rr_[:], in0=prow[:],
                                              scalar1=1.0 / 128.0, scalar2=EPSR,
                                              op0=ALU.mult, op1=ALU.add), "rsa")
                        S.wait(v, "pv", pv0 + h + 1)
                        if h < 4:
                            inst = v.tensor_scalar_mul(out=prepS[:],
                                                       in0=prepS[:],
                                                       scalar1=ghcol[:, h:h + 1])
                            vfence(v, S, inst)
                        S.inc(v.tensor_tensor(
                            out=dstbuf[:, dcol:dcol + SLAB],
                            in0=prepS[:], in1=pj[b][:], op=ALU.mult), "qm")
                        S.wait(v, "us2", us20 + h + 1)
                        if h > 0:
                            S.wait(v, "hd", hd0 + h)
                        c_sl = cost[:, sl * SLAB:(sl + 1) * SLAB]
                        s_sl = cost[:, seq + sl * SLAB:seq + (sl + 1) * SLAB]
                        v.tensor_tensor(out=rop_a[:],
                                        in0=dstbuf[0:64, dcol:dcol + SLAB],
                                        in1=c_sl, op=ALU.mult)
                        inst = v.tensor_tensor(out=rop_b[:], in0=usw[:],
                                               in1=s_sl, op=ALU.mult)
                        vfence(v, S, inst)
                        S.inc(v.tensor_tensor(out=dstbuf[0:64, dcol:dcol + SLAB],
                                              in0=rop_a[:], in1=rop_b[:],
                                              op=ALU.add), "hd")

        hd_all = S.n["hd"]
        ve_all = S.n["ve"]
        if not attn:
            fin0 = S.reserve("fin", 16)

            @block.scalar
            def _(s_):
                S.wait(s_, "hd", hd_all)
                S.wait(s_, "ve", ve_all)
                S.inc(s_.dma_start(out=ctx_o[0:128, :],
                                   in_=q_s[:, 0:512].bitcast(BF16)), "fin", 16)

            @block.gpsimd
            def _(g):
                S.wait(g, "fin", 16)
            return nc
        ctxst0 = S.reserve("ctxst", 16 * 4 * NTS)

        with ExitStack() as ph:
            psc = [ph.enter_context(nc.psum_tensor(f"psc{j}", [128, 512], F32))
                   for j in range(2)]
            ptp = [ph.enter_context(nc.psum_tensor(f"ptp{j}", [128, 128], BF16))
                   for j in range(2)]
            pcx = ph.enter_context(nc.psum_tensor("pcx", [128, 128], F32))
            tile_no = 0

            for qh in range(4):
                kvh = qh // 2
                for i in range(NTS):
                    Li = (i + 1) * 128
                    nch = (Li + 511) // 512
                    sct0 = S.reserve("sct", nch)
                    sc0 = S.reserve("sce", nch)
                    sm0 = S.reserve("sm", 3)
                    pt0 = S.reserve("ptt", i + 1)
                    pe0 = S.reserve("pte", i + 1)
                    cxa0 = S.reserve("cxa", i + 1)
                    cx0 = S.reserve("cx", 1)
                    tile_no += 1
                    tn = tile_no

                    @block.tensor
                    def _(t, qh=qh, kvh=kvh, i=i, Li=Li, nch=nch, sct0=sct0,
                          sc0=sc0, sm0=sm0, pt0=pt0, pe0=pe0, cxa0=cxa0,
                          cx0=cx0):
                        if qh == 0 and i == 0:
                            S.wait(t, "hd", hd_all)
                            S.wait(t, "ve", ve_all)
                            S.wait(t, "msk", 1)
                        for c in range(nch):
                            w = min(512, Li - c * 512)
                            if c >= 2:
                                S.wait(t, "sce", sc0 + c - 1)
                            elif sc0 > 0:
                                S.wait(t, "sce", sc0)
                            S.inc(t.matmul(
                                psc[c % 2][0:128, 0:w],
                                lhsT=q_s[:, qh * seq + i * 128:
                                         qh * seq + (i + 1) * 128],
                                rhs=k_s[:, kvh * seq + c * 512:
                                        kvh * seq + c * 512 + w],
                                start=True, stop=True), "sct")
                        S.wait(t, "sm", sm0 + 2)

                        def avmm(j):
                            if j == 0 and cx0 > 0:
                                S.wait(t, "cx", cx0)
                            S.wait(t, "pte", pe0 + j + 1)
                            S.inc(t.matmul(
                                pcx[:],
                                lhsT=ptb[:, (j % 2) * 128:(j % 2 + 1) * 128],
                                rhs=v_tm[:, kvh * seq + j * 128:
                                         kvh * seq + (j + 1) * 128],
                                start=(j == 0), stop=(j == i)), "cxa")

                        for tt_ in range(i + 1):
                            if tt_ >= 2:
                                S.wait(t, "pte", pe0 + tt_ - 1)
                            elif pe0 > 0:
                                S.wait(t, "pte", pe0)
                            S.inc(t.transpose(ptp[tt_ % 2][:],
                                              in_=pb[:, tt_ * 128:(tt_ + 1) * 128],
                                              identity=ident[:]), "ptt")
                            if tt_ >= 1:
                                avmm(tt_ - 1)
                        avmm(i)

                    @block.vector
                    def _(v, i=i, Li=Li, nch=nch, sct0=sct0, sc0=sc0, sm0=sm0):
                        for c in range(nch):
                            w = min(512, Li - c * 512)
                            S.wait(v, "sct", sct0 + c + 1)
                            inst = v.tensor_copy(
                                out=scores[:, c * 512:c * 512 + w],
                                in_=psc[c % 2][0:128, 0:w])
                            S.inc(inst, "sce")
                        S.wait(v, "sce", sc0 + nch)
                        inst = v.tensor_tensor(out=scores[:, Li - 128:Li],
                                               in0=scores[:, Li - 128:Li],
                                               in1=m128[:], op=ALU.add)
                        vfence(v, S, inst)
                        S.inc(v.reduce_max(out=smc[:, 0:1], in_=scores[:, 0:Li],
                                           axis=AX.X, negate=True), "sm")
                        S.wait(v, "sm", sm0 + 2)
                        S.inc(v.reciprocal(out=smc[:, 2:3], in_=smc[:, 1:2]), "sm")

                    @block.scalar
                    def _(s_, i=i, Li=Li, sm0=sm0, pt0=pt0, pe0=pe0, cxa0=cxa0,
                          cx0=cx0, qh=qh, tn=tn):
                        S.wait(s_, "sm", sm0 + 1)
                        S.inc(s_.activation(out=pb[:, 0:Li], in_=scores[:, 0:Li],
                                            func=AF.Exp, bias=smc[:, 0:1],
                                            accum_out=smc[:, 1:2]), "sm")
                        for tt_ in range(i + 1):
                            S.wait(s_, "ptt", pt0 + tt_ + 1)
                            if tt_ >= 2:
                                S.wait(s_, "cxa", cxa0 + tt_ - 1)
                            S.inc(s_.copy(out=ptb[:, (tt_ % 2) * 128:
                                              (tt_ % 2 + 1) * 128],
                                          in_=ptp[tt_ % 2][:]), "pte")
                        S.wait(s_, "cxa", cxa0 + i + 1)
                        S.wait(s_, "sm", sm0 + 3)
                        S.wait(s_, "ctxst", ctxst0 + 16 * (tn - 1))
                        inst = s_.mul(out=ctxev[:], in_=pcx[:], mul=smc[:, 2:3])
                        vfence(s_, S, inst)
                        S.inc(s_.dma_start(
                            out=ctx_o.rearrange("(a p) e -> p a e", p=128)
                                [:, i, qh * 128:(qh + 1) * 128],
                            in_=ctxev[:]), "ctxst", 16)
                        S.inc(s_.copy(out=smc[:, 3:4], in_=smc[:, 2:3]), "cx")

        final = S.n["ctxst"]

        @block.gpsimd
        def _(g):
            S.wait(g, "ctxst", final)

    return nc




def build_L3a(seq=4096, dbg=False, attn=True, maxtiles=None):
    import math
    nc = bass.Bass("TRN2", target_bir_lowering=False, debug=False)
    x_in = nc.dram_tensor("x", [seq, D], F32, kind="ExternalInput").ap()
    Wsl = nc.dram_tensor("Wsl", [D, 1536], F32, kind="ExternalInput").ap()
    lnsc = nc.dram_tensor("ln_scale", [128], F32, kind="ExternalInput").ap()
    lnbi = nc.dram_tensor("ln_bias", [128], F32, kind="ExternalInput").ap()
    ctx_o = nc.dram_tensor("ctx", [seq, 512], BF16, kind="ExternalOutput").ap()
    qd = nc.dram_tensor("qdram", [128, 4 * seq], BF16, kind="Internal").ap()

    SLAB = 256
    NSL = seq // SLAB
    NTS = seq // 128
    EPSL = 1e-5

    with ExitStack() as st:
        sb = lambda n_, sh, dt=F32: st.enter_context(nc.sbuf_tensor(n_, sh, dt))
        S = Sems(nc, st)

        ident = sb("ident", [128, 128], BF16)
        xr = sb("xr", [128, 2 * D])
        stats = sb("stats", [128, 24])
        ynt = sb("ynt", [128, D], BF16)
        yT = sb("yT", [128, NFT * SLAB], BF16)
        wf = sb("wf", [128, 2 * 384])
        wqkv_b = sb("wqkv_b", [128, 16 * 1536], BF16)
        k_s = sb("k_s", [128, 4 * seq], BF16)
        v_tm = sb("v_tm", [128, 4 * seq], BF16)
        cost = st.enter_context(nc.sbuf_tensor("cost", [128, 2 * seq], BF16))
        scratch = sb("scratch", [128, D])
        pswap = st.enter_context(nc.sbuf_tensor("pswap", [128, 128], BF16))
        m128m = sb("m128m", [128, 128])
        qraw = st.enter_context(nc.sbuf_tensor("qraw", [128, 2 * SLAB], BF16))
        usw = st.enter_context(nc.sbuf_tensor("usw", [128, SLAB], BF16))
        rop_a = st.enter_context(nc.sbuf_tensor("rop_a", [128, SLAB], BF16))
        rop_b = st.enter_context(nc.sbuf_tensor("rop_b", [128, SLAB], BF16))
        qrp = st.enter_context(nc.sbuf_tensor("qrp", [128, 2 * SLAB], BF16))
        sqscr = sb("sqscr", [128, SLAB])
        lnt = sb("lnt", [128, SLAB])
        qsl = st.enter_context(nc.sbuf_tensor("qsl", [128, 2 * SLAB], BF16))
        vtmp = st.enter_context(nc.sbuf_tensor("vtmp", [128, SLAB], BF16))
        pb = sb("pb", [128, seq], BF16)
        asum = sb("asum", [128, 8])
        qtb = st.enter_context(nc.sbuf_tensor("qtb", [128, 2 * 128], BF16))
        ptb = st.enter_context(nc.sbuf_tensor("ptb", [128, 2 * 128], BF16))
        ctxev = st.enter_context(nc.sbuf_tensor("ctxev", [128, 128], BF16))
        smc = sb("smc", [128, 4])
        rr_ = st.enter_context(nc.sbuf_tensor("rr_", [1, 3 * SLAB], F32))
        onesR = st.enter_context(nc.sbuf_tensor("onesR", [1, SLAB], F32))
        scaleR = st.enter_context(nc.sbuf_tensor("scaleR", [1, 128], F32))
        nbiasR = st.enter_context(nc.sbuf_tensor("nbiasR", [1, 128], F32))
        epsc = st.enter_context(nc.sbuf_tensor("epsc", [1, 1], F32))
        onescol = sb("onescol", [128, 1])
        onescolb = st.enter_context(nc.sbuf_tensor("onescolb", [128, 1], BF16))
        lnscr = scratch[:, 0:D]
        t1 = scratch[:, 0:512]
        t2 = scratch[:, 512:1024]

        block = st.enter_context(nc.Block())
        emit_ident(block, S, ident[:])

        msk0 = S.reserve("msk", 2)

        @block.gpsimd
        def _(g):
            inst = g.memset(m128m[:], 1.0)
            vfence(g, S, inst)
            S.inc(g.affine_select(out=m128m[:], in_=m128m[:], compare_op=ALU.is_ge,
                                  fill=0.0, base=0, pattern=[[-1, 128]],
                                  channel_multiplier=1), "msk")
            inst = g.memset(pswap[:], 0.0)
            vfence(g, S, inst)
            inst = g.affine_select(out=pswap[:], in_=pswap[:],
                                   compare_op=ALU.not_equal, fill=1.0,
                                   base=-64, pattern=[[1, 128]],
                                   channel_multiplier=-1)
            vfence(g, S, inst)
            S.inc(g.affine_select(out=pswap[:], in_=pswap[:],
                                  compare_op=ALU.not_equal, fill=1.0,
                                  base=64, pattern=[[1, 128]],
                                  channel_multiplier=-1), "msk")

        lr0_pre = S.reserve("lr", 16 * 2)
        sr0 = S.reserve("sr", 16 * 2)
        cv0 = S.reserve("cv", 1)

        @block.sync
        def _(sync):
            for i in range(2):
                S.inc(sync.dma_start(
                    out=xr[:, i * D:(i + 1) * D],
                    in_=x_in.rearrange("(a p) d -> p a d", p=128)[:, i, :]),
                    "lr", 16)
            S.inc(sync.dma_start(out=scaleR[:], in_=lnsc.unsqueeze(0)), "sr", 16)
            S.inc(sync.dma_start(out=nbiasR[:], in_=lnbi.unsqueeze(0)), "sr", 16)

        @block.vector
        def _(v):
            v.memset(onescol[:], 1.0)
            v.memset(onescolb[:], 1.0)
            v.memset(onesR[:], 1.0)
            v.memset(epsc[:], EPSL)
            S.wait(v, "sr", sr0 + 32)
            S.inc(v.tensor_scalar_mul(out=nbiasR[:], in0=nbiasR[:],
                                      scalar1=-1.0), "cv")

        NSTR = 64
        rl = [S.reserve(f"rl{k}", 16 * (NSTR // 2)) for k in range(2)]
        c2w0 = S.reserve("c2w", NSTR)

        @block.sync
        def _(sync):
            for j in range(NSTR):
                kk, qq = j // 4, j % 4
                if j >= 2:
                    S.wait(sync, "c2w", c2w0 + j - 1)
                S.inc(sync.dma_start(
                    out=wf[:, (j % 2) * 384:(j % 2) * 384 + 384],
                    in_=Wsl[kk * 128:(kk + 1) * 128, qq * 384:(qq + 1) * 384]),
                    f"rl{j % 2}", 16)

        @block.vector
        def _(v):
            for j in range(NSTR):
                kk, qq = j // 4, j % 4
                S.wait(v, f"rl{j % 2}", rl[j % 2] + 16 * (j // 2 + 1))
                S.inc(v.tensor_copy(
                    out=wqkv_b[:, kk * 1536 + qq * 384:kk * 1536 + (qq + 1) * 384],
                    in_=wf[:, (j % 2) * 384:(j % 2) * 384 + 384]), "c2w")

        rp_total = emit_rope_tables(block, S, None, cost, scratch[:, 0:769],
                                    tag="rp", npos=seq, nf=128, base_div=64.0,
                                    chunk=256, flip_first=True)

        @block.scalar
        def _(s_):
            S.wait(s_, "rp", rp_total)

        for sl in range(NSL):
            t0 = sl * 2
            if sl == 0:
                lr0 = lr0_pre
            else:
                lr0 = S.reserve("lr", 16 * 2)
                lnprev = S.n.get("ln", 0) - 2

                @block.sync
                def _(sync, sl=sl, t0=t0, lnprev=lnprev):
                    for i in range(2):
                        S.wait(sync, "ln", lnprev + 2)
                        S.inc(sync.dma_start(
                            out=xr[:, i * D:(i + 1) * D],
                            in_=x_in.rearrange("(a p) d -> p a d", p=128)
                                [:, t0 + i, :]), "lr", 16)

            tpB0 = S.n.get("tp", 0)
            ln0 = emit_ln(
                block, S, xsrc_fn=lambda i: xr[:, i * D:(i + 1) * D],
                ntiles=2, stats=stats, lnscr=lnscr,
                apply_fn=lambda v, i, mean, rstd: v.tensor_scalar(
                    out=ynt[:], in0=xr[:, i * D:(i + 1) * D],
                    scalar1=mean, scalar2=rstd, op0=ALU.subtract, op1=ALU.mult),
                lr_wait=("lr", lr0 + 32),
                tp_base_fn=lambda i, tpB0=tpB0: ("tp", tpB0 + i * NFT)
                if (i > 0 or tpB0 > 0) else (None, 0))

            with ExitStack() as ph:
                ptr = [ph.enter_context(nc.psum_tensor(f"ptA{sl}_{j}",
                                                       [128, 128], BF16))
                       for j in range(2)]
                items = [(i, ft) for i in range(2) for ft in range(NFT)]
                emit_transposes(
                    block, S, items,
                    src_fn=lambda it: ynt[:, it[1] * 128:(it[1] + 1) * 128],
                    dst_fn=lambda it: yT[:, it[1] * SLAB + it[0] * 128:
                                         it[1] * SLAB + (it[0] + 1) * 128],
                    ident=ident[:], ptr=ptr,
                    src_tick=lambda j, ln0=ln0: ("ln", ln0 + j // NFT + 1))
            yT_done = S.n["tev"]

            with ExitStack() as ph:
                pj = [ph.enter_context(nc.psum_tensor(f"pj{sl}_{j}",
                                                      [128, SLAB], F32))
                      for j in range(4)]
                psw = ph.enter_context(nc.psum_tensor(f"psw{sl}", [128, SLAB],
                                                      F32))
                prow = ph.enter_context(nc.psum_tensor(f"prow{sl}",
                                                       [1, 2 * SLAB], F32))
                prep = ph.enter_context(nc.psum_tensor(f"prep{sl}",
                                                       [128, 2 * SLAB], F32))
                ptv = ph.enter_context(nc.psum_tensor(f"ptv{sl}",
                                                      [128, 2 * 128], BF16))
                vea_prev = S.n.get("vea", 0)
                pm0 = S.reserve("pm", 12)
                qe0 = S.reserve("qe", 8)
                sw0 = S.reserve("sw", 8)
                us20 = S.reserve("us2", 8)
                hd0 = S.reserve("hd", 8)
                sq10 = S.reserve("sq1", 8)
                rw0 = S.reserve("rw", 16)
                rsa0 = S.reserve("rsa", 8)
                rs0 = S.reserve("rs", 8)
                mr0 = S.reserve("mr", 8)
                pp0 = S.reserve("pp", 24)
                qm0 = S.reserve("qm", 8)
                qsA = [S.reserve(f"qs{j}", 16 * 2) for j in range(2)]
                vea0 = S.reserve("vea", 4)
                vt0 = S.reserve("vt", 8)
                veb0 = S.reserve("veb", 8)

                @block.tensor
                def _(t, sl=sl, yT_done=yT_done, vea_prev=vea_prev, pm0=pm0,
                      qe0=qe0, sw0=sw0, us20=us20, hd0=hd0, sq10=sq10, rw0=rw0,
                      rsa0=rsa0, rs0=rs0, mr0=mr0, pp0=pp0, qm0=qm0, vea0=vea0,
                      vt0=vt0, veb0=veb0):
                    S.wait(t, "tev", yT_done)
                    if sl == 0:
                        S.wait(t, "cv", cv0 + 1)
                        S.wait(t, "c2w", c2w0 + NSTR)
                        S.wait(t, "msk", 2)

                    def proj(m):
                        b = m % 4
                        if m < 4:
                            if vea_prev > 0:
                                S.wait(t, "vea", vea_prev)
                        else:
                            S.wait(t, "qe", qe0 + (m - 4) + 1)
                        for k in range(NFT):
                            inst = t.matmul(
                                pj[b][:],
                                lhsT=wqkv_b[:, k * 1536 + m * 128:
                                            k * 1536 + (m + 1) * 128],
                                rhs=yT[:, k * SLAB:(k + 1) * SLAB],
                                start=(k == 0), stop=(k == NFT - 1))
                        S.inc(inst, "pm")

                    for m in range(4):
                        proj(m)
                    for h in range(8):
                        slot = (h % 2) * SLAB
                        S.wait(t, "qe", qe0 + h + 1)
                        S.wait(t, "us2", us20 + h)
                        S.inc(t.matmul(psw[:], lhsT=pswap[:],
                                       rhs=qraw[:, slot:slot + SLAB],
                                       start=True, stop=True), "sw")
                        if h + 4 <= 11:
                            proj(h + 4)
                        S.wait(t, "hd", hd0 + h + 1)
                        S.wait(t, "rsa", rsa0 + h)
                        S.inc(t.matmul(prow[0:1, 0:SLAB], lhsT=onescolb[:],
                                       rhs=qrp[:, slot:slot + SLAB],
                                       start=True, stop=True), "rw")
                        S.wait(t, "sq1", sq10 + h + 1)
                        S.inc(t.matmul(prow[0:1, SLAB:2 * SLAB],
                                       lhsT=onescol[:], rhs=sqscr[:],
                                       start=True, stop=True), "rw")
                        S.wait(t, "rs", rs0 + h + 1)
                        S.wait(t, "qm", qm0 + h)
                        S.inc(t.matmul(prep[:, 0:SLAB], lhsT=scaleR[:],
                                       rhs=rr_[0:1, SLAB:2 * SLAB],
                                       start=True, stop=True), "pp")
                        S.wait(t, "mr", mr0 + h + 1)
                        S.inc(t.matmul(prep[:, SLAB:2 * SLAB], lhsT=scaleR[:],
                                       rhs=rr_[0:1, 2 * SLAB:3 * SLAB],
                                       start=True, stop=False), "pp")
                        S.inc(t.matmul(prep[:, SLAB:2 * SLAB], lhsT=nbiasR[:],
                                       rhs=onesR[:],
                                       start=False, stop=True), "pp")
                    for vh in range(4):
                        S.wait(t, "vea", vea0 + vh + 1)
                        if vh > 0:
                            S.wait(t, "veb", veb0 + vh * 2)
                        elif veb0 > 0:
                            S.wait(t, "veb", veb0)
                        for ht in range(2):
                            S.inc(t.transpose(
                                ptv[:, ht * 128:(ht + 1) * 128],
                                in_=vtmp[:, ht * 128:(ht + 1) * 128],
                                identity=ident[:]), "vt")

                @block.scalar
                def _(s_, sl=sl, pm0=pm0, qe0=qe0, sw0=sw0, us20=us20, hd0=hd0,
                      sq10=sq10, rw0=rw0, rsa0=rsa0, rs0=rs0, qm0=qm0,
                      vea0=vea0, vt0=vt0, veb0=veb0):
                    if sl == 0:
                        S.wait(s_, "cv", cv0 + 1)
                    for h in range(8):
                        slot = (h % 2) * SLAB
                        S.wait(s_, "pm", pm0 + h + 1)
                        if h >= 2:
                            S.wait(s_, "hd", hd0 + h - 1)
                        elif hd0 > 0:
                            S.wait(s_, "hd", hd0)
                        S.inc(s_.copy(out=qraw[:, slot:slot + SLAB],
                                      in_=pj[h % 4][:]), "qe")
                        S.wait(s_, "sw", sw0 + h + 1)
                        if h > 0:
                            S.wait(s_, "hd", hd0 + h)
                        S.inc(s_.copy(out=usw[:], in_=psw[:]), "us2")
                        S.wait(s_, "hd", hd0 + h + 1)
                        if h > 0:
                            S.wait(s_, "rw", rw0 + 2 * h)
                        elif rw0 > 0:
                            S.wait(s_, "rw", rw0)
                        S.inc(s_.activation(out=sqscr[:],
                                            in_=qrp[:, slot:slot + SLAB],
                                            func=AF.Square), "sq1")
                        S.wait(s_, "rsa", rsa0 + h + 1)
                        inst = s_.activation(out=rr_[0:1, SLAB:2 * SLAB],
                                             in_=rr_[0:1, SLAB:2 * SLAB],
                                             func=AF.Ln, bias=epsc[0:1, 0:1])
                        vfence(s_, S, inst)
                        S.inc(s_.activation(out=rr_[0:1, SLAB:2 * SLAB],
                                            in_=rr_[0:1, SLAB:2 * SLAB],
                                            func=AF.Exp, scale=-0.5), "rs")
                        if h < 4:
                            S.wait(s_, "qm", qm0 + h + 1)
                            S.inc(s_.dma_start(
                                out=qd[:, h * seq + sl * SLAB:
                                       h * seq + (sl + 1) * SLAB],
                                in_=qsl[:, slot:slot + SLAB]), f"qs{h % 2}", 16)
                    for vh in range(4):
                        S.wait(s_, "pm", pm0 + 9 + vh)
                        if vh >= 1:
                            S.wait(s_, "vt", vt0 + 2 * vh)
                        elif vt0 > 0:
                            S.wait(s_, "vt", vt0)
                        S.inc(s_.copy(out=vtmp[:], in_=pj[vh][:]), "vea")
                        for ht in range(2):
                            S.wait(s_, "vt", vt0 + 2 * vh + 2)
                            S.inc(s_.copy(
                                out=v_tm[:, vh * seq + (sl * 2 + ht) * 128:
                                         vh * seq + (sl * 2 + ht + 1) * 128],
                                in_=ptv[:, ht * 128:(ht + 1) * 128]), "veb")

                @block.vector
                def _(v, sl=sl, qe0=qe0, us20=us20, hd0=hd0, rw0=rw0, rsa0=rsa0,
                      rs0=rs0, mr0=mr0, pp0=pp0, qm0=qm0, qsA=qsA):
                    for h in range(8):
                        slot = (h % 2) * SLAB
                        S.wait(v, "us2", us20 + h + 1)
                        if sl == 0 and h == 0:
                            S.wait(v, "rp", rp_total)
                        c_sl = cost[:, sl * SLAB:(sl + 1) * SLAB]
                        s_sl = cost[:, seq + sl * SLAB:seq + (sl + 1) * SLAB]
                        v.tensor_tensor(out=rop_a[:],
                                        in0=qraw[:, slot:slot + SLAB],
                                        in1=c_sl, op=ALU.mult)
                        inst = v.tensor_tensor(out=rop_b[:], in0=usw[:],
                                               in1=s_sl, op=ALU.mult)
                        vfence(v, S, inst)
                        S.inc(v.tensor_tensor(out=qrp[:, slot:slot + SLAB],
                                              in0=rop_a[:], in1=rop_b[:],
                                              op=ALU.add), "hd")
                        S.wait(v, "rw", rw0 + 2 * h + 2)
                        S.wait(v, "pp", pp0 + 3 * h)
                        inst = v.tensor_scalar_mul(out=rr_[0:1, 0:SLAB],
                                                   in0=prow[0:1, 0:SLAB],
                                                   scalar1=1.0 / 128.0)
                        vfence(v, S, inst)
                        inst = v.tensor_tensor(out=rr_[0:1, 2 * SLAB:3 * SLAB],
                                               in0=rr_[0:1, 0:SLAB],
                                               in1=rr_[0:1, 0:SLAB],
                                               op=ALU.mult)
                        vfence(v, S, inst)
                        S.inc(v.scalar_tensor_tensor(
                            out=rr_[0:1, SLAB:2 * SLAB],
                            in0=prow[0:1, SLAB:2 * SLAB], scalar=1.0 / 128.0,
                            in1=rr_[0:1, 2 * SLAB:3 * SLAB],
                            op0=ALU.mult, op1=ALU.subtract), "rsa")
                        S.wait(v, "rs", rs0 + h + 1)
                        S.inc(v.tensor_tensor(out=rr_[0:1, 2 * SLAB:3 * SLAB],
                                              in0=rr_[0:1, 0:SLAB],
                                              in1=rr_[0:1, SLAB:2 * SLAB],
                                              op=ALU.mult), "mr")
                        S.wait(v, "pp", pp0 + 3 * h + 3)
                        inst = v.tensor_copy(out=lnt[:],
                                             in_=qrp[:, slot:slot + SLAB])
                        vfence(v, S, inst)
                        inst = v.tensor_tensor(out=lnt[:], in0=lnt[:],
                                               in1=prep[:, 0:SLAB],
                                               op=ALU.mult)
                        vfence(v, S, inst)
                        if h < 4:
                            need = qsA[h % 2] + (16 if h >= 2 else 0)
                            if need > 0:
                                S.wait(v, f"qs{h % 2}", need)
                            dst = qsl[:, slot:slot + SLAB]
                        else:
                            dst = k_s[:, (h - 4) * seq + sl * SLAB:
                                      (h - 4) * seq + (sl + 1) * SLAB]
                        S.inc(v.tensor_tensor(out=dst, in0=lnt[:],
                                              in1=prep[:, SLAB:2 * SLAB],
                                              op=ALU.subtract), "qm")

        qm_all = S.n["qm"]
        veb_all = S.n["veb"]
        qs_all = [S.n["qs0"], S.n["qs1"]]
        if not attn:
            fin0 = S.reserve("fin", 16)

            @block.scalar
            def _(s_):
                S.wait(s_, "qm", qm_all)
                S.wait(s_, "veb", veb_all)
                S.wait(s_, "qs0", qs_all[0])
                S.wait(s_, "qs1", qs_all[1])
                S.inc(s_.dma_start(out=ctx_o[0:128, :], in_=k_s[:, 0:512]),
                      "fin", 16)

            @block.gpsimd
            def _(g):
                S.wait(g, "fin", 16)
            return nc
        ctxst0 = 0
        S.sem("ctxst")
        sct_cum = []
        ptt_cum = 0

        with ExitStack() as ph:
            psc = [ph.enter_context(nc.psum_tensor(f"psc{j}", [128, 512], F32))
                   for j in range(2)]
            ptp = [ph.enter_context(nc.psum_tensor(f"ptp{j}", [128, 128], BF16))
                   for j in range(2)]
            pcx = ph.enter_context(nc.psum_tensor("pcx", [128, 128], F32))
            ti = -1

            for qh in range(4):
                for i in range(NTS):
                    if maxtiles is not None and ti + 1 >= maxtiles:
                        break
                    ti += 1
                    Li = (i + 1) * 128
                    nch = (i + 4) // 4
                    wd = Li - (nch - 1) * 512
                    ptt_prev = ptt_cum
                    ptt_cum += i + 1
                    sct0 = S.reserve("sct", nch)
                    sct_cum.append(S.n["sct"])
                    pbd0 = S.reserve("pbd", nch)
                    sq2d0 = S.reserve("sq2d", 1)
                    rcd0 = S.reserve("rcd", 1)
                    cxm0 = S.reserve("cxm", 1)
                    pt0 = S.reserve("ptt", i + 1)
                    pe0 = S.reserve("pte", i + 1)
                    cxa0 = S.reserve("cxa", i + 1)
                    S.reserve(f"ql{ti % 2}", 16)
                    S.reserve("ctxst", 16)

                    @block.sync
                    def _(sync, qh=qh, i=i, ti=ti):
                        if ti == 0:
                            S.wait(sync, "qs0", qs_all[0])
                            S.wait(sync, "qs1", qs_all[1])
                        if ti >= 2:
                            S.wait(sync, "sct", sct_cum[ti - 2])
                        S.inc(sync.dma_start(
                            out=qtb[:, (ti % 2) * 128:(ti % 2 + 1) * 128],
                            in_=qd[:, qh * seq + i * 128:
                                   qh * seq + (i + 1) * 128]),
                            f"ql{ti % 2}", 16)

                    @block.tensor
                    def _(t, qh=qh, i=i, ti=ti, Li=Li, nch=nch, sct0=sct0,
                          pbd0=pbd0, pt0=pt0, pe0=pe0, cxm0=cxm0):
                        if ti == 0:
                            S.wait(t, "qm", qm_all)
                            S.wait(t, "veb", veb_all)
                        S.wait(t, f"ql{ti % 2}", 16 * (ti // 2 + 1))
                        for c in range(nch):
                            w = min(512, Li - c * 512)
                            if c >= 2:
                                S.wait(t, "pbd", pbd0 + c - 1)
                            elif pbd0 > 0:
                                S.wait(t, "pbd", pbd0)
                            S.inc(t.matmul(
                                psc[c % 2][0:128, 0:w],
                                lhsT=qtb[:, (ti % 2) * 128:(ti % 2 + 1) * 128],
                                rhs=k_s[:, qh * seq + c * 512:
                                        qh * seq + c * 512 + w],
                                start=True, stop=True), "sct")

                        def avmm(j):
                            if j == 0 and ti > 0:
                                S.wait(t, "cxm", cxm0)
                            S.wait(t, "pte", pe0 + j + 1)
                            S.inc(t.matmul(
                                pcx[:],
                                lhsT=ptb[:, (j % 2) * 128:(j % 2 + 1) * 128],
                                rhs=v_tm[:, qh * seq + j * 128:
                                         qh * seq + (j + 1) * 128],
                                start=(j == 0), stop=(j == i)), "cxa")

                        for tt_ in range(i + 1):
                            S.wait(t, "pbd", pbd0 + (tt_ * 128) // 512 + 1)
                            if tt_ >= 2:
                                S.wait(t, "pte", pe0 + tt_ - 1)
                            elif pe0 > 0:
                                S.wait(t, "pte", pe0)
                            S.inc(t.transpose(
                                ptp[tt_ % 2][:],
                                in_=pb[:, tt_ * 128:(tt_ + 1) * 128],
                                identity=ident[:]), "ptt")
                            if tt_ >= 1:
                                avmm(tt_ - 1)
                        avmm(i)

                    @block.scalar
                    def _(s_, qh=qh, i=i, ti=ti, Li=Li, nch=nch, wd=wd,
                          sct0=sct0, rcd0=rcd0, pt0=pt0, pe0=pe0, cxa0=cxa0,
                          pbd0=pbd0):
                        for c in range(nch):
                            w = min(512, Li - c * 512)
                            S.wait(s_, "sct", sct0 + c + 1)
                            if c == 0 and ti > 0:
                                S.wait(s_, "rcd", rcd0)
                            if c >= 1:
                                S.wait(s_, "pbd", pbd0 + c)
                            inst = s_.activation(out=t1[:, 0:w],
                                                 in_=psc[c % 2][0:128, 0:w],
                                                 func=AF.Square)
                            vfence(s_, S, inst)
                            if c < nch - 1:
                                S.inc(s_.activation(
                                    out=pb[:, c * 512:(c + 1) * 512],
                                    in_=t1[:, 0:512], func=AF.Square,
                                    accum_out=asum[:, c:c + 1]), "pbd")
                            else:
                                S.inc(s_.activation(out=t2[:, 0:wd],
                                                    in_=t1[:, 0:wd],
                                                    func=AF.Square), "sq2d")
                        for tt_ in range(i + 1):
                            S.wait(s_, "ptt", pt0 + tt_ + 1)
                            if tt_ >= 2:
                                S.wait(s_, "cxa", cxa0 + tt_ - 1)
                            S.inc(s_.copy(
                                out=ptb[:, (tt_ % 2) * 128:(tt_ % 2 + 1) * 128],
                                in_=ptp[tt_ % 2][:]), "pte")
                        S.wait(s_, "cxa", cxa0 + i + 1)
                        S.wait(s_, "rcd", rcd0 + 1)
                        S.wait(s_, "ctxst", ctxst0 + 16 * ti)
                        inst = s_.mul(out=ctxev[:], in_=pcx[:], mul=smc[:, 1:2])
                        vfence(s_, S, inst)
                        S.inc(s_.dma_start(
                            out=ctx_o.rearrange("(a p) e -> p a e", p=128)
                                [:, i, qh * 128:(qh + 1) * 128],
                            in_=ctxev[:]), "ctxst", 16)
                        S.inc(s_.copy(out=smc[:, 2:3], in_=smc[:, 1:2]), "cxm")

                    @block.vector
                    def _(v, i=i, ti=ti, nch=nch, wd=wd, pbd0=pbd0,
                          sq2d0=sq2d0, cxm0=cxm0, ptt_prev=ptt_prev):
                        if ti == 0:
                            S.wait(v, "msk", 2)
                        if ti > 0:
                            S.wait(v, "ptt", ptt_prev)
                        S.wait(v, "sq2d", sq2d0 + 1)
                        inst = v.tensor_tensor(out=t2[:, wd - 128:wd],
                                               in0=t2[:, wd - 128:wd],
                                               in1=m128m[:], op=ALU.mult)
                        vfence(v, S, inst)
                        S.inc(v.tensor_copy(
                            out=pb[:, (nch - 1) * 512:(nch - 1) * 512 + wd],
                            in_=t2[:, 0:wd]), "pbd")
                        inst = v.reduce_sum(out=asum[:, nch - 1:nch],
                                            in_=t2[:, 0:wd], axis=AX.X)
                        vfence(v, S, inst)
                        S.wait(v, "pbd", pbd0 + nch)
                        inst = v.reduce_sum(out=smc[:, 0:1],
                                            in_=asum[:, 0:nch], axis=AX.X)
                        vfence(v, S, inst)
                        if ti > 0:
                            S.wait(v, "cxm", cxm0)
                        S.inc(v.reciprocal(out=smc[:, 1:2], in_=smc[:, 0:1]),
                              "rcd")

        final = S.n["ctxst"]

        @block.gpsimd
        def _(g):
            S.wait(g, "ctxst", final)

    return nc


def build_L0a(seq=4096, dbg=False):
    import math
    nc = bass.Bass("TRN2", target_bir_lowering=False, debug=False)
    x_in = nc.dram_tensor("x", [seq, D], F32, kind="ExternalInput").ap()
    Wsl = nc.dram_tensor("Wsl", [D, 3072], F32, kind="ExternalInput").ap()
    Wba = nc.dram_tensor("Wba", [D, 16], F32, kind="ExternalInput").ap()
    convw = nc.dram_tensor("convw", [2048, 4], F32, kind="ExternalInput").ap()
    alog = nc.dram_tensor("alog", [8], F32, kind="ExternalInput").ap()
    dtb = nc.dram_tensor("dtb", [8], F32, kind="ExternalInput").ap()
    nw_in = nc.dram_tensor("nw", [128], F32, kind="ExternalInput").ap()
    hout = nc.dram_tensor("hout", [seq, 1024], BF16, kind="ExternalOutput").ap()
    mixd = nc.dram_tensor("mixd", [seq, 3072], BF16, kind="Internal").ap()
    bad = nc.dram_tensor("bad", [16, seq], F32, kind="Internal").ap()
    gd = nc.dram_tensor("gd", [8, seq], F32, kind="Internal").ap()
    lnbd = nc.dram_tensor("lnbd", [8, seq], F32, kind="Internal").ap()

    SLAB = 256
    NCH = seq // 128
    HSEQ = seq // 2
    NSLH = HSEQ // SLAB
    HCH = HSEQ // 128
    W4 = min(512, HSEQ)
    T4N = HSEQ // W4
    EPSL2 = 1e-6

    with ExitStack() as st:
        sb = lambda n_, sh, dt=F32: st.enter_context(nc.sbuf_tensor(n_, sh, dt))
        S = Sems(nc, st)

        ident = sb("ident", [128, 128], BF16)
        AR = st.enter_context(nc.sbuf_tensor("AR", [128, 32768], BF16))
        yT = AR[:, 0:16 * HSEQ]
        qtm = AR[:, 0:NCH * 128]
        ktm = AR[:, NCH * 128:2 * NCH * 128]
        qnfm = AR[:, 2 * NCH * 128:3 * NCH * 128]
        knfm = AR[:, 3 * NCH * 128:4 * NCH * 128]
        valab = [AR[:, (4 + 2 * j) * NCH * 128:(5 + 2 * j) * NCH * 128]
                 for j in range(2)]
        kcdab = [AR[:, (5 + 2 * j) * NCH * 128:(6 + 2 * j) * NCH * 128]
                 for j in range(2)]
        attnab = [qnfm, knfm]

        xr = sb("xr", [128, 2 * D])
        bab = xr[0:16, 0:seq]
        stats = sb("stats", [128, 24])
        lnscr = sb("lnscr", [128, D])
        ynt = sb("ynt", [128, D], BF16)
        wmf = sb("wmf", [128, 16 * 128])
        wmb = st.enter_context(nc.sbuf_tensor("wmb", [128, 2 * 16 * 128], BF16))
        wbab = st.enter_context(nc.sbuf_tensor("wbab", [128, 16 * 16], BF16))
        cbuf = sb("cbuf", [128, HSEQ + 3])
        cvt = sb("cvt", [128, HSEQ])
        svt = st.enter_context(nc.sbuf_tensor("svt", [128, HSEQ], BF16))
        stg = st.enter_context(nc.sbuf_tensor("stg", [128, 2 * 128], BF16))
        bstage = sb("bstage", [16, HSEQ])
        cmarg = sb("cmarg", [128, 3 * 16])
        wcv = sb("wcv", [128, 4 * 16])
        gc = sb("gc", [128, 2 * 128])
        gcng = sb("gcng", [128, 2 * 128])
        glb = sb("glb", [128, 2 * 128])
        egc = sb("egc", [128, 2 * 128])
        eglc = sb("eglc", [128, 2 * 128])
        ebgc = sb("ebgc", [128, 2 * 128])
        btc = sb("btc", [128, 2 * 128])
        eglr = sb("eglr", [128, 2 * 128])
        gcT = sb("gcT", [128, 2 * 128])
        lnbT = sb("lnbT", [128, 2 * 128])
        identf = sb("identf", [128, 128])
        onesr = st.enter_context(nc.sbuf_tensor("onesr", [1, 128], F32))
        acol = sb("acol", [128, 8])
        onerow = st.enter_context(nc.sbuf_tensor("onerow", [1, 128], F32))
        nwrep = sb("nwrep", [128, 128])
        mstr = sb("mstr", [128, 128])
        mupi = sb("mupi", [128, 128])
        dexp = sb("dexp", [128, 2 * 128])
        dmt_s = sb("dmt_s", [128, 2 * 128])
        At_s = st.enter_context(nc.sbuf_tensor("At_s", [128, 2 * 128], BF16))
        Ap_s = st.enter_context(nc.sbuf_tensor("Ap_s", [128, 2 * 128], BF16))
        Atp_s = st.enter_context(nc.sbuf_tensor("Atp_s", [128, 2 * 128], BF16))
        App_s = st.enter_context(nc.sbuf_tensor("App_s", [128, 2 * 128], BF16))
        Qp_s = st.enter_context(nc.sbuf_tensor("Qp_s", [128, 2 * 2 * 128], BF16))
        vbt = st.enter_context(nc.sbuf_tensor("vbt", [128, 2 * 128], BF16))
        kbgt = st.enter_context(nc.sbuf_tensor("kbgt", [128, 2 * 128], BF16))
        vstr = st.enter_context(nc.sbuf_tensor("vstr", [128, 2 * 2 * 128], BF16))
        zstr = st.enter_context(nc.sbuf_tensor("zstr", [128, 2 * 2 * 128], BF16))
        vnew = st.enter_context(nc.sbuf_tensor("vnew", [128, 2 * 128], BF16))
        qg_s = st.enter_context(nc.sbuf_tensor("qg_s", [128, 2 * 2 * 128], BF16))
        qgT_s = st.enter_context(nc.sbuf_tensor("qgT_s", [128, 2 * 2 * 128], BF16))
        kg2_s = st.enter_context(nc.sbuf_tensor("kg2_s", [128, 2 * 2 * 128], BF16))
        Sf = sb("Sf", [128, 2 * 128])
        Sb = st.enter_context(nc.sbuf_tensor("Sb", [128, 2 * 128], BF16))
        zsil = sb("zsil", [128, 2 * 128])
        hti = st.enter_context(nc.sbuf_tensor("hti", [128, 2 * 128], BF16))
        h1t = sb("h1t", [128, 2 * 128])
        gcol = sb("gcol", [128, 8])

        block = st.enter_context(nc.Block())
        emit_ident(block, S, ident[:])

        msk0 = S.reserve("msk", 3)

        @block.gpsimd
        def _(g):
            inst = g.memset(mstr[:], -1.0)
            vfence(g, S, inst)
            S.inc(g.affine_select(out=mstr[:], in_=mstr[:], compare_op=ALU.is_ge,
                                  fill=0.0, base=-1, pattern=[[1, 128]],
                                  channel_multiplier=-1), "msk")
            inst = g.memset(mupi[:], 1.0)
            vfence(g, S, inst)
            S.inc(g.affine_select(out=mupi[:], in_=mupi[:], compare_op=ALU.is_ge,
                                  fill=0.0, base=0, pattern=[[1, 128]],
                                  channel_multiplier=-1), "msk")
            inst = g.memset(identf[:], 0.0)
            vfence(g, S, inst)
            S.inc(g.affine_select(out=identf[:], in_=identf[:],
                                  compare_op=ALU.not_equal, fill=1.0, base=0,
                                  pattern=[[-1, 128]], channel_multiplier=1),
                  "msk")

        sr0 = S.reserve("sr", 16 * 5)
        cv0 = S.reserve("cv", 5)

        @block.sync
        def _(sync):
            S.inc(sync.dma_start(out=wmf[:, 0:256], in_=Wba.rearrange(
                "(k p) c -> p k c", p=128)), "sr", 16)
            S.inc(sync.dma_start(out=wcv[:, 0:64], in_=convw.rearrange(
                "(m p) t -> p m t", p=128)), "sr", 16)
            S.inc(sync.dma_start(out=acol[0:8, 0:1], in_=alog.unsqueeze(1)),
                  "sr", 16)
            S.inc(sync.dma_start(out=acol[0:8, 1:2], in_=dtb.unsqueeze(1)),
                  "sr", 16)
            S.inc(sync.dma_start(out=onerow[0:1, :], in_=nw_in.unsqueeze(0)),
                  "sr", 16)

        ea0 = S.reserve("ea", 1)

        @block.scalar
        def _(s_):
            S.wait(s_, "sr", sr0 + 16 * 5)
            S.inc(s_.activation(out=acol[0:8, 3:4], in_=acol[0:8, 0:1],
                                func=AF.Exp), "ea")

        @block.vector
        def _(v):
            S.wait(v, "sr", sr0 + 16 * 5)
            S.inc(v.tensor_copy(out=wbab[:], in_=wmf[:, 0:256]), "cv")
            S.inc(v.memset(acol[:, 2:3], 1.0), "cv")
            S.inc(v.memset(acol[:, 5:6], 0.0), "cv")
            S.inc(v.memset(onesr[:], 1.0), "cv")
            S.wait(v, "ea", ea0 + 1)
            S.inc(v.tensor_scalar_mul(out=acol[0:8, 4:5], in0=acol[0:8, 3:4],
                                      scalar1=-1.0), "cv")

        for hp in range(2):
            for sl in range(NSLH):
                gsl = hp * NSLH + sl
                t0 = gsl * 2
                lr0 = S.reserve("lr", 16 * 2)
                lnprev = S.n.get("ln", 0)

                @block.sync
                def _(sync, t0=t0, lnprev=lnprev, gsl=gsl):
                    for i in range(2):
                        if gsl > 0:
                            S.wait(sync, "ln", lnprev)
                        S.inc(sync.dma_start(
                            out=xr[:, i * D:(i + 1) * D],
                            in_=x_in.rearrange("(a p) d -> p a d", p=128)
                                [:, t0 + i, :]), "lr", 16)

                tpB0 = S.n.get("tp", 0)
                ln0 = emit_ln(
                    block, S, xsrc_fn=lambda i: xr[:, i * D:(i + 1) * D],
                    ntiles=2, stats=stats, lnscr=lnscr[:, 0:D],
                    apply_fn=lambda v, i, mean, rstd: v.tensor_scalar(
                        out=ynt[:], in0=xr[:, i * D:(i + 1) * D],
                        scalar1=mean, scalar2=rstd,
                        op0=ALU.subtract, op1=ALU.mult),
                    lr_wait=("lr", lr0 + 32),
                    tp_base_fn=lambda i, tpB0=tpB0: ("tp", tpB0 + i * NFT)
                    if (i > 0 or tpB0 > 0) else (None, 0))

                with ExitStack() as ph:
                    ptr = [ph.enter_context(nc.psum_tensor(f"ptA{gsl}_{j}",
                                                           [128, 128], BF16))
                           for j in range(2)]
                    items = [(i, ft) for i in range(2) for ft in range(NFT)]
                    pmprev = S.n.get("pm", 0)
                    emit_transposes(
                        block, S, items,
                        src_fn=lambda it: ynt[:, it[1] * 128:(it[1] + 1) * 128],
                        dst_fn=lambda it: yT[:, it[1] * HSEQ +
                                             (sl * 2 + it[0]) * 128:
                                             it[1] * HSEQ +
                                             (sl * 2 + it[0] + 1) * 128],
                        ident=ident[:], ptr=ptr,
                        pe_pre=(lambda t, pmprev=pmprev:
                                S.wait(t, "pm", pmprev))
                        if (sl == 0 and hp == 1) else None,
                        src_tick=lambda j, ln0=ln0: ("ln", ln0 + j // NFT + 1))
            yT_done = S.n["tev"]

            with ExitStack() as ph:
                pp_ = [ph.enter_context(nc.psum_tensor(f"pp{hp}_{j}",
                                                       [128, 512], F32))
                       for j in range(2)]
                ptm = [ph.enter_context(nc.psum_tensor(f"ptm{hp}_{j}",
                                                       [128, 128], BF16))
                       for j in range(2)]
                pba = ph.enter_context(nc.psum_tensor(f"pba{hp}", [16, 512],
                                                      F32))
                for m in range(25):
                    isconv = m < 16
                    isz = 16 <= m < 24
                    isba = m == 24
                    wl0 = S.reserve(f"wl{m % 2}", 16) if m < 24 else 0
                    cw0 = S.reserve("cw", 1)
                    pm0 = S.reserve("pm", T4N)
                    ev0 = S.reserve("evp", T4N)
                    mseq = hp * 25 + m

                    @block.sync
                    def _(sync, m=m, mseq=mseq, hp=hp):
                        if mseq >= 1:
                            S.wait(sync, "cw", mseq)
                        else:
                            S.wait(sync, "cv", cv0 + 1)
                        if m < 24:
                            S.inc(sync.dma_start(
                                out=wmf[:],
                                in_=Wsl.rearrange("(k p) c -> p k c", p=128)
                                    [:, :, m * 128:(m + 1) * 128]),
                                f"wl{m % 2}", 16)

                    @block.vector
                    def _(v, m=m, mseq=mseq, wl0=wl0):
                        if m < 24:
                            S.wait(v, f"wl{m % 2}", wl0 + 16)
                            if mseq >= 2:
                                S.wait(v, "pm", T4N * (mseq - 1))
                            S.inc(v.tensor_copy(
                                out=wmb[:, (m % 2) * 2048:(m % 2 + 1) * 2048],
                                in_=wmf[:]), "cw")
                        else:
                            S.inc(v.memset(gcol[:, 0:1], 0.0), "cw")

                    @block.tensor
                    def _(t, m=m, mseq=mseq, hp=hp, cw0=cw0, ev0=ev0,
                          yT_done=yT_done, isba=isba):
                        S.wait(t, "tev", yT_done)
                        if not isba:
                            S.wait(t, "cw", cw0 + 1)
                        else:
                            S.wait(t, "cv", cv0 + 1)
                        for t4 in range(T4N):
                            if isba:
                                if mseq * T4N + t4 >= 1:
                                    S.wait(t, "evp", T4N * mseq + t4)
                            elif mseq * T4N + t4 >= 2:
                                S.wait(t, "evp", T4N * mseq + t4 - 1)
                            for k in range(NFT):
                                if isba:
                                    inst = t.matmul(
                                        pba[0:16, 0:W4],
                                        lhsT=wbab[:, k * 16:(k + 1) * 16],
                                        rhs=yT[:, k * HSEQ + t4 * W4:
                                               k * HSEQ + (t4 + 1) * W4],
                                        start=(k == 0), stop=(k == NFT - 1))
                                else:
                                    inst = t.matmul(
                                        pp_[t4 % 2][0:128, 0:W4],
                                        lhsT=wmb[:, (m % 2) * 2048 + k * 128:
                                                 (m % 2) * 2048 + (k + 1) * 128],
                                        rhs=yT[:, k * HSEQ + t4 * W4:
                                               k * HSEQ + (t4 + 1) * W4],
                                        start=(k == 0), stop=(k == NFT - 1))
                            S.inc(inst, "pm")

                    if isba:
                        bv0 = S.reserve("bv", 16)

                        @block.scalar
                        def _(s_, pm0=pm0, ev0=ev0, hp=hp):
                            for t4 in range(T4N):
                                S.wait(s_, "pm", pm0 + t4 + 1)
                                if t4 == 0 and hp == 1:
                                    S.wait(s_, "bv", 16)
                                S.inc(s_.copy(out=bstage[:, t4 * W4:
                                                         (t4 + 1) * W4],
                                              in_=pba[0:16, 0:W4]), "evp")
                            S.wait(s_, "evp", ev0 + T4N)
                            S.inc(s_.dma_start(
                                out=bad[:, hp * HSEQ:(hp + 1) * HSEQ],
                                in_=bstage[:]), "bv", 16)
                        continue

                    cvd0 = S.reserve("cvd", 5 if isconv else 0)
                    tpvP = S.n.get("tpv", 0)
                    sil0 = S.reserve("sil", 1) if isconv else None
                    tpv0 = S.reserve("tpv", HCH)
                    tpe0 = S.reserve("tpe", HCH)
                    md0 = [S.reserve(f"md{sl_}", 16 * (HCH // 2))
                           for sl_ in range(2)]

                    @block.scalar
                    def _(s_, m=m, mseq=mseq, hp=hp, pm0=pm0, ev0=ev0,
                          cvd0=cvd0, isconv=isconv, tpvP=tpvP):
                        for t4 in range(T4N):
                            S.wait(s_, "pm", pm0 + t4 + 1)
                            if isconv:
                                if t4 == 0 and cvd0 > 0:
                                    S.wait(s_, "cvd", cvd0)
                                S.inc(s_.copy(
                                    out=cbuf[:, 3 + t4 * W4:3 + (t4 + 1) * W4],
                                    in_=pp_[t4 % 2][0:128, 0:W4]), "evp")
                            else:
                                if t4 == 0 and tpvP > 0:
                                    S.wait(s_, "tpv", tpvP)
                                S.inc(s_.activation(
                                    out=svt[:, t4 * W4:(t4 + 1) * W4],
                                    in_=pp_[t4 % 2][0:128, 0:W4],
                                    func=AF.Silu), "evp")

                    if isconv:
                        @block.vector
                        def _(v, m=m, mseq=mseq, hp=hp, ev0=ev0, cvd0=cvd0,
                              sil0=sil0):
                            S.wait(v, "evp", ev0 + T4N)
                            if hp == 0:
                                inst = v.memset(cbuf[:, 0:3], 0.0)
                            else:
                                inst = v.tensor_copy(
                                    out=cbuf[:, 0:3],
                                    in_=cmarg[:, m * 3:(m + 1) * 3])
                            vfence(v, S, inst)
                            inst = v.tensor_scalar_mul(
                                out=cvt[:], in0=cbuf[:, 3:3 + HSEQ],
                                scalar1=wcv[:, m * 4 + 3:m * 4 + 4])
                            vfence(v, S, inst)
                            inst = v.scalar_tensor_tensor(
                                out=cvt[:], in0=cbuf[:, 2:2 + HSEQ],
                                scalar=wcv[:, m * 4 + 2:m * 4 + 3],
                                in1=cvt[:], op0=ALU.mult, op1=ALU.add)
                            vfence(v, S, inst)
                            inst = v.scalar_tensor_tensor(
                                out=cvt[:], in0=cbuf[:, 1:1 + HSEQ],
                                scalar=wcv[:, m * 4 + 1:m * 4 + 2],
                                in1=cvt[:], op0=ALU.mult, op1=ALU.add)
                            vfence(v, S, inst)
                            inst = v.scalar_tensor_tensor(
                                out=cvt[:], in0=cbuf[:, 0:0 + HSEQ],
                                scalar=wcv[:, m * 4 + 0:m * 4 + 1],
                                in1=cvt[:], op0=ALU.mult, op1=ALU.add)
                            vfence(v, S, inst)
                            if hp == 0:
                                inst = v.tensor_copy(
                                    out=cmarg[:, m * 3:(m + 1) * 3],
                                    in_=cbuf[:, HSEQ:HSEQ + 3])
                            else:
                                inst = v.tensor_copy(out=gcol[:, 0:1],
                                                     in_=gcol[:, 0:1])
                            S.inc(inst, "cvd", 5)

                        @block.scalar
                        def _(s_, cvd0=cvd0, sil0=sil0, tpvP=tpvP):
                            S.wait(s_, "cvd", cvd0 + 5)
                            if tpvP > 0:
                                S.wait(s_, "tpv", tpvP)
                            S.inc(s_.activation(out=svt[:], in_=cvt[:],
                                                func=AF.Silu), "sil")

                    @block.tensor
                    def _(t, m=m, sil0=sil0, tpv0=tpv0, tpe0=tpe0, ev0=ev0,
                          isconv=isconv):
                        if isconv:
                            S.wait(t, "sil", sil0 + 1)
                        else:
                            S.wait(t, "evp", ev0 + T4N)
                        for j in range(HCH):
                            if j >= 2:
                                S.wait(t, "tpe", tpe0 + j - 1)
                            elif tpe0 > 0:
                                S.wait(t, "tpe", tpe0)
                            S.inc(t.transpose(ptm[j % 2][:],
                                              in_=svt[:, j * 128:(j + 1) * 128],
                                              identity=ident[:]), "tpv")

                    @block.scalar
                    def _(s_, m=m, hp=hp, tpv0=tpv0, tpe0=tpe0, md0=md0):
                        for j in range(HCH):
                            S.wait(s_, "tpv", tpv0 + j + 1)
                            if j >= 2:
                                S.wait(s_, f"md{j % 2}",
                                       md0[j % 2] + 16 * (j // 2))
                            elif md0[j % 2] > 0:
                                S.wait(s_, f"md{j % 2}", md0[j % 2])
                            inst = s_.copy(out=stg[:, (j % 2) * 128:
                                               (j % 2 + 1) * 128],
                                           in_=ptm[j % 2][:])
                            S.inc(inst, "tpe")
                            S.wait(s_, "tpe", tpe0 + j + 1)
                            S.inc(s_.dma_start(
                                out=mixd[hp * HSEQ + j * 128:
                                         hp * HSEQ + (j + 1) * 128,
                                         m * 128:(m + 1) * 128],
                                in_=stg[:, (j % 2) * 128:(j % 2 + 1) * 128]),
                                f"md{j % 2}", 16)

        md_all = [S.n["md0"], S.n["md1"]]
        bv_all = S.n["bv"]
        ln_all = S.n["ln"]
        pm_all4 = S.n["pm"]

        GW = min(seq, 2048)
        NGP = seq // GW

        def emit_gvar(var):
            gld = S.reserve("gld", 16)

            @block.sync
            def _(sync, var=var):
                if var == 0:
                    S.wait(sync, "bv", bv_all)
                    S.wait(sync, "ln", ln_all)
                else:
                    S.wait(sync, "gw", 16)
                S.inc(sync.dma_start(
                    out=xr[0:8, 0:seq],
                    in_=bad[8:16, :] if var == 0 else bad[0:8, :]), "gld", 16)

            gw0 = S.reserve("gw", 16)
            ts0 = S.reserve("gts", NGP)
            ln0_ = S.reserve("gln", NGP)
            st0g = S.reserve("gst", NGP)
            mu0 = S.reserve("gmu", NGP)

            @block.vector
            def _(v, var=var, gld=gld):
                S.wait(v, "gld", gld + 16)
                if var == 0:
                    S.wait(v, "sr", sr0 + 16 * 5)
                    S.wait(v, "cv", cv0 + 3)
                for p in range(NGP):
                    cs = slice(p * GW, (p + 1) * GW)
                    if var == 0:
                        inst = v.tensor_scalar(out=xr[0:8, cs], in0=xr[0:8, cs],
                                               scalar1=acol[0:8, 1:2],
                                               scalar2=acol[0:8, 5:6],
                                               op0=ALU.add, op1=ALU.add)
                    else:
                        inst = v.tensor_scalar_mul(out=xr[0:8, cs],
                                                   in0=xr[0:8, cs],
                                                   scalar1=-1.0)
                    S.inc(inst, "gts")

            @block.scalar
            def _(s_, var=var, ts0=ts0, st0g=st0g):
                for p in range(NGP):
                    cs = slice(p * GW, (p + 1) * GW)
                    S.wait(s_, "gts", ts0 + p + 1)
                    if p > 0:
                        S.wait(s_, "gst", st0g + p)
                    inst = s_.activation(out=lnscr[0:8, 0:GW], in_=xr[0:8, cs],
                                         func=AF.Abs)
                    vfence(s_, S, inst)
                    inst = s_.activation(out=lnscr[0:8, 0:GW],
                                         in_=lnscr[0:8, 0:GW],
                                         func=AF.Exp, scale=-1.0)
                    vfence(s_, S, inst)
                    S.inc(s_.activation(out=lnscr[0:8, 0:GW],
                                        in_=lnscr[0:8, 0:GW],
                                        func=AF.Ln, bias=acol[0:8, 2:3]),
                          "gln")

            @block.vector
            def _(v, var=var, ln0_=ln0_, st0g=st0g):
                if var == 0:
                    S.wait(v, "cv", cv0 + 5)
                for p in range(NGP):
                    cs = slice(p * GW, (p + 1) * GW)
                    S.wait(v, "gln", ln0_ + p + 1)
                    S.inc(v.scalar_tensor_tensor(out=xr[0:8, cs],
                                                 in0=xr[0:8, cs], scalar=0.0,
                                                 in1=lnscr[0:8, 0:GW],
                                                 op0=ALU.max, op1=ALU.add),
                          "gst")
                    S.wait(v, "gst", st0g + p + 1)
                    if var == 0:
                        inst = v.tensor_scalar_mul(out=xr[0:8, cs],
                                                   in0=xr[0:8, cs],
                                                   scalar1=acol[0:8, 4:5])
                    else:
                        inst = v.tensor_scalar_mul(out=xr[0:8, cs],
                                                   in0=xr[0:8, cs],
                                                   scalar1=-1.0)
                    S.inc(inst, "gmu")

            @block.scalar
            def _(s_, var=var, mu0=mu0, gw0=gw0):
                S.wait(s_, "gmu", mu0 + NGP)
                S.inc(s_.dma_start(out=gd if var == 0 else lnbd,
                                   in_=xr[0:8, 0:seq]), "gw", 16)
            return gw0

        gw_g = emit_gvar(0)
        gw_b = emit_gvar(1)

        NB = 4 * NCH
        gq0 = S.reserve("gq", 16 * 8)
        lq0 = S.reserve("lq", 16 * 8)
        gs0 = S.reserve("gsc", 4)
        gt0 = S.reserve("gtp", 7)
        ge0 = S.reserve("gev", 15)

        @block.sync
        def _(sync):
            S.wait(sync, "gw", gw_g + 16)
            for blk in range(2):
                for hrel in range(4):
                    S.inc(sync.dma_start(
                        out=gcng[hrel * NCH:(hrel + 1) * NCH,
                                 blk * 128:blk * 128 + 128],
                        in_=gd[blk * 4 + hrel].rearrange("(c t) -> c t", t=128)),
                        "gq", 16)

        @block.vector
        def _(v):
            S.wait(v, "gq", gq0 + 16 * 8)
            for blk in range(2):
                bs = slice(blk * 128, blk * 128 + 128)
                inst = v.tensor_tensor_scan(
                    out=gc[0:NB, bs], data0=gcng[0:NB, bs],
                    data1=acol[0:NB, 5:6].broadcast_to([NB, 128]),
                    initial=0.0, op0=ALU.add, op1=ALU.add)
                vfence(v, S, inst)
                S.inc(v.tensor_scalar(
                    out=dmt_s[0:NB, bs], in0=gc[0:NB, bs],
                    scalar1=gc[0:NB, blk * 128 + 127:blk * 128 + 128],
                    scalar2=0.0, op0=ALU.subtract, op1=ALU.add), "gsc", 2)

        @block.sync
        def _(sync):
            S.wait(sync, "gsc", gs0 + 4)
            S.wait(sync, "gw", gw_b + 16)
            for blk in range(2):
                for hrel in range(4):
                    S.inc(sync.dma_start(
                        out=gcng[hrel * NCH:(hrel + 1) * NCH,
                                 blk * 128:blk * 128 + 128],
                        in_=lnbd[blk * 4 + hrel].rearrange("(c t) -> c t",
                                                           t=128)), "lq", 16)

        with ExitStack() as ph:
            pg = [ph.enter_context(nc.psum_tensor(f"pg{j}", [128, 128], F32))
                  for j in range(2)]

            @block.tensor
            def _(t):
                S.wait(t, "msk", 3)
                S.wait(t, "gsc", gs0 + 4)
                S.wait(t, "lq", lq0 + 16 * 8)
                srcs = [gc[0:NB, 0:128], gc[0:NB, 128:256],
                        gcng[0:NB, 0:128], gcng[0:NB, 128:256],
                        dmt_s[0:NB, 0:128], dmt_s[0:NB, 128:256]]
                for j, sp in enumerate(srcs):
                    if j >= 2:
                        S.wait(t, "gev", ge0 + 2 * (j - 1))
                    S.inc(t.transpose(pg[j % 2][0:128, 0:NB], in_=sp,
                                      identity=identf[0:NB, 0:NB]), "gtp")
                S.wait(t, "sr", sr0 + 16 * 5)
                S.wait(t, "cv", cv0 + 5)
                S.wait(t, "gev", ge0 + 10)
                S.inc(t.matmul(pg[0][:], lhsT=onesr[:], rhs=onerow[0:1, :],
                               start=True, stop=True), "gtp")

            @block.scalar
            def _(s_):
                for j in range(6):
                    S.wait(s_, "gtp", gt0 + j + 1)
                    blk = j % 2
                    bs = slice(blk * 128, blk * 128 + NB)
                    p_ = pg[j % 2][0:128, 0:NB]
                    if j < 2:
                        S.inc(s_.copy(out=gcT[:, bs], in_=p_), "gev")
                        S.inc(s_.activation(out=egc[:, bs], in_=p_,
                                            func=AF.Exp), "gev")
                    elif j < 4:
                        S.inc(s_.copy(out=lnbT[:, bs], in_=p_), "gev")
                        S.inc(s_.activation(out=btc[:, bs], in_=p_,
                                            func=AF.Exp), "gev")
                    else:
                        S.inc(s_.activation(out=eglc[:, bs], in_=p_,
                                            func=AF.Exp, scale=-1.0), "gev", 2)
                S.wait(s_, "gtp", gt0 + 7)
                S.inc(s_.copy(out=nwrep[:], in_=pg[0][:]), "gev")

            gv20 = S.reserve("gv2", 4)

            @block.vector
            def _(v):
                S.wait(v, "gev", ge0 + 8)
                for blk in range(2):
                    bs = slice(blk * 128, blk * 128 + NB)
                    S.inc(v.tensor_tensor(out=glb[:, bs], in0=gcT[:, bs],
                                          in1=lnbT[:, bs], op=ALU.add), "gv2")

            @block.scalar
            def _(s_, gv20=gv20):
                for blk in range(2):
                    bs = slice(blk * 128, blk * 128 + NB)
                    S.wait(s_, "gv2", gv20 + blk + 1)
                    S.inc(s_.activation(out=ebgc[:, bs], in_=glb[:, bs],
                                        func=AF.Exp), "gev")

            @block.vector
            def _(v, gv20=gv20):
                S.wait(v, "gev", ge0 + 12)
                for blk in range(2):
                    bs = slice(blk * 128, blk * 128 + NB)
                    S.inc(v.tensor_tensor(out=eglr[:, bs], in0=egc[:, bs],
                                          in1=eglc[:, bs], op=ALU.mult),
                          "gv2")

        gev_all = S.n["gev"]
        gv2_all = S.n["gv2"]
        tev_all = S.n["tev"]
        DKS = float(128.0 ** -0.5)

        def ccol(buf, vh, c):
            b = vh // 4
            hc = (vh % 4) * NCH + c
            return buf[:, b * 128 + hc:b * 128 + hc + 1]

        if True:
            for si in range(4):
                hps = ExitStack()
                pF = [hps.enter_context(nc.psum_tensor(f"pF{si}_{j}",
                                                       [128, 128], BF16))
                      for j in range(2)]
                ho_prev = S.n.get("ho0", 0)
                hq0 = S.reserve("hq", 16 * 2 * NCH)

                @block.sync
                def _(sync, si=si, ho_prev=ho_prev):
                    if si == 0:
                        S.wait(sync, "pm", pm_all4)
                        S.wait(sync, "tev", tev_all)
                        S.wait(sync, "md0", md_all[0])
                        S.wait(sync, "md1", md_all[1])
                    else:
                        S.wait(sync, "ho0", 16 * NCH * si)
                        S.wait(sync, "ho1", 16 * NCH * si)
                    for c in range(NCH):
                        S.inc(sync.dma_start(
                            out=qtm[:, c * 128:(c + 1) * 128],
                            in_=mixd[c * 128:(c + 1) * 128,
                                     si * 128:(si + 1) * 128]), "hq", 16)
                        S.inc(sync.dma_start(
                            out=ktm[:, c * 128:(c + 1) * 128],
                            in_=mixd[c * 128:(c + 1) * 128,
                                     512 + si * 128:512 + (si + 1) * 128]),
                            "hq", 16)

                nr0 = S.reserve("nrm", 6 * NCH)
                ft0 = S.reserve("ftp", 2 * NCH)
                fe0 = S.reserve("fte", 2 * NCH)

                @block.scalar
                def _(s_, hq0=hq0, nr0=nr0):
                    S.wait(s_, "hq", hq0 + 16 * 2 * NCH)
                    for c in range(NCH):
                        for iq in range(2):
                            src = qtm if iq == 0 else ktm
                            c0 = iq * 3
                            if c > 0 or iq > 0:
                                S.wait(s_, "nrm", nr0 + 6 * c + 3 * iq)
                            inst = s_.activation(
                                out=lnscr[:, 0:128],
                                in_=src[:, c * 128:(c + 1) * 128],
                                func=AF.Square, accum_out=gcol[:, c0:c0 + 1])
                            S.inc(inst, "nrm")
                            S.wait(s_, "nrm", nr0 + 6 * c + 3 * iq + 1)
                            S.inc(s_.activation(out=gcol[:, c0 + 1:c0 + 2],
                                                in_=gcol[:, c0:c0 + 1],
                                                func=AF.Sqrt), "nrm")

                @block.vector
                def _(v, nr0=nr0):
                    for c in range(NCH):
                        for iq in range(2):
                            src = qtm if iq == 0 else ktm
                            c0 = iq * 3
                            S.wait(v, "nrm", nr0 + 6 * c + 3 * iq + 2)
                            inst = v.tensor_scalar(
                                out=gcol[:, c0 + 1:c0 + 2],
                                in0=gcol[:, c0 + 1:c0 + 2],
                                scalar1=1e-6, scalar2=0.0,
                                op0=ALU.add, op1=ALU.add)
                            vfence(v, S, inst)
                            inst = v.reciprocal(out=gcol[:, c0 + 2:c0 + 3],
                                                in_=gcol[:, c0 + 1:c0 + 2])
                            vfence(v, S, inst)
                            if iq == 0:
                                inst = v.tensor_scalar_mul(
                                    out=gcol[:, c0 + 2:c0 + 3],
                                    in0=gcol[:, c0 + 2:c0 + 3], scalar1=DKS)
                                vfence(v, S, inst)
                            S.inc(v.tensor_scalar_mul(
                                out=src[:, c * 128:(c + 1) * 128],
                                in0=src[:, c * 128:(c + 1) * 128],
                                scalar1=gcol[:, c0 + 2:c0 + 3]), "nrm")

                @block.tensor
                def _(t, nr0=nr0, ft0=ft0, fe0=fe0):
                    for c in range(NCH):
                        for iq in range(2):
                            src = qtm if iq == 0 else ktm
                            S.wait(t, "nrm", nr0 + 6 * c + 3 * iq + 3)
                            if 2 * c + iq >= 2:
                                S.wait(t, "fte", fe0 + 2 * c + iq - 1)
                            S.inc(t.transpose(
                                pF[iq][:],
                                in_=src[:, c * 128:(c + 1) * 128],
                                identity=ident[:]), "ftp")

                @block.scalar
                def _(s_, ft0=ft0, fe0=fe0):
                    for c in range(NCH):
                        for iq in range(2):
                            dst = qnfm if iq == 0 else knfm
                            S.wait(s_, "ftp", ft0 + 2 * c + iq + 1)
                            S.inc(s_.copy(out=dst[:, c * 128:(c + 1) * 128],
                                          in_=pF[iq][:]),
                                  "fte")

                vl0 = [[S.reserve(f"vl{j}{sl_}", 16 * ((NCH + 1 - sl_) // 2))
                        for sl_ in range(2)] for j in range(2)]
                vb0 = [S.reserve(f"vbu{j}", NCH) for j in range(2)]
                mb0 = S.reserve("mbc", 4 * NCH)
                mt0 = S.reserve("mtp", 4 * NCH)
                mr0_ = S.reserve("mrd", 4 * NCH)
                me0 = S.reserve("mex", 4 * NCH)
                kd0 = S.reserve("kda", 2 * NCH)
                ad0 = S.reserve("ada", 2 * NCH)
                cw0_ = [S.reserve(f"cw{j}", 19 * NCH) for j in range(2)]
                cu0_ = [S.reserve(f"cu{j}", 19 * NCH) for j in range(2)]
                qd0 = [S.reserve(f"qdd{j}", NCH) for j in range(2)]
                pv0_ = S.reserve("pvw", 4 * NCH)
                pu0_ = S.reserve("pvu", 4 * NCH)
                kk0 = S.reserve("kkm", 2 * NCH)

                @block.sync
                def _(sync, si=si, vl0=vl0, vb0=vb0):
                    for c in range(NCH):
                        for j in range(2):
                            if c >= 2:
                                S.wait(sync, f"vbu{j}", vb0[j] + c - 1)
                            S.inc(sync.dma_start(
                                out=vstr[:, (2 * j + c % 2) * 128:
                                         (2 * j + c % 2 + 1) * 128],
                                in_=mixd[c * 128:(c + 1) * 128,
                                         1024 + (2 * si + j) * 128:
                                         1024 + (2 * si + j + 1) * 128]),
                                f"vl{j}{c % 2}", 16)

                hps.close()
                hps = ExitStack()
                pA = hps.enter_context(nc.psum_tensor(f"pA{si}",
                                                      [128, 128], F32))
                pM = hps.enter_context(nc.psum_tensor(f"pM{si}",
                                                      [128, 128], F32))
                pV = hps.enter_context(nc.psum_tensor(f"pV{si}",
                                                      [128, 128], F32))
                pT = [hps.enter_context(nc.psum_tensor(f"pT{si}_{j}",
                                                       [128, 128], BF16))
                      for j in range(2)]
                pTm = [hps.enter_context(nc.psum_tensor(f"pTm{si}_{j}",
                                                        [128, 128], F32))
                       for j in range(2)]

                @block.tensor
                def _(t, si=si, nr0=nr0, fe0=fe0, mb0=mb0, mt0=mt0,
                      mr0_=mr0_, me0=me0, kd0=kd0, ad0=ad0, cw0_=cw0_,
                      cu0_=cu0_, qd0=qd0, pv0_=pv0_, pu0_=pu0_, kk0=kk0,
                      vl0=vl0):
                    if si == 0:
                        S.wait(t, "gev", gev_all)
                        S.wait(t, "gv2", gv2_all)
                        S.wait(t, "msk", 3)
                    for c in range(NCH):
                        if c == 0:
                            S.wait(t, "fte", fe0 + 2 * NCH)
                        else:
                            S.wait(t, "fte", fe0 + 2 * c + 2)
                        S.wait(t, "ada", ad0 + 2 * c)
                        S.inc(t.matmul(pA[:],
                                       lhsT=knfm[:, c * 128:(c + 1) * 128],
                                       rhs=knfm[:, c * 128:(c + 1) * 128],
                                       start=True, stop=True), "kkm")
                        for j in range(2):
                            S.wait(t, "mbc", mb0 + 4 * c + j + 1)
                            S.wait(t, "mrd", mr0_ + 4 * c + j)
                            S.inc(t.transpose(pM[:],
                                              in_=dexp[:, j * 128:(j + 1) * 128],
                                              identity=identf[:]), "mtp")
                        for step in range(19):
                            for j in range(2):
                                cwb = cw0_[j] + 19 * c
                                cub = cu0_[j] + 19 * c
                                if step > 0:
                                    S.wait(t, f"cu{j}", cub + step)
                                elif 19 * c > 0:
                                    S.wait(t, f"cu{j}", cu0_[j] + 19 * c)
                                if step == 0:
                                    S.wait(t, "kda", kd0 + 2 * c + j + 1)
                                    inst = t.transpose(pT[j][:],
                                                       in_=At_s[:, j * 128:
                                                                (j + 1) * 128],
                                                       identity=ident[:])
                                else:
                                    jj_ = (step - 1) // 3
                                    ph_ = (step - 1) % 3
                                    if ph_ == 0:
                                        lsq = Ap_s if jj_ == 0 else App_s
                                        rsq = At_s if jj_ == 0 else Atp_s
                                        inst = t.matmul(
                                            pTm[j][:],
                                            lhsT=lsq[:, j * 128:(j + 1) * 128],
                                            rhs=rsq[:, j * 128:(j + 1) * 128],
                                            start=True, stop=True)
                                    elif ph_ == 1:
                                        inst = t.transpose(
                                            pT[j][:],
                                            in_=Atp_s[:, j * 128:(j + 1) * 128],
                                            identity=ident[:])
                                    else:
                                        if jj_ == 0:
                                            S.wait(t, f"qdd{j}",
                                                   qd0[j] + c + 1)
                                        inst = t.matmul(
                                            pTm[j][:],
                                            lhsT=App_s[:, j * 128:(j + 1) * 128],
                                            rhs=Qp_s[:, (2 * j + jj_ % 2) * 128:
                                                     (2 * j + jj_ % 2 + 1) * 128],
                                            start=True, stop=True)
                                S.inc(inst, f"cw{j}")
                        S.wait(t, "kda", kd0 + 2 * c + 2)
                        S.inc(t.matmul(pA[:],
                                       lhsT=knfm[:, c * 128:(c + 1) * 128],
                                       rhs=qnfm[:, c * 128:(c + 1) * 128],
                                       start=True, stop=True), "kkm")
                        for j in range(2):
                            S.wait(t, "mbc", mb0 + 4 * c + 2 + j + 1)
                            S.wait(t, "mrd", mr0_ + 4 * c + 2 + j)
                            S.inc(t.transpose(pM[:],
                                              in_=dexp[:, j * 128:(j + 1) * 128],
                                              identity=identf[:]), "mtp")
                        for j in range(2):
                            S.wait(t, f"cu{j}", cu0_[j] + 19 * (c + 1))
                            S.wait(t, "vbu" + str(j), vb0[j] + c + 1)
                            if 4 * c + 2 * j >= 1:
                                S.wait(t, "pvu", pu0_ + 4 * c + 2 * j)
                            S.inc(t.matmul(
                                pV[:],
                                lhsT=Qp_s[:, (2 * j + 0) * 128:
                                          (2 * j + 1) * 128],
                                rhs=vbt[:, j * 128:(j + 1) * 128],
                                start=True, stop=True), "pvw")
                            S.wait(t, "pvu", pu0_ + 4 * c + 2 * j + 1)
                            S.inc(t.matmul(
                                pV[:],
                                lhsT=kbgt[:, j * 128:(j + 1) * 128],
                                rhs=Qp_s[:, (2 * j + 0) * 128:
                                         (2 * j + 1) * 128],
                                start=True, stop=True), "pvw")

                @block.scalar
                def _(s_, si=si, mt0=mt0, mr0_=mr0_, me0=me0, cw0_=cw0_,
                      cu0_=cu0_, pv0_=pv0_, pu0_=pu0_):
                    for c in range(NCH):
                        for j in range(2):
                            S.wait(s_, "mrd", mr0_ + 4 * c + j + 1)
                            S.inc(s_.activation(out=dexp[:, j * 128:(j + 1) * 128],
                                                in_=dexp[:, j * 128:(j + 1) * 128],
                                                func=AF.Exp), "mex")
                        for step in range(19):
                            for j in range(2):
                                cwb = cw0_[j] + 19 * c
                                if step == 0:
                                    dst = Ap_s[:, j * 128:(j + 1) * 128]
                                    src_p = pT[j][:]
                                elif (step - 1) % 3 == 0:
                                    dst = Atp_s[:, j * 128:(j + 1) * 128]
                                    src_p = pTm[j][:]
                                elif (step - 1) % 3 == 1:
                                    dst = App_s[:, j * 128:(j + 1) * 128]
                                    src_p = pT[j][:]
                                else:
                                    continue
                                S.wait(s_, f"cw{j}", cwb + step + 1)
                                S.inc(s_.copy(out=dst, in_=src_p),
                                      f"cu{j}")
                        for j in range(2):
                            S.wait(s_, "mrd", mr0_ + 4 * c + 2 + j + 1)
                            S.inc(s_.activation(out=dexp[:, j * 128:(j + 1) * 128],
                                                in_=dexp[:, j * 128:(j + 1) * 128],
                                                func=AF.Exp), "mex")
                        for j in range(2):
                            S.wait(s_, "pvw", pv0_ + 4 * c + 2 * j + 1)
                            S.inc(s_.copy(
                                out=valab[j][:, c * 128:(c + 1) * 128],
                                in_=pV[:]), "pvu")
                            S.wait(s_, "pvw", pv0_ + 4 * c + 2 * j + 2)
                            S.inc(s_.copy(
                                out=kcdab[j][:, c * 128:(c + 1) * 128],
                                in_=pV[:]), "pvu")

                @block.vector
                def _(v, si=si, mb0=mb0, mt0=mt0, mr0_=mr0_, me0=me0,
                      kd0=kd0, ad0=ad0, cw0_=cw0_, cu0_=cu0_, qd0=qd0,
                      kk0=kk0, vl0=vl0, pv0_=pv0_):
                    if si == 0:
                        S.wait(v, "msk", 3)
                        S.wait(v, "gev", gev_all)
                        S.wait(v, "gv2", gv2_all)
                    for c in range(NCH):
                        for j in range(2):
                            vh = 2 * si + j
                            S.wait(v, f"vl{j}{c % 2}",
                                   vl0[j][c % 2] + 16 * (c // 2 + 1))
                            if c >= 1:
                                S.wait(v, "pvw", pv0_ + 4 * (c - 1) + 2 * j + 2)
                            v.tensor_scalar_mul(
                                out=vbt[:, j * 128:(j + 1) * 128],
                                in0=vstr[:, (2 * j + c % 2) * 128:
                                         (2 * j + c % 2 + 1) * 128],
                                scalar1=ccol(btc, vh, c))
                            S.inc(v.tensor_scalar_mul(
                                out=kbgt[:, j * 128:(j + 1) * 128],
                                in0=ktm[:, c * 128:(c + 1) * 128],
                                scalar1=ccol(ebgc, vh, c)), f"vbu{j}")
                            if c > 0:
                                S.wait(v, "mtp", mt0 + 4 * (c - 1) + 2 + j + 1)
                            S.inc(v.tensor_copy(
                                out=dexp[:, j * 128:(j + 1) * 128],
                                in_=ccol(glb, vh, c).broadcast_to([128, 128])),
                                "mbc")
                        for j in range(2):
                            vh = 2 * si + j
                            S.wait(v, "mtp", mt0 + 4 * c + j + 1)
                            inst = v.tensor_tensor(
                                out=dexp[:, j * 128:(j + 1) * 128],
                                in0=pM[:],
                                in1=ccol(gcT, vh, c).broadcast_to([128, 128]),
                                op=ALU.subtract)
                            vfence(v, S, inst)
                            S.inc(v.tensor_scalar(
                                out=dexp[:, j * 128:(j + 1) * 128],
                                in0=dexp[:, j * 128:(j + 1) * 128],
                                scalar1=0.0, scalar2=0.0,
                                op0=ALU.min, op1=ALU.add), "mrd")
                            S.wait(v, "mex", me0 + 4 * c + j + 1)
                            inst = v.tensor_tensor(
                                out=dmt_s[:, j * 128:(j + 1) * 128],
                                in0=dexp[:, j * 128:(j + 1) * 128],
                                in1=mstr[:], op=ALU.mult)
                            vfence(v, S, inst)
                            S.wait(v, "kkm", kk0 + 2 * c + 1)
                            S.inc(v.tensor_tensor(
                                out=At_s[:, j * 128:(j + 1) * 128],
                                in0=pA[:], in1=dmt_s[:, j * 128:(j + 1) * 128],
                                op=ALU.mult), "kda")
                            S.wait(v, "kda", kd0 + 2 * c + j + 1)
                            if c >= 1:
                                S.wait(v, "pvw", pv0_ + 4 * (c - 1) + 2 * j + 2)
                            S.inc(v.tensor_tensor(
                                out=Qp_s[:, (2 * j) * 128:(2 * j + 1) * 128],
                                in0=At_s[:, j * 128:(j + 1) * 128],
                                in1=ident[:], op=ALU.add), f"qdd{j}")
                        for jj_ in range(1, 7):
                            for j in range(2):
                                cwb = cw0_[j] + 19 * c
                                S.wait(v, f"cw{j}", cwb + 1 + (jj_ - 1) * 3 + 3)
                                S.inc(v.tensor_tensor(
                                    out=Qp_s[:, (2 * j + jj_ % 2) * 128:
                                             (2 * j + jj_ % 2 + 1) * 128],
                                    in0=Qp_s[:, (2 * j + (jj_ - 1) % 2) * 128:
                                             (2 * j + (jj_ - 1) % 2 + 1) * 128],
                                    in1=pTm[j][:], op=ALU.add), f"cu{j}")
                        for j in range(2):
                            vh = 2 * si + j
                            S.wait(v, "mtp", mt0 + 4 * c + j + 1)
                            S.inc(v.tensor_copy(
                                out=dexp[:, j * 128:(j + 1) * 128],
                                in_=ccol(gcT, vh, c).broadcast_to([128, 128])),
                                "mbc")
                        for j in range(2):
                            vh = 2 * si + j
                            S.wait(v, "mtp", mt0 + 4 * c + 2 + j + 1)
                            inst = v.tensor_tensor(
                                out=dexp[:, j * 128:(j + 1) * 128],
                                in0=pM[:],
                                in1=ccol(gcT, vh, c).broadcast_to([128, 128]),
                                op=ALU.subtract)
                            vfence(v, S, inst)
                            S.inc(v.tensor_scalar(
                                out=dexp[:, j * 128:(j + 1) * 128],
                                in0=dexp[:, j * 128:(j + 1) * 128],
                                scalar1=0.0, scalar2=0.0,
                                op0=ALU.min, op1=ALU.add), "mrd")
                            S.wait(v, "mex", me0 + 4 * c + 2 + j + 1)
                            S.wait(v, "kkm", kk0 + 2 * c + 2)
                            inst = v.tensor_tensor(
                                out=dmt_s[:, j * 128:(j + 1) * 128],
                                in0=pA[:],
                                in1=dexp[:, j * 128:(j + 1) * 128],
                                op=ALU.mult)
                            vfence(v, S, inst)
                            S.inc(v.tensor_tensor(
                                out=attnab[j][:, c * 128:(c + 1) * 128],
                                in0=dmt_s[:, j * 128:(j + 1) * 128],
                                in1=mupi[:], op=ALU.mult), "ada")

                hps.close()
                hps = ExitStack()
                pA = hps.enter_context(nc.psum_tensor(f"pA2{si}",
                                                      [128, 128], F32))
                pM = hps.enter_context(nc.psum_tensor(f"pM2{si}",
                                                      [128, 128], F32))
                pV = hps.enter_context(nc.psum_tensor(f"pV2{si}",
                                                      [128, 128], F32))
                pO = hps.enter_context(nc.psum_tensor(f"pO{si}",
                                                      [128, 128], F32))
                pS_ = hps.enter_context(nc.psum_tensor(f"pS{si}",
                                                       [128, 128], F32))
                pQG = hps.enter_context(nc.psum_tensor(f"pQG{si}",
                                                       [128, 128], BF16))
                zl0 = [[S.reserve(f"zl{j}{sl_}", 16 * ((NCH + 1 - sl_) // 2))
                        for sl_ in range(2)] for j in range(2)]
                zu0 = [S.reserve(f"zsu{j}", NCH) for j in range(2)]
                sb0 = [S.reserve(f"sbf{j}", NCH + 1) for j in range(2)]
                kcd0 = [S.reserve(f"skd{j}", NCH) for j in range(2)]
                vn0 = [S.reserve(f"svn{j}", NCH) for j in range(2)]
                ou0 = [S.reserve(f"sou{j}", 2 * NCH) for j in range(2)]
                st0_ = [S.reserve(f"sst{j}", NCH) for j in range(2)]
                sf0 = S.reserve("sfu", 2 * NCH)
                qt0 = S.reserve("qtp", 2 * NCH)
                qe0_ = S.reserve("qte", 2 * NCH)
                qg0 = [S.reserve(f"qgw{j}", NCH) for j in range(2)]
                sg0 = S.reserve("sga", 2 * NCH)
                sv0 = S.reserve("sgv", 2 * NCH)
                gr0 = [S.reserve(f"sgr{j}", NCH) for j in range(2)]
                gh0 = [S.reserve(f"sgh{j}", NCH) for j in range(2)]
                hz0 = [S.reserve(f"szs{j}", NCH) for j in range(2)]
                ho0_ = [S.reserve(f"ho{j}", 16 * NCH) for j in range(2)]
                hv0 = [S.reserve(f"shv{j}", NCH) for j in range(2)]

                @block.sync
                def _(sync, si=si, zl0=zl0, zu0=zu0):
                    for c in range(NCH):
                        for j in range(2):
                            if c >= 2:
                                S.wait(sync, f"zsu{j}", zu0[j] + c - 1)
                            S.inc(sync.dma_start(
                                out=zstr[:, (2 * j + c % 2) * 128:
                                         (2 * j + c % 2 + 1) * 128],
                                in_=mixd[c * 128:(c + 1) * 128,
                                         2048 + (2 * si + j) * 128:
                                         2048 + (2 * si + j + 1) * 128]),
                                f"zl{j}{c % 2}", 16)

                @block.vector
                def _(v, si=si, sb0=sb0, kcd0=kcd0, ou0=ou0):
                    for j in range(2):
                        if si > 0:
                            S.wait(v, f"skd{j}", kcd0[j])
                            S.wait(v, f"sou{j}", ou0[j])
                        v.memset(Sf[:, j * 128:(j + 1) * 128], 0.0)
                        S.inc(v.memset(Sb[:, j * 128:(j + 1) * 128], 0.0),
                              f"sbf{j}")

                @block.tensor
                def _(t, si=si, sb0=sb0, kcd0=kcd0, vn0=vn0, ou0=ou0,
                      st0_=st0_, sf0=sf0, qt0=qt0, qe0_=qe0_, qg0=qg0,
                      gh0=gh0, pu0_=pu0_, ad0=ad0):
                    for c in range(NCH):
                        for j in range(2):
                            pk = pA if j == 0 else pM
                            S.wait(t, f"sbf{j}", sb0[j] + c + 1)
                            if c == 0 and j == 0:
                                S.wait(t, "pvu", pu0_ + 4 * NCH)
                                S.wait(t, "ada", ad0 + 2 * NCH)
                            if c >= 1:
                                S.wait(t, f"svn{j}", vn0[j] + c)
                            S.inc(t.matmul(
                                pk[:],
                                lhsT=kcdab[j][:, c * 128:(c + 1) * 128],
                                rhs=Sb[:, j * 128:(j + 1) * 128],
                                start=True, stop=True), f"skd{j}")
                        for j in range(2):
                            S.wait(t, f"qgw{j}", qg0[j] + c + 1)
                            if 2 * c + j >= 1:
                                S.wait(t, "qte", qe0_ + 2 * c + j)
                            S.inc(t.transpose(
                                pQG[:],
                                in_=qg_s[:, (2 * j + c % 2) * 128:
                                         (2 * j + c % 2 + 1) * 128],
                                identity=ident[:]), "qtp")
                        for j in range(2):
                            po = pV if j == 0 else pO
                            S.wait(t, "qte", qe0_ + 2 * c + j + 1)
                            if c >= 1:
                                S.wait(t, f"sgh{j}", gh0[j] + c)
                            S.inc(t.matmul(
                                po[:],
                                lhsT=qgT_s[:, (2 * j + c % 2) * 128:
                                           (2 * j + c % 2 + 1) * 128],
                                rhs=Sb[:, j * 128:(j + 1) * 128],
                                start=True, stop=False), f"sou{j}")
                            S.wait(t, f"svn{j}", vn0[j] + c + 1)
                            S.inc(t.matmul(
                                po[:],
                                lhsT=attnab[j][:, c * 128:(c + 1) * 128],
                                rhs=vnew[:, j * 128:(j + 1) * 128],
                                start=False, stop=True), f"sou{j}")
                        for j in range(2):
                            S.wait(t, "sfu", sf0 + 2 * c + j)
                            S.inc(t.matmul(
                                pS_[:],
                                lhsT=kg2_s[:, (2 * j + c % 2) * 128:
                                           (2 * j + c % 2 + 1) * 128],
                                rhs=vnew[:, j * 128:(j + 1) * 128],
                                start=True, stop=True), f"sst{j}")

                @block.vector
                def _(v, si=si, kcd0=kcd0, vn0=vn0, st0_=st0_, sf0=sf0,
                      qt0=qt0, qg0=qg0, sg0=sg0, sv0=sv0, gr0=gr0, gh0=gh0,
                      hz0=hz0, ho0_=ho0_, hv0=hv0, pu0_=pu0_):
                    for c in range(NCH):
                        for j in range(2):
                            vh = 2 * si + j
                            if c >= 2:
                                S.wait(v, "qtp", qt0 + 2 * (c - 2) + j + 1)
                                S.wait(v, f"sst{j}", st0_[j] + c - 1)
                            v.tensor_scalar_mul(
                                out=qg_s[:, (2 * j + c % 2) * 128:
                                         (2 * j + c % 2 + 1) * 128],
                                in0=qtm[:, c * 128:(c + 1) * 128],
                                scalar1=ccol(egc, vh, c))
                            S.inc(v.tensor_scalar_mul(
                                out=kg2_s[:, (2 * j + c % 2) * 128:
                                         (2 * j + c % 2 + 1) * 128],
                                in0=ktm[:, c * 128:(c + 1) * 128],
                                scalar1=ccol(eglc, vh, c)), f"qgw{j}")
                        for j in range(2):
                            pk = pA if j == 0 else pM
                            S.wait(v, f"skd{j}", kcd0[j] + c + 1)
                            if c >= 1:
                                S.wait(v, f"sst{j}", st0_[j] + c)
                            S.inc(v.tensor_tensor(
                                out=vnew[:, j * 128:(j + 1) * 128],
                                in0=valab[j][:, c * 128:(c + 1) * 128],
                                in1=pk[:], op=ALU.subtract), f"svn{j}")
                        for j in range(2):
                            vh = 2 * si + j
                            S.wait(v, f"sst{j}", st0_[j] + c + 1)
                            inst = v.tensor_scalar_mul(
                                out=dmt_s[:, j * 128:(j + 1) * 128],
                                in0=Sf[:, j * 128:(j + 1) * 128],
                                scalar1=ccol(eglr, vh, c))
                            vfence(v, S, inst)
                            S.inc(v.tensor_tensor(
                                out=Sf[:, j * 128:(j + 1) * 128],
                                in0=dmt_s[:, j * 128:(j + 1) * 128],
                                in1=pS_[:], op=ALU.add), "sfu")
                            S.wait(v, "sfu", sf0 + 2 * c + j + 1)
                            S.inc(v.tensor_copy(
                                out=Sb[:, j * 128:(j + 1) * 128],
                                in_=Sf[:, j * 128:(j + 1) * 128]), f"sbf{j}")
                        for j in range(2):
                            po = pV if j == 0 else pO
                            S.wait(v, "sga", sg0 + 2 * c + j + 1)
                            S.inc(v.tensor_scalar(
                                out=gcol[:, 2 * j:2 * j + 1],
                                in0=gcol[:, 2 * j:2 * j + 1],
                                scalar1=1.0 / 128.0, scalar2=EPSL2,
                                op0=ALU.mult, op1=ALU.add), "sgv")
                            S.wait(v, f"sgr{j}", gr0[j] + c + 1)
                            S.wait(v, f"szs{j}", hz0[j] + c + 1)
                            inst = v.tensor_scalar_mul(
                                out=h1t[:, j * 128:(j + 1) * 128],
                                in0=po[:],
                                scalar1=gcol[:, 2 * j + 1:2 * j + 2])
                            S.inc(inst, f"sgh{j}")
                            S.wait(v, f"sgh{j}", gh0[j] + c + 1)
                            inst = v.tensor_tensor(
                                out=h1t[:, j * 128:(j + 1) * 128],
                                in0=h1t[:, j * 128:(j + 1) * 128],
                                in1=zsil[:, j * 128:(j + 1) * 128],
                                op=ALU.mult)
                            vfence(v, S, inst)
                            if c >= 1:
                                S.wait(v, f"ho{j}", ho0_[j] + 16 * c)
                            S.inc(v.tensor_tensor(
                                out=hti[:, j * 128:(j + 1) * 128],
                                in0=h1t[:, j * 128:(j + 1) * 128],
                                in1=nwrep[:], op=ALU.mult), f"shv{j}")

                @block.scalar
                def _(s_, si=si, ou0=ou0, sg0=sg0, sv0=sv0, gr0=gr0,
                      zl0=zl0, zu0=zu0, hz0=hz0, hv0=hv0, ho0_=ho0_,
                      qt0=qt0, qe0_=qe0_):
                    for c in range(NCH):
                        for j in range(2):
                            S.wait(s_, "qtp", qt0 + 2 * c + j + 1)
                            if c >= 2:
                                S.wait(s_, f"sou{j}", ou0[j] + 2 * (c - 2) + 1)
                            S.inc(s_.copy(
                                out=qgT_s[:, (2 * j + c % 2) * 128:
                                          (2 * j + c % 2 + 1) * 128],
                                in_=pQG[:]), "qte")
                        for j in range(2):
                            vh = 2 * si + j
                            po = pV if j == 0 else pO
                            S.wait(s_, f"sou{j}", ou0[j] + 2 * c + 2)
                            if 2 * c + j >= 1:
                                S.wait(s_, "sga", sg0 + 2 * c + j)
                            S.inc(s_.activation(
                                out=lnscr[:, 0:128], in_=po[:], func=AF.Square,
                                accum_out=gcol[:, 2 * j:2 * j + 1]), "sga")
                            S.wait(s_, "sgv", sv0 + 2 * c + j + 1)
                            inst = s_.activation(
                                out=gcol[:, 2 * j + 1:2 * j + 2],
                                in_=gcol[:, 2 * j:2 * j + 1], func=AF.Ln)
                            vfence(s_, S, inst)
                            S.inc(s_.activation(
                                out=gcol[:, 2 * j + 1:2 * j + 2],
                                in_=gcol[:, 2 * j + 1:2 * j + 2],
                                func=AF.Exp, scale=-0.5), f"sgr{j}")
                            S.wait(s_, f"zl{j}{c % 2}",
                                   zl0[j][c % 2] + 16 * (c // 2 + 1))
                            if c >= 1:
                                S.wait(s_, f"shv{j}", hv0[j] + c)
                            S.inc(s_.copy(
                                out=zsil[:, j * 128:(j + 1) * 128],
                                in_=zstr[:, (2 * j + c % 2) * 128:
                                         (2 * j + c % 2 + 1) * 128]),
                                f"szs{j}")
                            S.inc(s_.copy(out=gcol[:, 4:5],
                                          in_=acol[:, 2:3]), f"zsu{j}")
                            S.wait(s_, f"shv{j}", hv0[j] + c + 1)
                            S.inc(s_.dma_start(
                                out=hout[c * 128:(c + 1) * 128,
                                         (2 * si + j) * 128:
                                         (2 * si + j + 1) * 128],
                                in_=hti[:, j * 128:(j + 1) * 128]),
                                f"ho{j}", 16)

                hps.close()

        final0 = S.n["ho0"]
        final1 = S.n["ho1"]

        @block.gpsimd
        def _(g):
            S.wait(g, "ho0", final0)
            S.wait(g, "ho1", final1)

    return nc



def kernel(**inputs):
    import numpy as np
    import ml_dtypes
    from concourse.bass_utils import run_bass_kernel_spmd
    f = np.float32
    bf = ml_dtypes.bfloat16
    g = lambda k: np.asarray(inputs[k])
    X0 = np.ascontiguousarray(g("L0_attn_hidden_states"), dtype=f)
    B, seq, Dm = X0.shape
    TOK = B * seq // 8
    sc = lambda v: float(np.ravel(np.asarray(v, f))[0])

    Wqkvz = np.asarray(g("L0_attn_W_qkvz"), f)
    Wba_f = np.asarray(g("L0_attn_W_ba"), f)
    conv_w = np.asarray(g("L0_attn_conv_w"), f)
    dtb_f = np.asarray(g("L0_attn_dt_bias"), f)
    alog_f = np.asarray(g("L0_attn_A_log"), f)
    nw_f = np.asarray(g("L0_attn_norm_weight"), f)
    WT = Wqkvz.T
    WbaT = Wba_f.T
    cw = conv_w[:, 0, :]
    KD, VD = 2048, 4096
    maps = []
    for c in range(8):
        b, hg = c // 4, c % 4
        Wsl = np.concatenate([
            WT[:, 0 * KD + hg * 512:0 * KD + (hg + 1) * 512],
            WT[:, 1 * KD + hg * 512:1 * KD + (hg + 1) * 512],
            WT[:, 2 * KD + hg * 1024:2 * KD + (hg + 1) * 1024],
            WT[:, 2 * KD + VD + hg * 1024:2 * KD + VD + (hg + 1) * 1024]], 1)
        Wba_sl = np.concatenate([WbaT[:, hg * 8:(hg + 1) * 8],
                                 WbaT[:, 32 + hg * 8:32 + (hg + 1) * 8]], 1)
        convw = np.concatenate([cw[0 * KD + hg * 512:0 * KD + (hg + 1) * 512],
                                cw[1 * KD + hg * 512:1 * KD + (hg + 1) * 512],
                                cw[2 * KD + hg * 1024:
                                   2 * KD + (hg + 1) * 1024]], 0)
        maps.append(dict(x=np.ascontiguousarray(X0[b]),
                         Wsl=np.ascontiguousarray(Wsl),
                         Wba=np.ascontiguousarray(Wba_sl),
                         convw=np.ascontiguousarray(convw),
                         alog=np.ascontiguousarray(alog_f[hg * 8:(hg + 1) * 8]),
                         dtb=np.ascontiguousarray(dtb_f[hg * 8:(hg + 1) * 8]),
                         nw=nw_f))
    res = run_bass_kernel_spmd(build_L0a(seq), maps, core_ids=list(range(8)))
    o0 = np.zeros((B, seq, VD), bf)
    for c in range(8):
        b, hg = c // 4, c % 4
        o0[b][:, hg * 1024:(hg + 1) * 1024] = res.results[c]["hout"]
    o0f = o0.reshape(B * seq, VD)
    Xf = X0.reshape(B * seq, Dm)

    W_out0 = np.asarray(g("L0_attn_W_out"), f)
    w1 = np.asarray(g("L0_ffn_w1"), f)
    w2 = np.asarray(g("L0_ffn_w2"), f)
    w3 = np.asarray(g("L0_ffn_w3"), f)
    maps = []
    for c in range(8):
        sl_ = slice(c * TOK, (c + 1) * TOK)
        maps.append(dict(x=np.ascontiguousarray(Xf[sl_]),
                         o0T=np.ascontiguousarray(o0f[sl_].T),
                         W_out=W_out0, w1=w1, w2=w2, w3=w3))
    res = run_bass_kernel_spmd(build_L0f(TOK), maps, core_ids=list(range(8)))
    X1 = np.concatenate([res.results[c]["x_out"] for c in range(8)], 0)

    Wq = np.asarray(g("L1_attn_Wq"), f)
    Wk = np.asarray(g("L1_attn_Wk"), f)
    Wv = np.asarray(g("L1_attn_Wv"), f)
    Wo = np.asarray(g("L1_attn_Wo"), f)
    maps = [dict(x=np.ascontiguousarray(X1[c * TOK:(c + 1) * TOK]),
                 Wq=Wq, Wk=Wk, Wv=Wv, Wo=Wo) for c in range(8)]
    res = run_bass_kernel_spmd(build_L1a(TOK), maps, core_ids=list(range(8)))
    X2a = np.concatenate([res.results[c]["x_out"] for c in range(8)], 0)

    maps = [dict(x=np.ascontiguousarray(X2a[c * TOK:(c + 1) * TOK]),
                 wq_gate=np.asarray(g("L1_ffn_wq_gate"), np.int32),
                 s_gate=np.asarray(g("L1_ffn_s_gate"), f),
                 wq_up=np.asarray(g("L1_ffn_wq_up"), np.int32),
                 s_up=np.asarray(g("L1_ffn_s_up"), f),
                 wq_down=np.asarray(g("L1_ffn_wq_down"), np.int32),
                 s_down=np.asarray(g("L1_ffn_s_down"), f)) for c in range(8)]
    res = run_bass_kernel_spmd(build_L1f(TOK), maps, core_ids=list(range(8)))
    X2 = np.concatenate([res.results[c]["x_out"] for c in range(8)], 0)

    W_q2 = np.asarray(g("L2_attn_W_q"), f)
    W_k2 = np.asarray(g("L2_attn_W_k"), f)
    W_v2 = np.asarray(g("L2_attn_W_v"), f)
    A_q2 = np.asarray(g("L2_attn_A_q"), f)
    B_q2 = np.asarray(g("L2_attn_B_q"), f)
    A_k2 = np.asarray(g("L2_attn_A_k"), f)
    B_k2 = np.asarray(g("L2_attn_B_k"), f)
    A_v2 = np.asarray(g("L2_attn_A_v"), f)
    B_v2 = np.asarray(g("L2_attn_B_v"), f)
    qgain = np.asarray(g("L2_attn_q_gain"), f)
    sqkv = np.array([sc(g("L2_attn_s_q")), sc(g("L2_attn_s_k")),
                     sc(g("L2_attn_s_v"))], f)
    AqkvT = np.concatenate([A_q2.T, A_k2.T, A_v2.T], 1)
    X2r = X2.reshape(B, seq, Dm)
    maps = []
    for c in range(8):
        b, hg = c // 4, c % 4
        WqkvT = np.concatenate([W_q2.T[:, hg * 512:(hg + 1) * 512],
                                W_k2.T[:, hg * 256:(hg + 1) * 256],
                                W_v2.T[:, hg * 256:(hg + 1) * 256]], 1)
        Bcat = np.zeros((96, 1024), f)
        Bcat[0:32, 0:512] = B_q2.T[:, hg * 512:(hg + 1) * 512]
        Bcat[32:64, 512:768] = B_k2.T[:, hg * 256:(hg + 1) * 256]
        Bcat[64:96, 768:1024] = B_v2.T[:, hg * 256:(hg + 1) * 256]
        maps.append(dict(x=np.ascontiguousarray(X2r[b]),
                         WqkvT=np.ascontiguousarray(WqkvT), sqkv=sqkv,
                         AqkvT=np.ascontiguousarray(AqkvT), Bcat=Bcat,
                         qgain=np.ascontiguousarray(qgain[hg * 4:(hg + 1) * 4])))
    res = run_bass_kernel_spmd(build_L2a(seq), maps, core_ids=list(range(8)))
    ctx2 = np.zeros((B, seq, Dm), bf)
    for c in range(8):
        b, hg = c // 4, c % 4
        ctx2[b][:, hg * 512:(hg + 1) * 512] = res.results[c]["ctx"]
    ctx2f = ctx2.reshape(B * seq, Dm)

    W_o2 = np.asarray(g("L2_attn_W_o"), f)
    A_o2 = np.asarray(g("L2_attn_A_o"), f)
    B_o2 = np.asarray(g("L2_attn_B_o"), f)
    s_o2 = np.array([sc(g("L2_attn_s_o"))], f)
    W2f = np.asarray(g("L2_ffn_W"), f)
    gw2 = np.asarray(g("L2_ffn_gate_w"), f)
    A2f = np.asarray(g("L2_ffn_A"), f)
    B2f = np.asarray(g("L2_ffn_B"), f)
    WoT2 = np.ascontiguousarray(W_o2.T)
    AoT = np.ascontiguousarray(A_o2.T)
    BoT = np.ascontiguousarray(B_o2.T)
    WTf = np.ascontiguousarray(W2f.T)
    gwT = np.ascontiguousarray(gw2.T)
    AfT = np.ascontiguousarray(A2f.reshape(128, Dm).T)
    B_re = np.ascontiguousarray(B2f.transpose(0, 2, 1).reshape(128, Dm))
    maps = []
    for c in range(8):
        sl_ = slice(c * TOK, (c + 1) * TOK)
        maps.append(dict(x=np.ascontiguousarray(X2[sl_]),
                         ctx2T=np.ascontiguousarray(ctx2f[sl_].T),
                         WoT2=WoT2, AoT=AoT, BoT=BoT, s_o=s_o2, WT=WTf,
                         gwT=gwT, AfT=AfT, B_re=B_re))
    res = run_bass_kernel_spmd(build_L2f(TOK), maps, core_ids=list(range(8)))
    X3 = np.concatenate([res.results[c]["x_out"] for c in range(8)], 0)

    w_qkv3 = np.asarray(g("L3_attn_w_qkv"), f)
    lnsc3 = np.asarray(g("L3_attn_ln_scale"), f)
    lnbi3 = np.asarray(g("L3_attn_ln_bias"), f)
    X3r = X3.reshape(B, seq, Dm)
    maps = []
    for c in range(8):
        b, hg = c // 4, c % 4
        Wsl3 = np.concatenate(
            [w_qkv3[:, 0 * Dm + hg * 512:0 * Dm + (hg + 1) * 512],
             w_qkv3[:, 1 * Dm + hg * 512:1 * Dm + (hg + 1) * 512],
             w_qkv3[:, 2 * Dm + hg * 512:2 * Dm + (hg + 1) * 512]], 1)
        maps.append(dict(x=np.ascontiguousarray(X3r[b]),
                         Wsl=np.ascontiguousarray(Wsl3),
                         ln_scale=lnsc3, ln_bias=lnbi3))
    res = run_bass_kernel_spmd(build_L3a(seq), maps, core_ids=list(range(8)))
    ctx3 = np.zeros((B, seq, Dm), bf)
    for c in range(8):
        b, hg = c // 4, c % 4
        ctx3[b][:, hg * 512:(hg + 1) * 512] = res.results[c]["ctx"]
    ctx3f = ctx3.reshape(B * seq, Dm)

    w_out3 = np.asarray(g("L3_attn_w_out"), f)
    gw3T = np.ascontiguousarray(np.asarray(g("L3_ffn_gate_w"), f).T)
    guw = np.asarray(g("L3_ffn_gate_up_weight"), f)
    dnw = np.asarray(g("L3_ffn_down_weight"), f)
    maps = []
    for c in range(8):
        sl_ = slice(c * TOK, (c + 1) * TOK)
        maps.append(dict(x=np.ascontiguousarray(X3[sl_]),
                         ctx3T=np.ascontiguousarray(ctx3f[sl_].T),
                         w_out=w_out3, gwT=gw3T, guw=guw, dnw=dnw))
    res = run_bass_kernel_spmd(build_L3f(TOK), maps, core_ids=list(range(8)))
    X4 = np.concatenate([res.results[c]["x_out"] for c in range(8)], 0)
    return np.ascontiguousarray(X4.reshape(B, seq, Dm))
